# version 22
# baseline (speedup 1.0000x reference)
import numpy as np
import ml_dtypes
from contextlib import ExitStack
import concourse.bass as bass
import concourse.mybir as mybir
from concourse.bass_utils import run_bass_kernel_spmd

F32 = mybir.dt.float32
BF16 = mybir.dt.bfloat16
I32 = mybir.dt.int32
AF = mybir.ActivationFunctionType
ALU = mybir.AluOpType
AX = mybir.AxisListType

NCORES = 8
T = 2048
HW = 8
TH = T + 2 * HW
NPT = 8
EPS = 1e-5

ENGINES = ("sync", "gpsimd", "scalar", "vector", "tensor")


class K:

    def __init__(self, nc, es, block):
        self.nc = nc
        self.block = block
        chans = ("t", "s", "v", "g", "y", "gd", "cc", "m")
        self.sem = {c: es.enter_context(nc.semaphore("sm_" + c)) for c in chans}
        self.cnt = dict.fromkeys(chans, 0)
        self.maxw = {e: dict.fromkeys(chans, 0) for e in ENGINES}
        self.pend = {e: [] for e in ENGINES}

    def _emit(self, eng, fn, waits, sig, inc):
        tok = None
        if sig is not None:
            self.cnt[sig] += inc
            tok = (sig, self.cnt[sig])
        self.pend[eng].append((fn, tuple(w for w in waits if w is not None), sig, inc))
        return tok

    def dma(self, out, in_, waits=(), q="gd", sig=None):
        eng = "gpsimd" if q == "gd" else "sync"
        return self._emit(eng, lambda e, o=out, i=in_: e.dma_start(out=o, in_=i),
                          waits, sig if sig is not None else q, 16)

    def mm(self, out, lhsT, rhs, start=True, stop=True, waits=(), sig="t",
           tile_position=None):
        return self._emit(
            "tensor",
            lambda e, o=out, l=lhsT, r=rhs, a=start, b=stop, tp=tile_position:
                e.matmul(o, l, r, start=a, stop=b, tile_position=tp),
            waits, sig, 1)

    def tr(self, out, in_, ident, waits=(), sig="t"):
        return self._emit(
            "tensor", lambda e, o=out, i=in_, d=ident: e.transpose(o, i, d),
            waits, sig, 1)

    def act(self, out, in_, func, bias=0.0, scale=1.0, accum=None, waits=(), sig="s"):
        return self._emit(
            "scalar",
            lambda e, o=out, i=in_, f=func, b=bias, sc=scale, a=accum:
                e.activation(o, i, f, bias=b, scale=sc, accum_out=a),
            waits, sig, 1)

    def v(self, fn, waits=(), sig="v"):
        return self._emit("vector", fn, waits, sig, 1)

    def g(self, fn, waits=(), sig="g", inc=1):
        return self._emit("gpsimd", fn, waits, sig, inc)

    def barrier_tok(self):
        return [(c, self.cnt[c]) for c in ("t", "s", "v", "g", "y", "gd")]

    def flush(self):
        for eng in ENGINES:
            ops = self.pend[eng]
            if not ops:
                continue
            self.pend[eng] = []
            sem = self.sem
            maxw = self.maxw[eng]

            def body(e, ops=ops, maxw=maxw, sem=sem):
                for fn, waits, sig, inc in ops:
                    for ch, val in waits:
                        if maxw[ch] < val:
                            e.wait_ge(sem[ch], val)
                            maxw[ch] = val
                    inst = fn(e)
                    if sig is not None:
                        inst.then_inc(sem[sig], inc)
            getattr(self.block, eng)(body)


def _bd(a):
    return a.astype(ml_dtypes.bfloat16)


def build_program(taps=()):
    nc = bass.Bass("TRN2", target_bir_lowering=False)

    D = {}

    def din(name, shape, dt=F32):
        D[name] = nc.dram_tensor(name, list(shape), dt, kind="ExternalInput").ap()
        return D[name]

    din("x_t", (128, NPT, TH))
    din("kv_mask", (1, TH))
    din("cos_t", (128, T))
    din("sin_t", (128, T))
    din("wq_t", (8, 128, 8, 128))
    din("wkv_t", (8, 128, 8, 256))
    din("wwin_t", (128, 8, 32))
    din("bwin_c", (32, 1))
    din("qg_c", (128, 1)); din("qb_c", (128, 1))
    din("kg_c", (128, 1)); din("kb_c", (128, 1))
    din("wout_t", (128, 8, 1024))
    din("gam_c", (128, 8)); din("bet_c", (128, 8))
    din("fc1q_t", (2, 128, 8, 2048), I32)
    din("fc1s_t", (2, 32, 1, 512))
    din("fc1b_c", (2, 128, 16))
    din("fc2q_t", (2, 128, 16, 1024), I32)
    din("fc2s_t", (2, 32, 1, 512))
    din("fc2b_c", (128, 8))
    din("wk_t", (128, 8, 1024))
    din("wv_t", (128, 8, 1024))
    din("wg_t", (128, 8, 1024))
    din("wp_t", (128, 8, 1024))
    din("latq_a", (128, 256)); din("latq_b", (128, 256))
    din("rmbdT_t", (128, 32, 128))
    din("rupbd_t", (128, 32, 512))
    din("wc_t", (128, 128, 1024))

    out_x = nc.dram_tensor("out_x", [128, NPT, T], F32, kind="ExternalOutput").ap()
    tap_d = {}
    for tname in taps:
        if tname in ("x1", "x2", "x3", "x4"):
            tap_d[tname] = nc.dram_tensor("tap_" + tname, [128, NPT, TH], F32,
                                          kind="ExternalOutput").ap()

    ar_in = nc.dram_tensor("ar_in", [64, 1024], F32)
    ar_out = nc.dram_tensor("ar_out", [64, 1024], F32)

    ident_f = nc.inline_tensor(np.eye(128, dtype=np.float32), name="c_identf").ap()
    ident_b = nc.inline_tensor(_bd(np.eye(128, dtype=np.float32)), name="c_identb").ap()
    ones1 = nc.inline_tensor(_bd(np.ones((128, 1), np.float32)), name="c_ones1").ap()
    onesr = nc.inline_tensor(_bd(np.ones((1, 128), np.float32)), name="c_onesr").ap()
    e2h = np.zeros((128, 2), np.float32); e2h[:64, 0] = 1; e2h[64:, 1] = 1
    e2h_c = nc.inline_tensor(_bd(e2h), name="c_e2h").ap()
    e2t = np.zeros((2, 128), np.float32); e2t[0, :64] = 1; e2t[1, 64:] = 1
    e2t_c = nc.inline_tensor(_bd(e2t), name="c_e2t").ap()
    jj, ll = np.meshgrid(np.arange(144), np.arange(128))
    R = np.abs(jj - ll - 8.0).astype(np.float32)
    R_c = nc.inline_tensor(R, name="c_R").ap()
    NEG2 = np.where((jj - ll >= 0) & (jj - ll <= 16), -5e3, -1e9).astype(np.float32)
    NEG2_c = nc.inline_tensor(NEG2, name="c_NEG2").ap()
    wsc1 = np.concatenate([np.full(16, 4.0, np.float32),
                           np.full(16, 4.75, np.float32)]).reshape(32, 1)
    wsc1_c = nc.inline_tensor(wsc1, name="c_wsc1").ap()
    wsc2 = np.concatenate([np.full(16, 4.5, np.float32),
                           np.full(16, 5.25, np.float32)]).reshape(32, 1)
    wsc2_c = nc.inline_tensor(wsc2, name="c_wsc2").ap()
    cc_, ll2 = np.meshgrid(np.arange(128), np.arange(64))
    C1 = np.where(cc_ <= ll2, 0.0, -1e9).astype(np.float32)
    C2_c = nc.inline_tensor(_bd(np.concatenate([C1, C1], 0)),
                            name="c_C2").ap()

    with ExitStack() as ES:
        block = ES.enter_context(nc.Block())
        k = K(nc, ES, block)

        def tapn(name, ap, cols, waits):
            if name in taps:
                d = nc.dram_tensor("tap_" + name, [128, cols], F32,
                                   kind="ExternalOutput").ap()
                k.dma(d, ap, waits=waits, q="gd")

        x_sb = ES.enter_context(nc.sbuf_tensor("x_sb", [128, NPT * TH], F32))
        xn_sb = ES.enter_context(nc.sbuf_tensor("xn_sb", [128, NPT * TH], BF16))
        PB = [ES.enter_context(nc.psum_tensor(f"pbank{i}", [128, 512], F32))
              for i in range(6)]
        PTB = [ES.enter_context(nc.psum_tensor(f"ptbank{i}", [128, 1024], BF16))
               for i in range(2)]
        LNS = ES.enter_context(ExitStack())
        idb_sb = LNS.enter_context(nc.sbuf_tensor("idb_sb", [128, 128], BF16))
        ones1_sb = LNS.enter_context(nc.sbuf_tensor("ones1_sb", [128, 1], BF16))
        onesr_sb = LNS.enter_context(nc.sbuf_tensor("onesr_sb", [1, 128], BF16))
        e2h_sb = LNS.enter_context(nc.sbuf_tensor("e2h_sb", [128, 2], BF16))
        e2t_sb = LNS.enter_context(nc.sbuf_tensor("e2t_sb", [2, 128], BF16))
        C2_sb = LNS.enter_context(nc.sbuf_tensor("C2_sb", [128, 128], BF16))
        mask_hb = LNS.enter_context(nc.sbuf_tensor("mask_hb", [128, 16], BF16))
        rowA = LNS.enter_context(nc.sbuf_tensor("rowA", [2, 512], F32))
        rowB = LNS.enter_context(nc.sbuf_tensor("rowB", [2, 512], F32))
        bc0s = LNS.enter_context(nc.sbuf_tensor("bc0s", [128, 512], F32))
        bc1s = LNS.enter_context(nc.sbuf_tensor("bc1s", [128, 512], F32))
        rAh = LNS.enter_context(nc.sbuf_tensor("rAh", [2, 512], BF16))
        rAl = LNS.enter_context(nc.sbuf_tensor("rAl", [2, 512], BF16))
        rBh = LNS.enter_context(nc.sbuf_tensor("rBh", [2, 512], BF16))
        rBl = LNS.enter_context(nc.sbuf_tensor("rBl", [2, 512], BF16))
        eps_sb = LNS.enter_context(nc.sbuf_tensor("eps_sb", [128, 1], F32))

        tk_const = []
        for dst, src in ((idb_sb, ident_b), (ones1_sb, ones1),
                         (onesr_sb, onesr), (e2h_sb, e2h_c), (e2t_sb, e2t_c),
                         (C2_sb, C2_c)):
            tk_const.append(k.dma(dst[:], src, q="y"))
        tok_eps = k.v(lambda e: e.memset(eps_sb[:], EPS))
        tk_x = k.dma(x_sb[:].rearrange("p (n c) -> p n c", c=TH), D["x_t"], q="y")

        def xs(pt, lo, hi):
            return x_sb[:, pt * TH + lo: pt * TH + hi]

        def xns(pt, lo, hi):
            return xn_sb[:, pt * TH + lo: pt * TH + hi]

        CH_TH = [(0, 512, 0), (512, 1024, 512), (1024, 1536, 1024),
                 (1536, 2048, 1536), (1552, TH, 2048)]
        CH_T = [(HW, 520), (520, 1032), (1032, 1544), (1544, 2056)]

        state = {"war_cb": None, "war_st": None, "war_bc": None, "war_rows": None}

        def rows_chunk(W, P, tok_in, nfeat):
            k.v(lambda e: e.tensor_scalar_mul(rowA[0:P, :W], rowA[0:P, :W],
                                              1.0 / nfeat), waits=(tok_in,))
            k.v(lambda e: e.tensor_mul(bc1s[0:P, :W], rowA[0:P, :W],
                                       rowA[0:P, :W]))
            tvv = k.v(lambda e: e.scalar_tensor_tensor(
                rowB[0:P, :W], rowB[0:P, :W], 1.0 / nfeat, bc1s[0:P, :W],
                ALU.mult, ALU.subtract))
            tss = k.act(rowB[0:P, :W], rowB[0:P, :W], AF.Ln,
                        bias=eps_sb[0:P, :], waits=(tvv, tok_eps))
            tss = k.act(rowB[0:P, :W], rowB[0:P, :W], AF.Exp, scale=-0.5)
            k.v(lambda e: e.tensor_copy(rAh[0:P, :W], rowA[0:P, :W]))
            k.v(lambda e: e.tensor_sub(rAl[0:P, :W], rowA[0:P, :W], rAh[0:P, :W]))
            k.v(lambda e: e.tensor_copy(rBh[0:P, :W], rowB[0:P, :W]),
                waits=(tss,))
            return k.v(lambda e: e.tensor_sub(rBl[0:P, :W], rowB[0:P, :W],
                                              rBh[0:P, :W]))

        def bcast2(ps, W, lhs, tok_rows, war):
            k.mm(ps["bc0"][:, :W], lhs, rAh[0:lhs.shape[0], :W],
                 start=True, stop=False, waits=(tok_rows, war))
            b1 = k.mm(ps["bc0"][:, :W], lhs, rAl[0:lhs.shape[0], :W],
                      start=False, stop=True)
            k.mm(ps["bc1"][:, :W], lhs, rBh[0:lhs.shape[0], :W],
                 start=True, stop=False)
            b2 = k.mm(ps["bc1"][:, :W], lhs, rBl[0:lhs.shape[0], :W],
                      start=False, stop=True)
            e1 = k.act(bc0s[:, :W], ps["bc0"][:, :W], AF.Copy,
                       waits=(b1, war))
            e2 = k.act(bc1s[:, :W], ps["bc1"][:, :W], AF.Copy, waits=(b2,))
            return (e1, e2)

        def emit_ln(rng, ps, scr, tok_ready, dst_fn=None, add_to_x=False,
                    xin_fn=None, src_fn=None, src_bf16=False, gain=None,
                    dbg=False):
            if src_fn is None:
                src_fn = xs
            out_tok = None
            war_cb = [state.get("war_cb"), state.get("war_cb")]
            war_sq = [state.get("war_cb"), state.get("war_cb")]
            for chunk in rng:
                lo, hi = chunk[0], chunk[1]
                alo = chunk[2] if len(chunk) > 2 else lo
                w = hi - lo
                off = alo - lo
                wa = hi - alo
                tm1 = tm2 = None
                for pt in range(NPT):
                    pr = pt % 2
                    cb = scr["cb"] if pr == 0 else scr["cb2"]
                    sq = scr["sq"] if pr == 0 else scr["sq2"]
                    if src_bf16:
                        cbv = src_fn(pt, lo, lo + w)
                        tv1 = tok_ready
                    else:
                        tv1 = k.v(lambda e, pt=pt, lo=lo, w=w, cb=cb:
                                  e.tensor_copy(cb[:, :w], src_fn(pt, lo, lo + w)),
                                  waits=(tok_ready, war_cb[pr]))
                        cbv = cb[:, :w]
                    tv2 = k.v(lambda e, w=w, cbv=cbv, sq=sq:
                              e.tensor_mul(sq[:, :w], cbv, cbv),
                              waits=(war_sq[pr], tv1))
                    tm1 = k.mm(ps["st0"][0:1, :w], ones1_sb[:], cbv,
                               start=(pt == 0), stop=(pt == NPT - 1),
                               waits=(tv1, state["war_st"]))
                    tm2 = k.mm(ps["st1"][0:1, :w], ones1_sb[:], sq[:, :w],
                               start=(pt == 0), stop=(pt == NPT - 1),
                               waits=(tv2, state["war_st"]))
                    war_cb[pr] = tm1
                    war_sq[pr] = tm2
                state["war_cb"] = tm2
                k.v(lambda e, w=w: e.tensor_copy(rowA[0:1, :w], ps["st0"][0:1, :w]),
                    waits=(tm1, state["war_rows"]))
                tev = k.v(lambda e, w=w: e.tensor_copy(rowB[0:1, :w],
                                                       ps["st1"][0:1, :w]),
                          waits=(tm2,))
                state["war_st"] = tev
                tok_rows = rows_chunk(w, 1, tev, 1024.0)
                b1, b2 = bcast2(ps, w, onesr_sb[:], tok_rows, state["war_bc"])
                for pt in range(NPT):
                    k.v(lambda e, pt=pt, lo=lo, w=w:
                        e.tensor_sub(scr["t1"][:, :w], src_fn(pt, lo, lo + w),
                                     bc0s[:, :w]),
                        waits=(b1, b2))
                    if add_to_x or xin_fn is not None:
                        k.v(lambda e, w=w:
                            e.tensor_mul(scr["t2"][:, :w], scr["t1"][:, :w],
                                         bc1s[:, :w]))
                        tmv = None
                        if add_to_x:
                            tmv = k.v(lambda e, pt=pt, alo=alo, wa=wa:
                                      e.tensor_add(xs(pt, alo, alo + wa),
                                                   xs(pt, alo, alo + wa),
                                                   scr["t2"][:, :wa]))
                        if xin_fn is not None:
                            tmv = k.v(lambda e, pt=pt, alo=alo, wa=wa:
                                      e.tensor_copy(xin_fn(pt, alo, wa),
                                                    scr["t2"][:, :wa]))
                        out_tok = tmv
                    elif gain is not None:
                        k.v(lambda e, w=w:
                            e.tensor_mul(scr["t2"][:, :w], scr["t1"][:, :w],
                                         bc1s[:, :w]))
                        out_tok = k.v(lambda e, pt=pt, alo=alo, wa=wa:
                                      e.tensor_scalar(dst_fn(pt, alo, wa),
                                                      scr["t2"][:, :wa],
                                                      gain[0](pt), gain[1](pt),
                                                      ALU.mult, ALU.add),
                                      waits=gain[2] if len(gain) > 2 else ())
                        if dbg and pt == 0 and lo == rng[0][0] and "dbg" in taps:
                            ddb = nc.dram_tensor("tap_dbg", [128, 2048], F32,
                                                 kind="ExternalOutput").ap()
                            db_sb = scr["dbg"]
                            k.v(lambda e: e.tensor_copy(db_sb[:, 0:512],
                                                        ps["bc0"][:, :]),
                                waits=(out_tok,))
                            k.v(lambda e: e.tensor_copy(db_sb[:, 512:1024],
                                                        ps["bc1"][:, :]))
                            k.v(lambda e: e.tensor_copy(db_sb[:, 1024:1536],
                                                        scr["t1"][:, :]))
                            tdb = k.v(lambda e: e.tensor_copy(db_sb[:, 1536:2048],
                                                              scr["t2"][:, :]))
                            k.dma(ddb, db_sb[:], waits=(tdb,), q="y")
                    elif wa == w:
                        out_tok = k.v(lambda e, pt=pt, alo=alo, wa=wa:
                                      e.tensor_mul(dst_fn(pt, alo, wa),
                                                   scr["t1"][:, :wa],
                                                   bc1s[:, :wa]))
                    else:
                        tvm = k.v(lambda e, w=w:
                                  e.tensor_mul(scr["t2"][:, :w], scr["t1"][:, :w],
                                               bc1s[:, :w]))
                        out_tok = k.act(dst_fn(pt, alo, wa),
                                        scr["t2"][:, off:off + wa], AF.Copy,
                                        waits=(tvm,))
                state["war_bc"] = out_tok
                state["war_rows"] = out_tok
            return out_tok

        with ExitStack() as PH:
            ps = {"st0": PB[4], "st1": PB[5], "bc0": PB[2], "bc1": PB[3]}
            pmh = PB[0][:, 0:16]
            scr = {
                "cb": PH.enter_context(nc.sbuf_tensor("s_cb", [128, 512], BF16)),
                "cb2": PH.enter_context(nc.sbuf_tensor("s_cb2", [128, 512], BF16)),
                "sq": PH.enter_context(nc.sbuf_tensor("s_sq", [128, 512], BF16)),
                "sq2": PH.enter_context(nc.sbuf_tensor("s_sq2", [128, 512], BF16)),
                "t1": PH.enter_context(nc.sbuf_tensor("s_t1", [128, 512], F32)),
                "t2": PH.enter_context(nc.sbuf_tensor("s_t2", [128, 512], F32)),
            }
            msk_f = PH.enter_context(nc.sbuf_tensor("msk_f", [1, TH], F32))
            msk_b = PH.enter_context(nc.sbuf_tensor("msk_b", [1, TH], BF16))
            tk_msk = k.dma(msk_f[:], D["kv_mask"], q="y")
            tok_xn = emit_ln(CH_TH, ps, scr, tk_x,
                             dst_fn=lambda pt, lo, w: xns(pt, lo, lo + w))
            tmv = k.v(lambda e: e.tensor_copy(msk_b[:], msk_f[:]), waits=(tk_msk,))
            k.mm(pmh[:, 0:8], onesr_sb[:], msk_b[:, 0:8],
                 waits=(tmv, tk_const[3]))
            t2m = k.mm(pmh[:, 8:16], onesr_sb[:], msk_b[:, TH - 8:TH])
            tok_mask = k.v(lambda e: e.tensor_copy(mask_hb[:], pmh[:]),
                           waits=(t2m,))
            tapn("xn", xn_sb[:], NPT * TH, (tok_xn,))
            k.flush()

        with ExitStack() as PH:
            ao_sb = PH.enter_context(nc.sbuf_tensor("ao_sb", [128, 8 * T], BF16))
            qgb = {nm: PH.enter_context(nc.sbuf_tensor(nm + "_sb", [128, 1], F32))
                   for nm in ("qg", "qb", "kg", "kb")}
            g8 = {nm: PH.enter_context(nc.sbuf_tensor(nm + "8", [128, 1], F32))
                  for nm in ("qg", "qb")}
            scr = {
                "sq": PH.enter_context(nc.sbuf_tensor("a1_sq", [128, 512], BF16)),
                "t1": PH.enter_context(nc.sbuf_tensor("a1_t1", [128, 512], F32)),
                "t2": PH.enter_context(nc.sbuf_tensor("a1_t2", [128, 512], F32)),
                "tb": PH.enter_context(nc.sbuf_tensor("a1_tb", [128, 512], BF16)),
            }
            pk = [PB[0], PB[1]]
            pqs = PB[4][0:2, :]
            pqq = PB[5][0:2, :]
            bc0, bc1 = PB[2], PB[3]
            pss = [PB[0][:, 0:144], PB[1][:, 0:144],
                   PB[0][:, 144:288], PB[1][:, 144:288]]
            pta = [PTB[0][:, 0:128], PTB[1][:, 0:128],
                   PTB[0][:, 512:640], PTB[1][:, 512:640]]
            ptb = [PTB[0][0:16, 128:256], PTB[1][0:16, 128:256],
                   PTB[0][0:16, 640:768], PTB[1][0:16, 640:768]]
            pav = [PB[2][:, 0:128], PB[3][:, 0:128],
                   PB[2][:, 128:256], PB[3][:, 128:256]]
            ptv = [PTB[0][:, 256:384], PTB[1][:, 256:384]]
            pwp = PB[2][0:32, :]
            pwt = PB[3][:, 0:32]

            P1 = PH.enter_context(ExitStack())
            wq_sb = [P1.enter_context(nc.sbuf_tensor(f"wq{i}", [128, 8 * 128], BF16))
                     for i in range(2)]
            wkv_sb = [P1.enter_context(nc.sbuf_tensor(f"wkv{i}", [128, 8 * 256],
                                                      BF16)) for i in range(2)]
            wwin_sb = P1.enter_context(nc.sbuf_tensor("wwin_sb", [128, 8 * 32],
                                                      BF16))
            bwin_sb = P1.enter_context(nc.sbuf_tensor("bwin_sb", [32, 1], F32))
            wsc1_sb = P1.enter_context(nc.sbuf_tensor("wsc1_sb", [32, 1], F32))
            wsc2_sb = P1.enter_context(nc.sbuf_tensor("wsc2_sb", [32, 1], F32))
            idf_sb = P1.enter_context(nc.sbuf_tensor("idf_sb", [128, 128], F32))
            R_sb = P1.enter_context(nc.sbuf_tensor("R_sb", [128, 144], F32))
            NEG2_sb = P1.enter_context(nc.sbuf_tensor("NEG2_sb", [128, 144],
                                                      F32))
            q_sb = P1.enter_context(nc.sbuf_tensor("q_sb", [128, T], BF16))
            kk_sb = P1.enter_context(nc.sbuf_tensor("kk_sb", [128, TH], BF16))
            v_sb = P1.enter_context(nc.sbuf_tensor("v_sb", [128, TH], BF16))
            vtm_sb = P1.enter_context(nc.sbuf_tensor("vtm_sb", [128, 17 * 128],
                                                     BF16))
            sh2_sb = P1.enter_context(nc.sbuf_tensor("sh2_sb", [128, 16 * 16], F32))
            wsh_sb = P1.enter_context(nc.sbuf_tensor("wsh_sb", [128, 16 * 16], F32))
            s3_sb = [P1.enter_context(nc.sbuf_tensor(f"s3_{i}", [128, 144], F32))
                     for i in range(4)]
            m2_sb = [P1.enter_context(nc.sbuf_tensor(f"mm2_{i}", [128, 144], F32))
                     for i in range(2)]
            ee_all = [P1.enter_context(nc.sbuf_tensor(f"eea_{i}",
                                                      [128, 16 * 144], BF16))
                      for i in range(2)]
            aw_sb = [P1.enter_context(nc.sbuf_tensor(f"aw_{i}", [128, 144], BF16))
                     for i in range(4)]
            at1_sb = [P1.enter_context(nc.sbuf_tensor(f"at1_{i}", [128, 128], BF16))
                      for i in range(4)]
            at2_sb = [P1.enter_context(nc.sbuf_tensor(f"at2_{i}", [16, 128], BF16))
                      for i in range(4)]
            nmx = [P1.enter_context(nc.sbuf_tensor(f"nmx_{i}", [128, 16], F32))
                   for i in range(2)]
            den = [P1.enter_context(nc.sbuf_tensor(f"den_{i}", [128, 16], F32))
                   for i in range(2)]

            _a0done = (tok_xn, tok_mask)
            tk_wwin = k.dma(wwin_sb[:].rearrange("p (n c) -> p n c", c=32),
                            D["wwin_t"], waits=_a0done, q="gd")
            tk_bwin = k.dma(bwin_sb[:], D["bwin_c"], waits=_a0done, q="y")
            tk_wsc = k.dma(wsc1_sb[:], wsc1_c, waits=_a0done, q="y")
            tk_wsc = k.dma(wsc2_sb[:], wsc2_c, waits=_a0done, q="y")
            tk_idf = k.dma(idf_sb[:], ident_f, waits=_a0done, q="y")
            tk_R = k.dma(R_sb[:], R_c, waits=_a0done, q="y")
            tk_NEG = k.dma(NEG2_sb[:], NEG2_c, waits=_a0done, q="y")
            tk_gb = []
            for nm in ("qg", "qb", "kg", "kb"):
                tk_gb.append(k.dma(qgb[nm][:], D[nm + "_c"], waits=_a0done, q="y"))
            tk_wq = [None, None]
            tk_wkv = [None, None]
            tk_wq[0] = k.dma(wq_sb[0][:].rearrange("p (n c) -> p n c", c=128),
                             D["wq_t"][0], waits=_a0done, q="gd")
            tk_wkv[0] = k.dma(wkv_sb[0][:].rearrange("p (n c) -> p n c", c=256),
                              D["wkv_t"][0], waits=_a0done, q="gd")

            WP = ExitStack()
            wp_sb = WP.enter_context(nc.sbuf_tensor("wp_sb", [32, T], F32))
            wsp_tm = WP.enter_context(nc.sbuf_tensor("wsp_tm", [128, 16 * 32],
                                                     F32))
            tok_wp = None
            for ci, (lo, hi) in enumerate(CH_T):
                w = hi - lo
                tmm = None
                for pt in range(NPT):
                    tmm = k.mm(pwp[:, :w], wwin_sb[:, pt * 32:(pt + 1) * 32],
                               xns(pt, lo, hi), start=(pt == 0),
                               stop=(pt == NPT - 1),
                               waits=(tk_wwin, tok_xn, tok_wp))
                tok_wp = k.act(wp_sb[:, lo - HW:hi - HW], pwp[:, :w], AF.Silu,
                               bias=bwin_sb[:], waits=(tmm, tk_bwin))
            tt_ = k.act(wp_sb[:], wp_sb[:], AF.Tanh, scale=0.5, waits=(tok_wp,))
            tv_s = k.v(lambda e: e.tensor_scalar(wp_sb[:], wp_sb[:],
                                                 wsc1_sb[:], wsc2_sb[:],
                                                 ALU.mult, ALU.add),
                       waits=(tt_, tk_wsc))
            tok_wt = None
            for tt2 in range(16):
                trt = k.tr(pwt[:], wp_sb[:, tt2 * 128:(tt2 + 1) * 128],
                           idf_sb[0:32, 0:32], waits=(tv_s, tok_wt, tk_idf))
                tok_wt = k.v(lambda e, tt2=tt2:
                             e.tensor_copy(wsp_tm[:, tt2 * 32:(tt2 + 1) * 32],
                                           pwt[:]), waits=(trt,))
            w3 = wsp_tm[:].rearrange("p (t c) -> p t c", c=32)
            s3view = sh2_sb[:].rearrange("p (t c) -> p t c", c=16)
            wsview = wsh_sb[:].rearrange("p (t c) -> p t c", c=16)
            k.v(lambda e: e.tensor_scalar_mul(s3view, w3[:, :, 16:32], -0.5),
                waits=(tok_wt,))
            k.v(lambda e: e.tensor_mul(wsview, w3[:, :, 0:16], w3[:, :, 16:32]))
            tok_wsp = k.v(lambda e: e.tensor_scalar_mul(wsview, wsview, 0.5))
            WP.close()

            state["war_bc"] = tok_wsp
            state["war_st"] = tok_wsp
            k.v(lambda e: e.tensor_scalar_mul(g8["qg"][:], qgb["qg"][:], 0.125),
                waits=tuple(tk_gb))
            tok_g8 = k.v(lambda e: e.tensor_scalar_mul(g8["qb"][:], qgb["qb"][:],
                                                       0.125))

            def head_ln(src, toks, gcol, bcol):
                tok = None
                for tk5 in toks:
                    tsl, lo, w = tk5[0], tk5[1], tk5[2]
                    alo = tk5[3] if len(tk5) > 3 else lo
                    off = alo - lo
                    wa = lo + w - alo
                    tv2 = k.v(lambda e, src=src, lo=lo, w=w:
                              e.tensor_mul(scr["sq"][:, :w], src[:, lo:lo + w],
                                           src[:, lo:lo + w]),
                              waits=(tsl, state["war_cb"], tok))
                    t1 = k.mm(pqs[:, :w], e2h_sb[:], src[:, lo:lo + w],
                              waits=(tsl, state["war_st"], tok))
                    t2 = k.mm(pqq[:, :w], e2h_sb[:], scr["sq"][:, :w],
                              waits=(tv2, state["war_st"]))
                    state["war_cb"] = t2
                    k.v(lambda e, w=w: e.tensor_copy(rowA[0:2, :w], pqs[:, :w]),
                        waits=(t1, state["war_rows"]))
                    tev = k.v(lambda e, w=w: e.tensor_copy(rowB[0:2, :w],
                                                           pqq[:, :w]),
                              waits=(t2,))
                    state["war_st"] = tev
                    tok_rows = rows_chunk(w, 2, tev, 64.0)
                    b1, b2 = bcast2({"bc0": bc0, "bc1": bc1}, w, e2t_sb[:],
                                    tok_rows, state["war_bc"])
                    k.v(lambda e, src=src, lo=lo, w=w:
                        e.tensor_sub(scr["t1"][:, :w], src[:, lo:lo + w],
                                     bc0s[:, :w]), waits=(b1, b2))
                    k.v(lambda e, w=w:
                        e.tensor_mul(scr["t2"][:, :w], scr["t1"][:, :w],
                                     bc1s[:, :w]))
                    if wa == w:
                        tok = k.v(lambda e, src=src, lo=lo, w=w:
                                  e.tensor_scalar(src[:, lo:lo + w],
                                                  scr["t2"][:, :w],
                                                  gcol, bcol, ALU.mult, ALU.add),
                                  waits=(tok_g8,))
                    else:
                        tvs = k.v(lambda e, w=w:
                                  e.tensor_scalar(scr["tb"][:, :w],
                                                  scr["t2"][:, :w],
                                                  gcol, bcol, ALU.mult, ALU.add),
                                  waits=(tok_g8,))
                        tok = k.act(src[:, alo:alo + wa],
                                    scr["tb"][:, off:off + wa], AF.Copy,
                                    waits=(tvs,))
                    state["war_bc"] = tok
                    state["war_rows"] = tok
                return tok

            war_pk = [None, None]
            wfree = [None, None]
            tok_scores_done = None
            war_pass1 = [None, None]
            war_den = [None, None]

            for hp in range(8):
                par = hp % 2
                if hp < 7:
                    tk_wq[1 - par] = k.dma(
                        wq_sb[1 - par][:].rearrange("p (n c) -> p n c", c=128),
                        D["wq_t"][hp + 1], waits=(wfree[1 - par],), q="gd")
                    tk_wkv[1 - par] = k.dma(
                        wkv_sb[1 - par][:].rearrange("p (n c) -> p n c", c=256),
                        D["wkv_t"][hp + 1], waits=(wfree[1 - par],), q="gd")

                tok_q_ch = []
                for ci, (lo, hi) in enumerate(CH_T):
                    w = hi - lo
                    bank = pk[ci % 2]
                    tmm = None
                    for pt in range(NPT):
                        tmm = k.mm(bank[:, :w],
                                   wq_sb[par][:, pt * 128:(pt + 1) * 128],
                                   xns(pt, lo, hi), start=(pt == 0),
                                   stop=(pt == NPT - 1),
                                   waits=(tk_wq[par], tok_xn, war_pk[ci % 2],
                                          tok_scores_done, tok_mask))
                    tsl = k.act(q_sb[:, lo - HW:hi - HW], bank[:, :w], AF.Silu,
                                waits=(tmm,))
                    war_pk[ci % 2] = tsl
                    tok_q_ch.append((tsl, lo - HW, w))
                tok_k_ch = []
                tok_v_ch = []
                for ci, (lo, hi, alo) in enumerate(CH_TH):
                    w = hi - lo
                    bank = pk[ci % 2]
                    tmm = None
                    for pt in range(NPT):
                        tmm = k.mm(bank[:, :w],
                                   wkv_sb[par][:, pt * 256:pt * 256 + 128],
                                   xns(pt, lo, hi), start=(pt == 0),
                                   stop=(pt == NPT - 1),
                                   waits=(tk_wkv[par], war_pk[ci % 2],
                                          tok_scores_done))
                    tsl = k.act(kk_sb[:, lo:hi], bank[:, :w], AF.Silu,
                                waits=(tmm,))
                    war_pk[ci % 2] = tsl
                    tok_k_ch.append((tsl, lo, w, alo))
                    tmm = None
                    for pt in range(NPT):
                        tmm = k.mm(bank[:, :w],
                                   wkv_sb[par][:, pt * 256 + 128:pt * 256 + 256],
                                   xns(pt, lo, hi), start=(pt == 0),
                                   stop=(pt == NPT - 1), waits=(tsl,))
                    tsl = k.act(v_sb[:, lo:hi], bank[:, :w], AF.Silu,
                                waits=(tmm,))
                    war_pk[ci % 2] = tsl
                    tok_v_ch.append((tsl, lo, w))
                wfree[par] = tok_v_ch[-1][0]
                k.v(lambda e: e.tensor_mul(v_sb[:, 0:8], v_sb[:, 0:8],
                                           mask_hb[:, 0:8]),
                    waits=(tok_v_ch[0][0], tok_mask))
                tok_vmask = k.v(lambda e: e.tensor_mul(v_sb[:, TH - 8:TH],
                                                       v_sb[:, TH - 8:TH],
                                                       mask_hb[:, 8:16]),
                                waits=(tok_v_ch[-1][0],))

                tok_qn = head_ln(q_sb, tok_q_ch, g8["qg"][:], g8["qb"][:])
                tok_kn = head_ln(kk_sb, tok_k_ch, qgb["kg"][:], qgb["kb"][:])
                k.v(lambda e: e.tensor_mul(kk_sb[:, 0:8], kk_sb[:, 0:8],
                                           mask_hb[:, 0:8]), waits=(tok_kn,))
                tok_kn = k.v(lambda e: e.tensor_mul(kk_sb[:, TH - 8:TH],
                                                    kk_sb[:, TH - 8:TH],
                                                    mask_hb[:, 8:16]))

                tok_vt = {}
                war_tv = [None, None]
                for j in range(17):
                    wj = 128 if j < 16 else 16
                    pv = ptv[j % 2]
                    trt = k.tr(pv[0:wj, 0:128], v_sb[:, j * 128:j * 128 + wj],
                               idb_sb[:], waits=(tok_vmask, war_tv[j % 2]))
                    war_tv[j % 2] = k.v(
                        lambda e, j=j, wj=wj, pv=pv:
                        e.tensor_copy(vtm_sb[0:wj, j * 128:(j + 1) * 128],
                                      pv[0:wj, 0:128]), waits=(trt,))
                    tok_vt[j] = war_tv[j % 2]
                tok_vt[17] = tok_vt[16]

                war_ss = [None] * 4
                war_ex = [None] * 4
                war_at = [None] * 4
                war_av = [None] * 4
                it = 0
                it2 = 0
                for hd in range(2):
                    hb = hd * 64
                    hglob = hp * 2 + hd
                    par = hglob % 2
                    tse = None
                    for qt in range(16):
                        p2 = it % 4
                        tmm = k.mm(pss[p2][:, :],
                                   q_sb[hb:hb + 64, qt * 128:(qt + 1) * 128],
                                   kk_sb[hb:hb + 64, qt * 128:qt * 128 + 144],
                                   waits=(tok_qn, tok_kn, tok_vmask,
                                          war_ss[p2]))
                        tss = k.act(s3_sb[p2][:], R_sb[:], AF.Tanh,
                                    scale=sh2_sb[:, qt * 16 + hglob:
                                                 qt * 16 + hglob + 1],
                                    bias=wsh_sb[:, qt * 16 + hglob:
                                                qt * 16 + hglob + 1],
                                    waits=(tok_wsp, war_ss[p2], war_ex[p2],
                                           tk_R, tk_NEG))
                        tst = k.v(lambda e, p2=p2: e.scalar_tensor_tensor(
                            m2_sb[p2 % 2][:], s3_sb[p2][:], 5e3, NEG2_sb[:],
                            ALU.mult, ALU.add), waits=(tss,))
                        tad = k.v(lambda e, p2=p2: e.tensor_add(
                            s3_sb[p2][:], pss[p2][:], m2_sb[p2 % 2][:]),
                            waits=(tmm, tst,
                                   war_pass1[par] if qt == 0 else None))
                        war_ss[p2] = tad
                        tmx = k.v(lambda e, p2=p2, qt=qt, par=par:
                                  e.reduce_max(nmx[par][:, qt:qt + 1],
                                               s3_sb[p2][:], AX.X, negate=True))
                        tse = k.act(ee_all[par][:, qt * 144:(qt + 1) * 144],
                                    s3_sb[p2][:], AF.Exp,
                                    bias=nmx[par][:, qt:qt + 1],
                                    accum=den[par][:, qt:qt + 1],
                                    waits=(tmx, war_den[par]))
                        war_ex[p2] = tse
                        it += 1
                    war_pass1[par] = tse
                    k.act(den[par][:], den[par][:], AF.Ln, waits=(tse,))
                    trc = k.act(den[par][:], den[par][:], AF.Exp, scale=-1.0)
                    taw = None
                    for qt in range(16):
                        p2 = it2 % 4
                        taw = k.v(lambda e, p2=p2, qt=qt, par=par:
                                  e.tensor_scalar(aw_sb[p2][:],
                                                  ee_all[par][:, qt * 144:
                                                              (qt + 1) * 144],
                                                  den[par][:, qt:qt + 1], 0.0,
                                                  ALU.mult, ALU.add),
                                  waits=(trc, war_at[p2]))
                        tr1 = k.tr(pta[p2][:], aw_sb[p2][:, 0:128], idb_sb[:],
                                   waits=(taw, war_at[p2]))
                        tr2 = k.tr(ptb[p2][:], aw_sb[p2][:, 128:144], idb_sb[:])
                        te1 = k.act(at1_sb[p2][:], pta[p2][:], AF.Copy,
                                    waits=(tr1,))
                        te2 = k.act(at2_sb[p2][:], ptb[p2][:], AF.Copy,
                                    waits=(tr2,))
                        k.mm(pav[p2][hb:hb + 64, :],
                             vtm_sb[:, qt * 128 + hb:qt * 128 + hb + 64],
                             at1_sb[p2][:], start=True, stop=False,
                             waits=(te1, tok_vt[qt], tok_vt[qt + 1],
                                    war_av[p2]))
                        tav = k.mm(pav[p2][hb:hb + 64, :],
                                   vtm_sb[0:16, (qt + 1) * 128 + hb:
                                          (qt + 1) * 128 + hb + 64],
                                   at2_sb[p2][:], start=False, stop=True,
                                   waits=(te2,))
                        war_at[p2] = tav
                        tev = k.act(ao_sb[hb:hb + 64, hp * T + qt * 128:
                                          hp * T + (qt + 1) * 128],
                                    pav[p2][hb:hb + 64, :], AF.Copy,
                                    waits=(tav,))
                        war_av[p2] = tev
                        tok_scores_done = tev
                        it2 += 1
                    war_den[par] = taw
                state["war_bc"] = tok_scores_done
                if hp == 7:
                    tapn("q", q_sb[:], T, (tok_qn,))
                    tapn("kk", kk_sb[:], TH, (tok_kn,))
                    tapn("v", v_sb[:], TH, (tok_vmask,))
                    tapn("wsp", wsp_tm[:], 16 * 32, (tok_wsp,))
                    tapn("ao", ao_sb[:], 8 * T, (tok_scores_done,))

            k.flush()
            P1.close()

            wout_sb = PH.enter_context(nc.sbuf_tensor("wout_sb", [128, 8 * 1024],
                                                      BF16))
            tk_wout = k.dma(wout_sb[:].rearrange("p (n c) -> p n c", c=1024),
                            D["wout_t"], waits=(tok_scores_done,), q="gd")
            tok_x_upd = None
            war_tb = None
            for cc in range(8):
                for ci, (lo, hi) in enumerate(CH_T):
                    w = hi - lo
                    bank = pk[ci % 2]
                    tmm = None
                    for pt in range(NPT):
                        tmm = k.mm(bank[:, :w],
                                   wout_sb[:, pt * 1024 + cc * 128:
                                           pt * 1024 + (cc + 1) * 128],
                                   ao_sb[:, pt * T + lo - HW:pt * T + hi - HW],
                                   start=(pt == 0), stop=(pt == NPT - 1),
                                   waits=(tk_wout, tok_scores_done,
                                          war_pk[ci % 2]))
                    tsl = k.act(scr["tb"][:, :w], bank[:, :w], AF.Silu,
                                waits=(tmm, war_tb))
                    war_pk[ci % 2] = tsl
                    tok_x_upd = k.v(lambda e, cc=cc, lo=lo, w=w:
                                    e.tensor_add(xs(cc, lo, lo + w),
                                                 xs(cc, lo, lo + w),
                                                 scr["tb"][:, :w]), waits=(tsl,))
                    war_tb = tok_x_upd
            k.flush()
        tok_x1 = tok_x_upd

        if "x1" in tap_d:
            k.dma(tap_d["x1"], x_sb[:].rearrange("p (n c) -> p n c", c=TH),
                  waits=(tok_x1,), q="y")
            k.flush()

        CH_OWN = [(HW, 520), (520, 1032), (1032, 1544), (1544, 2056)]
        with ExitStack() as PH:
            scr = {
                "t1": PH.enter_context(nc.sbuf_tensor("a2_t1", [128, 512], F32)),
                "t2": PH.enter_context(nc.sbuf_tensor("a2_t2", [128, 512], F32)),
            }
            ps = {"st0": PB[4], "st1": PB[5], "bc0": PB[2], "bc1": PB[3]}
            pk = [PB[0], PB[1]]
            gam_sb = PH.enter_context(nc.sbuf_tensor("gam_sb", [128, 8], F32))
            bet_sb = PH.enter_context(nc.sbuf_tensor("bet_sb", [128, 8], F32))
            f1b_sb = PH.enter_context(nc.sbuf_tensor("f1b_sb", [128, 32], F32))
            f2b_sb = PH.enter_context(nc.sbuf_tensor("f2b_sb", [128, 8], F32))
            tk_gam = k.dma(gam_sb[:], D["gam_c"], waits=(tok_x1,), q="y")
            tk_bet = k.dma(bet_sb[:], D["bet_c"], waits=(tok_x1,), q="y")
            tk_f1b = k.dma(f1b_sb[:].rearrange("p (h n) -> p h n", n=16),
                           D["fc1b_c"].rearrange("h p n -> p h n"),
                           waits=(tok_x1,), q="y")
            tk_f2b = k.dma(f2b_sb[:], D["fc2b_c"], waits=(tok_x1,), q="y")

            w1_sb = PH.enter_context(nc.sbuf_tensor("w1_sb", [128, 8 * 2048],
                                                    BF16))
            qraw = [PH.enter_context(nc.sbuf_tensor(f"qraw{i}", [128, 2048],
                                                    BF16)) for i in range(2)]
            bcs = [PH.enter_context(nc.sbuf_tensor(f"bcs{i}", [128, 512], F32))
                   for i in range(2)]
            A2a = PH.enter_context(ExitStack())
            xin_sb = A2a.enter_context(nc.sbuf_tensor("xin_sb", [128, 8 * T], BF16))
            for nm2 in ("cb", "cb2", "sq", "sq2"):
                scr[nm2] = A2a.enter_context(
                    nc.sbuf_tensor("a2_" + nm2, [128, 512], BF16))

            def xinf(pt, lo, w):
                return xin_sb[:, pt * T + lo - HW: pt * T + lo - HW + w]

            def xin_src(pt, lo, hi):
                return xin_sb[:, pt * T + lo - HW: pt * T + hi - HW]

            tok_xin = emit_ln(CH_OWN, ps, scr, tok_x1, add_to_x=True, xin_fn=xinf)
            if "dbg" in taps:
                scr["dbg"] = A2a.enter_context(
                    nc.sbuf_tensor("db_sb", [128, 2048], F32))
            tok_h = emit_ln(CH_OWN, ps, scr, tok_xin,
                            dst_fn=lambda pt, lo, w: xns(pt, lo, lo + w),
                            src_fn=xin_src, src_bf16=True,
                            gain=(lambda pt: gam_sb[:, pt:pt + 1],
                                  lambda pt: bet_sb[:, pt:pt + 1],
                                  (tk_gam, tk_bet)), dbg=True)
            tapn("xin", xin_sb[:], 8 * T, (tok_xin,))
            tapn("xin2", xin_sb[:], 8 * T, (tok_h,))
            tapn("hh", xn_sb[:], 8 * TH, (tok_h,))
            if "rows" in taps:
                drow = nc.dram_tensor("tap_rows", [2, 1024], F32,
                                      kind="ExternalOutput").ap()
                rv_sb = A2a.enter_context(nc.sbuf_tensor("rv_sb", [2, 1024], F32))
                k.v(lambda e: e.tensor_copy(rv_sb[:, 0:512], rowA[0:2, :]),
                    waits=(tok_h,))
                trr = k.v(lambda e: e.tensor_copy(rv_sb[:, 512:1024], rowB[0:2, :]))
                k.dma(drow, rv_sb[:], waits=(trr,), q="y")
            k.flush()
            A2a.close()

            w2_sb = PH.enter_context(nc.sbuf_tensor("w2_sb", [128, 16 * 1024], BF16))
            y_sb = PH.enter_context(nc.sbuf_tensor("y_sb", [128, 16 * 512], BF16))

            war_qraw = [tok_x1, tok_x1]
            war_w1 = tok_x1
            war_w2 = tok_h
            war_bcs = [tok_x1, tok_x1]
            tk_bcs_prev = None
            tok_x2 = None
            war_pk2 = [state["war_bc"], state["war_bc"]]
            for half in (0, 1):
                tok_w1 = None
                for kt in range(8):
                    tkr = k.dma(qraw[kt % 2][:], D["fc1q_t"][half][:, kt, :],
                                waits=(war_qraw[kt % 2],), q="gd")
                    for c4 in range(4):
                        ib = (kt * 4 + c4) % 2
                        tkb = k.dma(bcs[ib][:],
                                    D["fc1s_t"][half][kt * 4 + c4]
                                    .broadcast_to([128, 512]),
                                    waits=(war_bcs[ib], tk_bcs_prev),
                                    q="y", sig="m")
                        tk_bcs_prev = tkb
                        tok_w1 = k.v(
                            lambda e, kt=kt, c4=c4, ib=ib:
                            e.scalar_tensor_tensor(
                                w1_sb[:, kt * 2048 + c4 * 512:
                                      kt * 2048 + (c4 + 1) * 512],
                                qraw[kt % 2][:, c4 * 512:(c4 + 1) * 512],
                                -128.0, bcs[ib][:], ALU.add, ALU.mult),
                            waits=(tkr, tkb, war_w1))
                        war_bcs[ib] = tok_w1
                    war_qraw[kt % 2] = tok_w1
                tok_w2 = None
                for kt in range(16):
                    tkr = k.dma(qraw[kt % 2][:, 0:1024],
                                D["fc2q_t"][half][:, kt, :],
                                waits=(war_qraw[kt % 2],), q="gd")
                    for c4 in range(2):
                        ib = (kt * 2 + c4) % 2
                        tkb = k.dma(bcs[ib][:],
                                    D["fc2s_t"][half][kt * 2 + c4]
                                    .broadcast_to([128, 512]),
                                    waits=(war_bcs[ib], tk_bcs_prev),
                                    q="y", sig="m")
                        tk_bcs_prev = tkb
                        tok_w2 = k.v(
                            lambda e, kt=kt, c4=c4, ib=ib:
                            e.scalar_tensor_tensor(
                                w2_sb[:, kt * 1024 + c4 * 512:
                                      kt * 1024 + (c4 + 1) * 512],
                                qraw[kt % 2][:, c4 * 512:(c4 + 1) * 512],
                                -128.0, bcs[ib][:], ALU.add, ALU.mult),
                            waits=(tkr, tkb, war_w2))
                        war_bcs[ib] = tok_w2
                    war_qraw[kt % 2] = tok_w2
                war_y = None
                last_fc1 = None
                last_fc2 = None
                for ci, (lo, hi) in enumerate(CH_OWN):
                    w = hi - lo
                    tok_y = []
                    for nc2 in range(16):
                        bank = pk[nc2 % 2]
                        tmm = None
                        for kt in range(8):
                            tmm = k.mm(
                                bank[:, :w],
                                w1_sb[:, kt * 2048 + nc2 * 128:
                                      kt * 2048 + (nc2 + 1) * 128],
                                xns(kt, lo, hi),
                                start=(kt == 0), stop=(kt == 7),
                                waits=(tok_w1, tok_h, war_pk2[nc2 % 2], war_y))
                        last_fc1 = tmm
                        tsy = k.act(y_sb[:, nc2 * 512:nc2 * 512 + w], bank[:, :w],
                                    AF.Silu,
                                    bias=f1b_sb[:, half * 16 + nc2:
                                                half * 16 + nc2 + 1],
                                    waits=(tmm, tk_f1b))
                        war_pk2[nc2 % 2] = tsy
                        tok_y.append(tsy)
                    for cc in range(8):
                        bank = pk[cc % 2]
                        tmm = None
                        for kt in range(16):
                            tmm = k.mm(
                                bank[:, :w],
                                w2_sb[:, kt * 1024 + cc * 128:
                                      kt * 1024 + (cc + 1) * 128],
                                y_sb[:, kt * 512:kt * 512 + w],
                                start=(kt == 0), stop=(kt == 15),
                                waits=(tok_w2, tok_y[-1], war_pk2[cc % 2]))
                        last_fc2 = tmm
                        if half == 0:
                            tvz = k.v(lambda e, w=w, cc=cc, bank=bank:
                                      e.tensor_scalar(scr["t2"][:, :w],
                                                      bank[:, :w],
                                                      f2b_sb[:, cc:cc + 1],
                                                      0.5, ALU.add, ALU.mult),
                                      waits=(tmm, tk_f2b))
                            tok_x2 = k.v(lambda e, cc=cc, lo=lo, w=w:
                                         e.tensor_add(xs(cc, lo, lo + w),
                                                      xs(cc, lo, lo + w),
                                                      scr["t2"][:, :w]))
                            war_pk2[cc % 2] = tok_x2
                        else:
                            tok_x2 = k.v(lambda e, cc=cc, lo=lo, w=w, bank=bank:
                                         e.scalar_tensor_tensor(
                                             xs(cc, lo, lo + w), bank[:, :w],
                                             0.5, xs(cc, lo, lo + w),
                                             ALU.mult, ALU.add),
                                         waits=(tmm,))
                            war_pk2[cc % 2] = tok_x2
                    war_y = last_fc2
                war_w1 = last_fc1
                war_w2 = last_fc2
            tapn("w1d", w1_sb[:], 8 * 2048, (tok_x2,))
            tapn("w2d", w2_sb[:], 16 * 1024, (tok_x2,))
            tapn("yv", y_sb[:], 16 * 512, (tok_x2,))
            k.flush()

        if "x2" in tap_d:
            k.dma(tap_d["x2"], x_sb[:].rearrange("p (n c) -> p n c", c=TH),
                  waits=(tok_x2,), q="y")
            k.flush()

        MC = 256
        with ExitStack() as P3:
            scr3 = {
                "cb": P3.enter_context(nc.sbuf_tensor("a3_cb", [128, 512], BF16)),
                "cb2": P3.enter_context(nc.sbuf_tensor("a3_cb2", [128, 512], BF16)),
                "sq": P3.enter_context(nc.sbuf_tensor("a3_sq", [128, 512], BF16)),
                "sq2": P3.enter_context(nc.sbuf_tensor("a3_sq2", [128, 512], BF16)),
                "t1": P3.enter_context(nc.sbuf_tensor("a3_t1", [128, 512], F32)),
                "t2": P3.enter_context(nc.sbuf_tensor("a3_t2", [128, 256], F32)),
            }
            ps3 = {"st0": PB[4], "st1": PB[5], "bc0": PB[2], "bc1": PB[3]}
            pk3 = [PB[0], PB[1]]
            gts_sb = P3.enter_context(nc.sbuf_tensor("gts_sb", [128, 8 * T], BF16))
            aac_sb = P3.enter_context(nc.sbuf_tensor("aac_sb", [64, 1024], F32))
            atRb_sb = P3.enter_context(nc.sbuf_tensor("atRb_sb", [128, 1024],
                                                      BF16))

            tok_xn3 = emit_ln(CH_OWN, ps3, scr3, tok_x2,
                              dst_fn=lambda pt, lo, w: xns(pt, lo, lo + w))
            if "xn3" in taps:
                tapn("xn3", xn_sb[:], 8 * TH, (tok_xn3,))

            with ExitStack() as PG:
                wg_sb = PG.enter_context(nc.sbuf_tensor("wg_sb", [128, 8 * 1024],
                                                        BF16))
                egs = PG.enter_context(nc.sbuf_tensor("egs", [128, 512], BF16))
                tk_wg = k.dma(wg_sb[:].rearrange("p (n c) -> p n c", c=1024),
                              D["wg_t"], waits=(tok_x2,), q="gd")
                war_eg = None
                war_rowsg = state["war_rows"]
                war_bcg = state["war_bc"]
                war_stg = state["war_st"]
                tok_gts = None
                gi2 = 0
                for pt in range(NPT):
                    for ci, (lo, hi) in enumerate(CH_OWN):
                        w = hi - lo
                        bank = pk3[gi2 % 2]
                        tmm = None
                        for kt in range(NPT):
                            tmm = k.mm(bank[:, :w],
                                       wg_sb[:, kt * 1024 + pt * 128:
                                             kt * 1024 + (pt + 1) * 128],
                                       xns(kt, lo, hi), start=(kt == 0),
                                       stop=(kt == NPT - 1),
                                       waits=(tk_wg, tok_xn3, war_eg))
                        tse = k.act(egs[:, :w], bank[:, :w], AF.Exp, scale=0.125,
                                    waits=(tmm, war_eg))
                        tden = k.mm(ps3["st0"][0:2, :w], e2h_sb[:], egs[:, :w],
                                    waits=(tse, war_stg))
                        tcp = k.v(lambda e, w=w: e.tensor_copy(rowB[0:2, :w],
                                                               ps3["st0"][0:2, :w]),
                                  waits=(tden, war_rowsg))
                        war_stg = tcp
                        tln = k.act(rowB[0:2, :w], rowB[0:2, :w], AF.Ln,
                                    waits=(tcp,))
                        tln = k.act(rowB[0:2, :w], rowB[0:2, :w], AF.Exp,
                                    scale=-1.0)
                        k.v(lambda e, w=w: e.tensor_copy(rBh[0:2, :w],
                                                         rowB[0:2, :w]),
                            waits=(tln,))
                        trl = k.v(lambda e, w=w: e.tensor_sub(rBl[0:2, :w],
                                                              rowB[0:2, :w],
                                                              rBh[0:2, :w]))
                        k.mm(ps3["bc0"][:, :w], e2t_sb[:], rBh[0:2, :w],
                             start=True, stop=False, waits=(trl, war_bcg))
                        tbc = k.mm(ps3["bc0"][:, :w], e2t_sb[:], rBl[0:2, :w],
                                   start=False, stop=True)
                        tbc = k.act(bc0s[:, :w], ps3["bc0"][:, :w], AF.Copy,
                                    waits=(tbc, war_bcg))
                        tok_gts = k.v(
                            lambda e, pt=pt, lo=lo, w=w:
                            e.tensor_mul(gts_sb[:, pt * T + lo - HW:
                                                pt * T + lo - HW + w],
                                         egs[:, :w], bc0s[:, :w]),
                            waits=(tbc,))
                        war_eg = tok_gts
                        war_bcg = tok_gts
                        war_rowsg = tok_gts
                        gi2 += 1
                state["war_st"] = war_stg
                state["war_bc"] = war_bcg
                state["war_rows"] = war_rowsg
                k.flush()
            if "gts" in taps:
                tapn("gts", gts_sb[:], 8 * T, (tok_gts,))

            with ExitStack() as PK:
                wk_sb = PK.enter_context(nc.sbuf_tensor("wk_sb", [128, 8 * 1024],
                                                        BF16))
                wv_sb = PK.enter_context(nc.sbuf_tensor("wv_sb", [128, 8 * 1024],
                                                        BF16))
                cos_sb = PK.enter_context(nc.sbuf_tensor("cos_sb", [128, MC],
                                                         BF16))
                sin_sb = PK.enter_context(nc.sbuf_tensor("sin_sb", [128, MC],
                                                         BF16))
                latq_sb = PK.enter_context(nc.sbuf_tensor("latq_sb", [128, 512],
                                                          BF16))
                kr_sb = PK.enter_context(nc.sbuf_tensor("kr_sb", [128, 8 * MC],
                                                        BF16))
                kk2_sb = PK.enter_context(nc.sbuf_tensor("kk2_sb", [128, 8 * MC],
                                                         BF16))
                v2_sb = PK.enter_context(nc.sbuf_tensor("v2_sb", [128, 8 * MC],
                                                        BF16))
                vt2_sb = PK.enter_context(nc.sbuf_tensor("vt2_sb", [128, 32 * 64],
                                                         BF16))
                sm3 = [PK.enter_context(nc.sbuf_tensor(f"sm3_{i}", [128, 128], F32))
                       for i in range(2)]
                ee3 = [PK.enter_context(nc.sbuf_tensor(f"ee3_{i}", [128, 128],
                                                       BF16)) for i in range(2)]
                pT3 = [PK.enter_context(nc.sbuf_tensor(f"pT3_{i}", [128, 128],
                                                       BF16)) for i in range(2)]
                nm3 = [PK.enter_context(nc.sbuf_tensor(f"nm3_{i}", [128, 1], F32))
                       for i in range(2)]
                dn3 = [PK.enter_context(nc.sbuf_tensor(f"dn3_{i}", [128, 1], F32))
                       for i in range(2)]

                tk_wk = k.dma(wk_sb[:].rearrange("p (n c) -> p n c", c=1024),
                              D["wk_t"], waits=(tok_gts,), q="gd")
                tk_wv = k.dma(wv_sb[:].rearrange("p (n c) -> p n c", c=1024),
                              D["wv_t"], waits=(tok_gts,), q="gd")
                war_trig = tok_gts
                tk_la = k.dma(latq_sb[:, 0:256], D["latq_a"], waits=(tok_gts,),
                              q="gd")
                tk_lb = k.dma(latq_sb[:, 256:512], D["latq_b"], q="gd")
                tok_lq = k.v(lambda e: e.tensor_scalar_mul(latq_sb[:], latq_sb[:],
                                                           0.125),
                             waits=(tk_la, tk_lb))

                war_pk3 = [state["war_bc"], state["war_bc"]]
                war_kr = None
                war_v2 = None
                war_vt = [None, None]
                war_sc = [None, None]
                war_ex3 = [None, None]
                war_pt3 = [None, None]
                war_pat = [None, None]
                tok_aac = None
                it3 = 0
                for mc in range(8):
                    lo = HW + mc * MC
                    hi = lo + MC
                    tk_cos = k.dma(cos_sb[:], D["cos_t"][:, mc * MC:(mc + 1) * MC],
                                   waits=(war_trig,), q="gd")
                    tk_sin = k.dma(sin_sb[:], D["sin_t"][:, mc * MC:(mc + 1) * MC],
                                   q="gd")
                    tok_kr = None
                    for opt in range(8):
                        bank = pk3[opt % 2]
                        tmm = None
                        for kt in range(NPT):
                            tmm = k.mm(bank[:, :MC],
                                       wk_sb[:, kt * 1024 + opt * 128:
                                             kt * 1024 + (opt + 1) * 128],
                                       xns(kt, lo, hi), start=(kt == 0),
                                       stop=(kt == NPT - 1),
                                       waits=(tk_wk, war_pk3[opt % 2]))
                        tok_kr = k.act(kr_sb[:, opt * MC:(opt + 1) * MC],
                                       bank[:, :MC], AF.Copy,
                                       waits=(tmm, war_kr))
                        war_pk3[opt % 2] = tok_kr
                    tok_rope = None
                    for j in range(4):
                        ka = kr_sb[:, j * MC:(j + 1) * MC]
                        kb = kr_sb[:, (j + 4) * MC:(j + 5) * MC]
                        cs = cos_sb[:]
                        sn = sin_sb[:]
                        k.v(lambda e, ka=ka, cs=cs:
                            e.tensor_mul(scr3["t1"][:, :MC], ka, cs),
                            waits=(tok_kr, tk_cos, tk_sin, tok_rope))
                        k.v(lambda e, kb=kb, sn=sn:
                            e.tensor_mul(scr3["t2"][:, :MC], kb, sn))
                        k.v(lambda e, j=j:
                            e.tensor_sub(kk2_sb[:, j * MC:(j + 1) * MC],
                                         scr3["t1"][:, :MC], scr3["t2"][:, :MC]))
                        k.v(lambda e, ka=ka, sn=sn:
                            e.tensor_mul(scr3["t1"][:, :MC], ka, sn))
                        k.v(lambda e, kb=kb, cs=cs:
                            e.tensor_mul(scr3["t2"][:, :MC], kb, cs))
                        tok_rope = k.v(
                            lambda e, j=j:
                            e.tensor_add(kk2_sb[:, (j + 4) * MC:(j + 5) * MC],
                                         scr3["t1"][:, :MC], scr3["t2"][:, :MC]))
                    war_kr = tok_rope
                    war_trig = tok_rope
                    tok_v2 = None
                    for opt in range(8):
                        bank = pk3[opt % 2]
                        tmm = None
                        for kt in range(NPT):
                            tmm = k.mm(bank[:, :MC],
                                       wv_sb[:, kt * 1024 + opt * 128:
                                             kt * 1024 + (opt + 1) * 128],
                                       xns(kt, lo, hi), start=(kt == 0),
                                       stop=(kt == NPT - 1),
                                       waits=(tk_wv, war_pk3[opt % 2]))
                        tok_v2 = k.act(v2_sb[:, opt * MC:(opt + 1) * MC],
                                       bank[:, :MC], AF.Copy,
                                       waits=(tmm, war_v2))
                        war_pk3[opt % 2] = tok_v2
                    for c1 in range(2):
                        for h in range(16):
                            i2 = (c1 * 16 + h) % 2
                            hb2 = (h % 2) * 64
                            src = v2_sb[hb2:hb2 + 64,
                                        (h // 2) * MC + c1 * 128:
                                        (h // 2) * MC + (c1 + 1) * 128]
                            pv = PTB[i2][:, 256:320]
                            trt = k.tr(pv, src, idb_sb[hb2:hb2 + 64,
                                                       hb2:hb2 + 64],
                                       waits=(tok_v2, war_vt[i2]))
                            war_vt[i2] = k.v(
                                lambda e, c1=c1, h=h, pv=pv:
                                e.tensor_copy(vt2_sb[:, (c1 * 16 + h) * 64:
                                                     (c1 * 16 + h + 1) * 64],
                                              pv), waits=(trt,))
                    tok_vt2 = war_vt[0]
                    tok_vt2b = war_vt[1]
                    war_v2 = war_vt[1]
                    for hp2 in range(8):
                        for c1 in range(2):
                            p2 = it3 % 2
                            bank = pk3[p2]
                            for hd in range(2):
                                h = hp2 * 2 + hd
                                pb = (h % 4) * 32
                                tp = (pb, (hd * 64) % 128) if pb >= 96 else None
                                k.mm(bank[hd * 64:hd * 64 + 64, 0:128],
                                     latq_sb[pb:pb + 32,
                                             (h // 4) * 64:(h // 4) * 64 + 64],
                                     kk2_sb[pb:pb + 32,
                                            (h // 4) * MC + c1 * 128:
                                            (h // 4) * MC + (c1 + 1) * 128],
                                     start=True, stop=False,
                                     waits=(tok_lq, tok_rope, tok_v2,
                                            war_sc[p2]),
                                     tile_position=tp)
                                k.mm(bank[hd * 64:hd * 64 + 64, 0:128],
                                     latq_sb[pb:pb + 32,
                                             256 + (h // 4) * 64:
                                             256 + (h // 4) * 64 + 64],
                                     kk2_sb[pb:pb + 32,
                                            (4 + h // 4) * MC + c1 * 128:
                                            (4 + h // 4) * MC + (c1 + 1) * 128],
                                     start=False, stop=True,
                                     tile_position=tp)
                            tmm = k._emit("tensor", lambda e: e.nop(), (), "t", 1)
                            tvv = k.v(lambda e, p2=p2, bank=bank:
                                      e.tensor_add(sm3[p2][:], bank[:, 0:128],
                                                   C2_sb[:]),
                                      waits=(tmm, war_ex3[p2]))
                            war_sc[p2] = tvv
                            tvv = k.v(lambda e, p2=p2:
                                      e.reduce_max(nm3[p2][:], sm3[p2][:],
                                                   AX.X, negate=True))
                            tse = k.act(ee3[p2][:], sm3[p2][:], AF.Exp,
                                        bias=nm3[p2][:], accum=dn3[p2][:],
                                        waits=(tvv,))
                            war_ex3[p2] = tse
                            k.act(dn3[p2][:], dn3[p2][:], AF.Ln,
                                  waits=(tse,))
                            tvr = k.act(dn3[p2][:], dn3[p2][:], AF.Exp,
                                        scale=-1.0)
                            taw = k.act(ee3[p2][:], ee3[p2][:], AF.Copy,
                                        scale=dn3[p2][:], waits=(tvr,))
                            trt = k.tr(PTB[p2][:, 0:128], ee3[p2][:], idb_sb[:],
                                       waits=(taw, war_pt3[p2]))
                            tev = k.v(lambda e, p2=p2:
                                      e.tensor_copy(pT3[p2][:], PTB[p2][:, 0:128]),
                                      waits=(trt,))
                            war_pt3[p2] = tev
                            for hd in range(2):
                                h = hp2 * 2 + hd
                                pat = ps3["bc0"] if p2 == 0 else ps3["bc1"]
                                tpa = k.mm(pat[0:64, 0:64],
                                           pT3[p2][:, hd * 64:hd * 64 + 64],
                                           vt2_sb[:, (c1 * 16 + h) * 64:
                                                  (c1 * 16 + h + 1) * 64],
                                           start=True, stop=True,
                                           waits=(tev, tok_vt2, tok_vt2b,
                                                  war_pat[p2]))
                                if mc == 0:
                                    tac = k.v(lambda e, h=h, pat=pat:
                                              e.tensor_copy(
                                                  aac_sb[:, h * 64:(h + 1) * 64],
                                                  pat[0:64, 0:64]), waits=(tpa,))
                                else:
                                    tac = k.v(lambda e, h=h, pat=pat:
                                              e.tensor_add(
                                                  aac_sb[:, h * 64:(h + 1) * 64],
                                                  aac_sb[:, h * 64:(h + 1) * 64],
                                                  pat[0:64, 0:64]), waits=(tpa,))
                                war_pat[p2] = tac
                                tok_aac = tac
                            it3 += 1
                    war_pk3[0] = war_sc[0]
                    war_pk3[1] = war_sc[1]
                k.flush()

            tk_ar = k.dma(ar_in.ap(), aac_sb[:], waits=(tok_aac,), q="gd")
            tok_cc = k.g(lambda e: e.collective_compute(
                "AllReduce", ALU.add, replica_groups=[list(range(NCORES))],
                ins=[ar_in.ap().opt()], outs=[ar_out.ap().opt()]),
                waits=(tk_ar,), sig="cc")
            tk_r0 = k.dma(atRb_sb[0:64, :], ar_out.ap(), waits=(tok_cc,), q="gd")
            tk_r1 = k.dma(atRb_sb[64:128, :], ar_out.ap(), q="gd")

            with ExitStack() as PW:
                wp_sb = PW.enter_context(nc.sbuf_tensor("wp_sb2", [128, 8 * 1024],
                                                        BF16))
                w8_sb = PW.enter_context(nc.sbuf_tensor("w8_sb", [128, 8 * T],
                                                        BF16))
                tk_wp = k.dma(wp_sb[:].rearrange("p (n c) -> p n c", c=1024),
                              D["wp_t"], waits=(tok_aac,), q="gd")
                war_pw = [None, None]
                tok_w8 = None
                iw = 0
                for h in range(16):
                    hb = (h % 2) * 64
                    for ci, (lo, hi) in enumerate(CH_OWN):
                        w = hi - lo
                        bank = pk3[iw % 2]
                        tmm = k.mm(bank[hb:hb + 64, :w],
                                   atRb_sb[hb:hb + 64, h * 64:(h + 1) * 64],
                                   gts_sb[hb:hb + 64,
                                          (h // 2) * T + lo - HW:
                                          (h // 2) * T + hi - HW],
                                   start=True, stop=True,
                                   waits=(tk_r0, tk_r1, tok_gts,
                                          war_pw[iw % 2]))
                        tok_w8 = k.act(w8_sb[hb:hb + 64,
                                             (h // 2) * T + lo - HW:
                                             (h // 2) * T + hi - HW],
                                       bank[hb:hb + 64, :w], AF.Copy,
                                       waits=(tmm,))
                        war_pw[iw % 2] = tok_w8
                        iw += 1
                tok_x3 = None
                war_tb3 = None
                for cc in range(8):
                    for ci, (lo, hi) in enumerate(CH_OWN):
                        w = hi - lo
                        bank = pk3[iw % 2]
                        tmm = None
                        for kt in range(NPT):
                            tmm = k.mm(bank[:, :w],
                                       wp_sb[:, kt * 1024 + cc * 128:
                                             kt * 1024 + (cc + 1) * 128],
                                       w8_sb[:, kt * T + lo - HW:
                                             kt * T + hi - HW],
                                       start=(kt == 0), stop=(kt == NPT - 1),
                                       waits=(tk_wp, tok_w8, war_pw[iw % 2]))
                        tok_x3 = k.v(lambda e, cc=cc, lo=lo, w=w, bank=bank:
                                     e.tensor_add(xs(cc, lo, lo + w),
                                                  xs(cc, lo, lo + w),
                                                  bank[:, :w]),
                                     waits=(tmm, war_tb3))
                        war_pw[iw % 2] = tok_x3
                        war_tb3 = tok_x3
                        iw += 1
                k.flush()
            if "x3" in tap_d:
                k.dma(tap_d["x3"], x_sb[:].rearrange("p (n c) -> p n c", c=TH),
                      waits=(tok_x3,), q="y")
            tok_xn4 = emit_ln(CH_OWN, ps3, scr3, tok_x3,
                              dst_fn=lambda pt, lo, w: xns(pt, lo, lo + w))
            tapn("xn4", xn_sb[:], 8 * TH, (tok_xn4,))
            k.flush()
        LNS.close()

        m2_d = nc.dram_tensor("m2_d", [128, 32, 512], BF16)
        with ExitStack() as P4:
            with ExitStack() as PM:
                rmb_sb = PM.enter_context(nc.sbuf_tensor("rmb_sb",
                                                         [128, 32 * 128], BF16))
                rub_sb = PM.enter_context(nc.sbuf_tensor("rub_sb",
                                                         [128, 32 * 512], BF16))
                m2s = PM.enter_context(nc.sbuf_tensor("m2s", [128, 512], BF16))
                tk_rm = k.dma(rmb_sb[:].rearrange("p (n c) -> p n c", c=128),
                              D["rmbdT_t"], waits=(tok_xn4,), q="gd")
                tk_ru = k.dma(rub_sb[:].rearrange("p (n c) -> p n c", c=512),
                              D["rupbd_t"], waits=(tok_xn4,), q="gd")
                war_m2mm = None
                war_m2s = None
                for gi in range(32):
                    tmm = k.mm(PB[0][:, :],
                               rmb_sb[:, gi * 128:(gi + 1) * 128],
                               rub_sb[:, gi * 512:(gi + 1) * 512],
                               start=True, stop=True,
                               waits=(tk_rm, tk_ru, war_m2mm))
                    tev = k.act(m2s[:], PB[0][:, :], AF.Copy,
                                waits=(tmm, war_m2s))
                    war_m2mm = tev
                    war_m2s = k.dma(m2_d.ap()[:, gi, :], m2s[:], waits=(tev,),
                                    q="y")
                tok_m2 = war_m2s
                k.flush()

            hf_sb = P4.enter_context(nc.sbuf_tensor("hf_sb", [128, 16 * T], BF16))
            wcb_sb = P4.enter_context(nc.sbuf_tensor("wcb_sb", [128, 16 * 1024],
                                                     BF16))
            m2b = [P4.enter_context(nc.sbuf_tensor(f"m2b{i}", [128, 512],
                                                   BF16)) for i in range(2)]
            war_cp = [None, None]
            tok_x4 = None
            war_wcb = tok_m2
            war_m2b = [tok_m2, tok_m2]
            war_z = [None, None]
            war_hf = None
            tk_m2b_prev = None
            war_hfspill = None
            hfd = None
            if "hf" in taps:
                hfd = nc.dram_tensor("tap_hf", [128, 8, 16 * T], BF16,
                                     kind="ExternalOutput").ap()
            for fb in range(8):
                tk_wcb = k.dma(
                    wcb_sb[:].rearrange("p (n c) -> p n c", c=1024),
                    D["wc_t"][:, fb * 16:(fb + 1) * 16, :],
                    waits=(war_wcb,), q="gd")
                tok_hf = None
                for gl in range(4):
                    gi = fb * 4 + gl
                    tk_m2b = k.dma(m2b[gl % 2][:], m2_d.ap()[:, gi, :],
                                   waits=(war_m2b[gl % 2], tk_m2b_prev),
                                   q="y", sig="m")
                    tk_m2b_prev = tk_m2b
                    for zc in range(4):
                        for ci, (lo, hi) in enumerate(CH_OWN):
                            w = hi - lo
                            zb = PB[4 + (zc * 4 + ci) % 2]
                            tmm = k.mm(zb[:, :w],
                                       m2b[gl % 2][:, zc * 128:(zc + 1) * 128],
                                       xns(gi % 8, lo, hi),
                                       start=True, stop=True,
                                       waits=(tk_m2b, tok_xn4, tok_x4,
                                              war_z[(zc * 4 + ci) % 2]))
                            ft = gl * 4 + zc
                            tok_hf = k.act(
                                hf_sb[:, ft * T + lo - HW:ft * T + hi - HW],
                                zb[:, :w], AF.Gelu,
                                waits=(tmm, war_hf, war_hfspill))
                            war_z[(zc * 4 + ci) % 2] = tok_hf
                    war_m2b[gl % 2] = tok_hf
                if hfd is not None:
                    war_hfspill = k.dma(hfd[:, fb, :], hf_sb[:],
                                        waits=(tok_hf,), q="gd")
                war_hf = None
                last_mm = None
                icp = 0
                for cc in range(8):
                    for ci, (lo, hi) in enumerate(CH_OWN):
                        w = hi - lo
                        bank = PB[icp % 2]
                        tmm = None
                        for fkt in range(16):
                            tmm = k.mm(bank[:, :w],
                                       wcb_sb[:, fkt * 1024 + cc * 128:
                                              fkt * 1024 + (cc + 1) * 128],
                                       hf_sb[:, fkt * T + lo - HW:
                                             fkt * T + hi - HW],
                                       start=(fkt == 0), stop=(fkt == 15),
                                       waits=(tk_wcb, tok_hf,
                                              war_cp[icp % 2]))
                            last_mm = tmm
                        tok_x4 = k.v(lambda e, cc=cc, lo=lo, w=w, bank=bank:
                                     e.tensor_add(xs(cc, lo, lo + w),
                                                  xs(cc, lo, lo + w),
                                                  bank[:, :w]),
                                     waits=(tmm,))
                        war_cp[icp % 2] = tok_x4
                        icp += 1
                war_wcb = last_mm
                war_hf = tok_x4
            k.flush()

        tok_out = k.dma(out_x,
                        x_sb[:].rearrange("p (n c) -> p n c", c=TH)[:, :, HW:HW + T],
                        waits=(tok_x4,), q="y")
        k._emit("sync", lambda e: e.nop(), (tok_out,), None, 1)
        k.flush()

    return nc



def _tile_pm(a, kt, cols):
    return np.ascontiguousarray(a.reshape(kt, 128, cols).transpose(1, 0, 2))


def stage_inputs(inp):
    f = np.float32
    x = np.asarray(inp["L0_attn_x"], f)[0]
    xT = np.ascontiguousarray(x.T)

    shared = {}
    wq = np.asarray(inp["L0_attn_Wq"], f)
    shared["wq_t"] = np.ascontiguousarray(
        wq.reshape(8, 128, 8, 128).transpose(2, 1, 0, 3))
    wkv = np.asarray(inp["L0_attn_Wkv"], f)
    kcols = wkv[:, :1024].reshape(1024, 8, 128)
    vcols = wkv[:, 1024:].reshape(1024, 8, 128)
    kv = np.concatenate([kcols, vcols], axis=2)
    shared["wkv_t"] = np.ascontiguousarray(
        kv.reshape(8, 128, 8, 256).transpose(2, 1, 0, 3))
    shared["wwin_t"] = _tile_pm(np.asarray(inp["L0_attn_Wwin"], f), 8, 32)
    shared["bwin_c"] = np.asarray(inp["L0_attn_bwin"], f).reshape(32, 1)
    for nm, key in (("qg", "L0_attn_q_g"), ("qb", "L0_attn_q_b"),
                    ("kg", "L0_attn_k_g"), ("kb", "L0_attn_k_b")):
        v = np.asarray(inp[key], f)
        shared[nm + "_c"] = np.ascontiguousarray(
            np.concatenate([v, v]).reshape(128, 1))
    shared["wout_t"] = _tile_pm(np.asarray(inp["L0_attn_Wout"], f), 8, 1024)
    shared["gam_c"] = np.ascontiguousarray(
        np.asarray(inp["L0_ffn_ln_gamma"], f).reshape(8, 128).T)
    shared["bet_c"] = np.ascontiguousarray(
        np.asarray(inp["L0_ffn_ln_beta"], f).reshape(8, 128).T)
    fc1q = np.asarray(inp["L0_ffn_fc1_q"], np.int32)
    shared["fc1q_t"] = np.ascontiguousarray(
        fc1q.reshape(8, 128, 2, 2048).transpose(2, 1, 0, 3))
    shared["fc1s_t"] = np.ascontiguousarray(
        np.asarray(inp["L0_ffn_fc1_scales"], f).reshape(8, 2, 2048)
        .transpose(1, 0, 2).reshape(2, 32, 1, 512))
    shared["fc1b_c"] = np.ascontiguousarray(
        np.asarray(inp["L0_ffn_fc1_bias"], f).reshape(2, 16, 128)
        .transpose(0, 2, 1))
    fc2q = np.asarray(inp["L0_ffn_fc2_q"], np.int32)
    shared["fc2q_t"] = np.ascontiguousarray(
        fc2q.reshape(2, 16, 128, 1024).transpose(0, 2, 1, 3))
    shared["fc2s_t"] = np.ascontiguousarray(
        np.asarray(inp["L0_ffn_fc2_scales"], f).reshape(2, 32, 1, 512))
    shared["fc2b_c"] = np.ascontiguousarray(
        np.asarray(inp["L0_ffn_fc2_bias"], f).reshape(8, 128).T)

    wk = np.asarray(inp["L1_attn_W_k"], f)
    perm = np.empty(1024, np.int64)
    for h in range(16):
        for i in range(32):
            perm[h * 32 + i] = h * 64 + 2 * i
            perm[512 + h * 32 + i] = h * 64 + 2 * i + 1
    shared["wk_t"] = _tile_pm(np.ascontiguousarray(wk.T[:, perm]), 8, 1024)
    shared["wv_t"] = _tile_pm(np.ascontiguousarray(
        np.asarray(inp["L1_attn_W_v"], f).T), 8, 1024)
    shared["wg_t"] = _tile_pm(np.ascontiguousarray(
        np.asarray(inp["L1_attn_W_g"], f).T), 8, 1024)
    shared["wp_t"] = _tile_pm(np.ascontiguousarray(
        np.asarray(inp["L1_attn_W_p"], f).T), 8, 1024)
    lq = np.asarray(inp["L1_attn_latent_queries"], f)[0]
    la = np.zeros((128, 256), f)
    lb = np.zeros((128, 256), f)
    for h in range(16):
        q = lq[:, h, :]
        pbase = (h % 4) * 32
        cbase = (h // 4) * 64
        la[pbase:pbase + 32, cbase:cbase + 64] = q.T[0::2, :]
        lb[pbase:pbase + 32, cbase:cbase + 64] = q.T[1::2, :]
    shared["latq_a"] = la
    shared["latq_b"] = lb

    rm = np.asarray(inp["L1_ffn_rail_maker"], f)
    rup = np.asarray(inp["L1_ffn_rail_up_proj"], f)
    rmT = np.zeros((32, 128, 128), f)
    rub = np.zeros((32, 128, 512), f)
    for h in range(4):
        for g in range(8):
            gi = h * 8 + g
            for rl in range(8):
                r = g * 8 + rl
                rmT[gi, rl * 16:(rl + 1) * 16, rl * 16:(rl + 1) * 16] = rm[h, r].T
                rub[gi, rl * 16:(rl + 1) * 16, rl * 64:(rl + 1) * 64] = rup[h, r].T
    shared["rmbdT_t"] = np.ascontiguousarray(rmT.transpose(1, 0, 2))
    shared["rupbd_t"] = np.ascontiguousarray(rub.transpose(1, 0, 2))
    wc = np.asarray(inp["L1_ffn_w_cproj"], f)
    shared["wc_t"] = np.ascontiguousarray(
        wc.T.reshape(128, 128, 1024).transpose(1, 0, 2))

    in_maps = []
    inv = 1.0 / (10000.0 ** (np.arange(0, 64, 2, dtype=np.float64) / 64.0))
    for c in range(NCORES):
        m = dict(shared)
        t0 = c * T
        xt = np.zeros((1024, TH), f)
        lo = max(0, t0 - HW); hi = min(16384, t0 + T + HW)
        xt[:, lo - (t0 - HW):hi - (t0 - HW)] = xT[:, lo:hi]
        m["x_t"] = np.ascontiguousarray(xt.reshape(8, 128, TH).transpose(1, 0, 2))
        km = np.zeros((1, TH), f)
        km[0, lo - (t0 - HW):hi - (t0 - HW)] = 1.0
        m["kv_mask"] = km
        ang = np.arange(t0, t0 + T, dtype=np.float64)[None, :] * inv[:, None]
        m["cos_t"] = np.ascontiguousarray(np.tile(np.cos(ang).astype(f), (4, 1)))
        m["sin_t"] = np.ascontiguousarray(np.tile(np.sin(ang).astype(f), (4, 1)))
        in_maps.append(m)
    return in_maps


_NC_CACHE = {}


def _get_nc(taps=()):
    key = tuple(taps)
    if key not in _NC_CACHE:
        _NC_CACHE[key] = build_program(taps=key)
    return _NC_CACHE[key]


def run_cores(inputs, taps=()):
    in_maps = stage_inputs(inputs)
    nc = _get_nc(taps)
    return run_bass_kernel_spmd(nc, in_maps, core_ids=list(range(NCORES)))


def kernel(**inputs):
    res = run_cores(inputs)
    y = np.empty((16384, 1024), np.float32)
    for i, r in enumerate(res.results):
        p = r["out_x"]
        y[T * i:T * (i + 1)] = p.transpose(1, 0, 2).reshape(1024, T).T
    return y[None]


# version 23
# speedup vs baseline: 1.0015x; 1.0015x over previous
import numpy as np
import ml_dtypes
from contextlib import ExitStack
import concourse.bass as bass
import concourse.mybir as mybir
from concourse.bass_utils import run_bass_kernel_spmd

F32 = mybir.dt.float32
BF16 = mybir.dt.bfloat16
I32 = mybir.dt.int32
AF = mybir.ActivationFunctionType
ALU = mybir.AluOpType
AX = mybir.AxisListType

NCORES = 8
T = 2048
HW = 8
TH = T + 2 * HW
NPT = 8
EPS = 1e-5

ENGINES = ("sync", "gpsimd", "scalar", "vector", "tensor")


class K:

    def __init__(self, nc, es, block):
        self.nc = nc
        self.block = block
        chans = ("t", "s", "v", "g", "y", "gd", "cc", "m")
        self.sem = {c: es.enter_context(nc.semaphore("sm_" + c)) for c in chans}
        self.cnt = dict.fromkeys(chans, 0)
        self.maxw = {e: dict.fromkeys(chans, 0) for e in ENGINES}
        self.pend = {e: [] for e in ENGINES}

    def _emit(self, eng, fn, waits, sig, inc):
        tok = None
        if sig is not None:
            self.cnt[sig] += inc
            tok = (sig, self.cnt[sig])
        self.pend[eng].append((fn, tuple(w for w in waits if w is not None), sig, inc))
        return tok

    def dma(self, out, in_, waits=(), q="gd", sig=None):
        eng = "gpsimd" if q == "gd" else "sync"
        return self._emit(eng, lambda e, o=out, i=in_: e.dma_start(out=o, in_=i),
                          waits, sig if sig is not None else q, 16)

    def mm(self, out, lhsT, rhs, start=True, stop=True, waits=(), sig="t",
           tile_position=None):
        return self._emit(
            "tensor",
            lambda e, o=out, l=lhsT, r=rhs, a=start, b=stop, tp=tile_position:
                e.matmul(o, l, r, start=a, stop=b, tile_position=tp),
            waits, sig, 1)

    def tr(self, out, in_, ident, waits=(), sig="t"):
        return self._emit(
            "tensor", lambda e, o=out, i=in_, d=ident: e.transpose(o, i, d),
            waits, sig, 1)

    def act(self, out, in_, func, bias=0.0, scale=1.0, accum=None, waits=(), sig="s"):
        return self._emit(
            "scalar",
            lambda e, o=out, i=in_, f=func, b=bias, sc=scale, a=accum:
                e.activation(o, i, f, bias=b, scale=sc, accum_out=a),
            waits, sig, 1)

    def v(self, fn, waits=(), sig="v"):
        return self._emit("vector", fn, waits, sig, 1)

    def g(self, fn, waits=(), sig="g", inc=1):
        return self._emit("gpsimd", fn, waits, sig, inc)

    def barrier_tok(self):
        return [(c, self.cnt[c]) for c in ("t", "s", "v", "g", "y", "gd")]

    def flush(self):
        for eng in ENGINES:
            ops = self.pend[eng]
            if not ops:
                continue
            self.pend[eng] = []
            sem = self.sem
            maxw = self.maxw[eng]

            def body(e, ops=ops, maxw=maxw, sem=sem):
                for fn, waits, sig, inc in ops:
                    for ch, val in waits:
                        if maxw[ch] < val:
                            e.wait_ge(sem[ch], val)
                            maxw[ch] = val
                    inst = fn(e)
                    if sig is not None:
                        inst.then_inc(sem[sig], inc)
            getattr(self.block, eng)(body)


def _bd(a):
    return a.astype(ml_dtypes.bfloat16)


def build_program(taps=()):
    nc = bass.Bass("TRN2", target_bir_lowering=False)

    D = {}

    def din(name, shape, dt=F32):
        D[name] = nc.dram_tensor(name, list(shape), dt, kind="ExternalInput").ap()
        return D[name]

    din("x_t", (128, NPT, TH))
    din("kv_mask", (1, TH))
    din("cos_t", (128, T))
    din("sin_t", (128, T))
    din("wq_t", (8, 128, 8, 128))
    din("wkv_t", (8, 128, 8, 256))
    din("wwin_t", (128, 8, 32))
    din("bwin_c", (32, 1))
    din("qg_c", (128, 1)); din("qb_c", (128, 1))
    din("kg_c", (128, 1)); din("kb_c", (128, 1))
    din("wout_t", (128, 8, 1024))
    din("gam_c", (128, 8)); din("bet_c", (128, 8))
    din("fc1q_t", (2, 128, 8, 2048), I32)
    din("fc1s_t", (2, 32, 1, 512))
    din("fc1b_c", (2, 128, 16))
    din("fc2q_t", (2, 128, 16, 1024), I32)
    din("fc2s_t", (2, 32, 1, 512))
    din("fc2b_c", (128, 8))
    din("wk_t", (128, 8, 1024))
    din("wv_t", (128, 8, 1024))
    din("wg_t", (128, 8, 1024))
    din("wp_t", (128, 8, 1024))
    din("latq_a", (128, 256)); din("latq_b", (128, 256))
    din("rmbdT_t", (128, 32, 128))
    din("rupbd_t", (128, 32, 512))
    din("wc_t", (128, 128, 1024))

    out_x = nc.dram_tensor("out_x", [128, NPT, T], F32, kind="ExternalOutput").ap()
    tap_d = {}
    for tname in taps:
        if tname in ("x1", "x2", "x3", "x4"):
            tap_d[tname] = nc.dram_tensor("tap_" + tname, [128, NPT, TH], F32,
                                          kind="ExternalOutput").ap()

    ar_in = nc.dram_tensor("ar_in", [64, 1024], F32)
    ar_out = nc.dram_tensor("ar_out", [64, 1024], F32)

    ident_f = nc.inline_tensor(np.eye(128, dtype=np.float32), name="c_identf").ap()
    ident_b = nc.inline_tensor(_bd(np.eye(128, dtype=np.float32)), name="c_identb").ap()
    ones1 = nc.inline_tensor(_bd(np.ones((128, 1), np.float32)), name="c_ones1").ap()
    onesr = nc.inline_tensor(_bd(np.ones((1, 128), np.float32)), name="c_onesr").ap()
    e2h = np.zeros((128, 2), np.float32); e2h[:64, 0] = 1; e2h[64:, 1] = 1
    e2h_c = nc.inline_tensor(_bd(e2h), name="c_e2h").ap()
    e2t = np.zeros((2, 128), np.float32); e2t[0, :64] = 1; e2t[1, 64:] = 1
    e2t_c = nc.inline_tensor(_bd(e2t), name="c_e2t").ap()
    jj, ll = np.meshgrid(np.arange(144), np.arange(128))
    R = np.abs(jj - ll - 8.0).astype(np.float32)
    R_c = nc.inline_tensor(R, name="c_R").ap()
    NEG2 = np.where((jj - ll >= 0) & (jj - ll <= 16), -5e3, -1e9).astype(np.float32)
    NEG2_c = nc.inline_tensor(NEG2, name="c_NEG2").ap()
    wsc1 = np.concatenate([np.full(16, 4.0, np.float32),
                           np.full(16, 4.75, np.float32)]).reshape(32, 1)
    wsc1_c = nc.inline_tensor(wsc1, name="c_wsc1").ap()
    wsc2 = np.concatenate([np.full(16, 4.5, np.float32),
                           np.full(16, 5.25, np.float32)]).reshape(32, 1)
    wsc2_c = nc.inline_tensor(wsc2, name="c_wsc2").ap()
    cc_, ll2 = np.meshgrid(np.arange(128), np.arange(64))
    C1 = np.where(cc_ <= ll2, 0.0, -1e9).astype(np.float32)
    C2_c = nc.inline_tensor(np.concatenate([C1, C1], 0), name="c_C2").ap()

    with ExitStack() as ES:
        block = ES.enter_context(nc.Block())
        k = K(nc, ES, block)

        def tapn(name, ap, cols, waits):
            if name in taps:
                d = nc.dram_tensor("tap_" + name, [128, cols], F32,
                                   kind="ExternalOutput").ap()
                k.dma(d, ap, waits=waits, q="gd")

        x_sb = ES.enter_context(nc.sbuf_tensor("x_sb", [128, NPT * TH], F32))
        xn_sb = ES.enter_context(nc.sbuf_tensor("xn_sb", [128, NPT * TH], BF16))
        PB = [ES.enter_context(nc.psum_tensor(f"pbank{i}", [128, 512], F32))
              for i in range(6)]
        PTB = [ES.enter_context(nc.psum_tensor(f"ptbank{i}", [128, 1024], BF16))
               for i in range(2)]
        LNS = ES.enter_context(ExitStack())
        idb_sb = LNS.enter_context(nc.sbuf_tensor("idb_sb", [128, 128], BF16))
        ones1_sb = LNS.enter_context(nc.sbuf_tensor("ones1_sb", [128, 1], BF16))
        onesr_sb = LNS.enter_context(nc.sbuf_tensor("onesr_sb", [1, 128], BF16))
        e2h_sb = LNS.enter_context(nc.sbuf_tensor("e2h_sb", [128, 2], BF16))
        e2t_sb = LNS.enter_context(nc.sbuf_tensor("e2t_sb", [2, 128], BF16))
        C2_sb = LNS.enter_context(nc.sbuf_tensor("C2_sb", [128, 128], F32))
        mask_hb = LNS.enter_context(nc.sbuf_tensor("mask_hb", [128, 16], BF16))
        rowA = LNS.enter_context(nc.sbuf_tensor("rowA", [2, 512], F32))
        rowB = LNS.enter_context(nc.sbuf_tensor("rowB", [2, 512], F32))
        rowC = LNS.enter_context(nc.sbuf_tensor("rowC", [2, 512], F32))
        rAh = LNS.enter_context(nc.sbuf_tensor("rAh", [2, 512], BF16))
        rAl = LNS.enter_context(nc.sbuf_tensor("rAl", [2, 512], BF16))
        rBh = LNS.enter_context(nc.sbuf_tensor("rBh", [2, 512], BF16))
        rBl = LNS.enter_context(nc.sbuf_tensor("rBl", [2, 512], BF16))
        eps_sb = LNS.enter_context(nc.sbuf_tensor("eps_sb", [128, 1], F32))

        tk_const = []
        for dst, src in ((idb_sb, ident_b), (ones1_sb, ones1),
                         (onesr_sb, onesr), (e2h_sb, e2h_c), (e2t_sb, e2t_c),
                         (C2_sb, C2_c)):
            tk_const.append(k.dma(dst[:], src, q="y"))
        tok_eps = k.v(lambda e: e.memset(eps_sb[:], EPS))
        tk_x = k.dma(x_sb[:].rearrange("p (n c) -> p n c", c=TH), D["x_t"], q="y")

        def xs(pt, lo, hi):
            return x_sb[:, pt * TH + lo: pt * TH + hi]

        def xns(pt, lo, hi):
            return xn_sb[:, pt * TH + lo: pt * TH + hi]

        CH_TH = [(0, 512, 0), (512, 1024, 512), (1024, 1536, 1024),
                 (1536, 2048, 1536), (1552, TH, 2048)]
        CH_T = [(HW, 520), (520, 1032), (1032, 1544), (1544, 2056)]

        state = {"war_cb": None, "war_st": None, "war_bc": None, "war_rows": None}

        def rows_chunk(W, P, tok_in, nfeat):
            k.v(lambda e: e.tensor_scalar_mul(rowA[0:P, :W], rowA[0:P, :W],
                                              1.0 / nfeat), waits=(tok_in,))
            k.v(lambda e: e.tensor_mul(rowC[0:P, :W], rowA[0:P, :W],
                                       rowA[0:P, :W]))
            tvv = k.v(lambda e: e.scalar_tensor_tensor(
                rowB[0:P, :W], rowB[0:P, :W], 1.0 / nfeat, rowC[0:P, :W],
                ALU.mult, ALU.subtract))
            tss = k.act(rowB[0:P, :W], rowB[0:P, :W], AF.Ln,
                        bias=eps_sb[0:P, :], waits=(tvv, tok_eps))
            tss = k.act(rowB[0:P, :W], rowB[0:P, :W], AF.Exp, scale=-0.5)
            k.v(lambda e: e.tensor_copy(rAh[0:P, :W], rowA[0:P, :W]))
            k.v(lambda e: e.tensor_sub(rAl[0:P, :W], rowA[0:P, :W], rAh[0:P, :W]))
            k.v(lambda e: e.tensor_copy(rBh[0:P, :W], rowB[0:P, :W]),
                waits=(tss,))
            return k.v(lambda e: e.tensor_sub(rBl[0:P, :W], rowB[0:P, :W],
                                              rBh[0:P, :W]))

        def bcast2(ps, W, lhs, tok_rows, war):
            k.mm(ps["bc0"][:, :W], lhs, rAh[0:lhs.shape[0], :W],
                 start=True, stop=False, waits=(tok_rows, war))
            b1 = k.mm(ps["bc0"][:, :W], lhs, rAl[0:lhs.shape[0], :W],
                      start=False, stop=True)
            k.mm(ps["bc1"][:, :W], lhs, rBh[0:lhs.shape[0], :W],
                 start=True, stop=False)
            b2 = k.mm(ps["bc1"][:, :W], lhs, rBl[0:lhs.shape[0], :W],
                      start=False, stop=True)
            return (b1, b2)

        def emit_ln(rng, ps, scr, tok_ready, dst_fn=None, add_to_x=False,
                    xin_fn=None, src_fn=None, src_bf16=False, gain=None,
                    dbg=False):
            if src_fn is None:
                src_fn = xs
            out_tok = None
            war_cb = [state.get("war_cb"), state.get("war_cb")]
            war_sq = [state.get("war_cb"), state.get("war_cb")]
            for chunk in rng:
                lo, hi = chunk[0], chunk[1]
                alo = chunk[2] if len(chunk) > 2 else lo
                w = hi - lo
                off = alo - lo
                wa = hi - alo
                tm1 = tm2 = None
                for pt in range(NPT):
                    pr = pt % 2
                    cb = scr["cb"] if pr == 0 else scr["cb2"]
                    sq = scr["sq"] if pr == 0 else scr["sq2"]
                    if src_bf16:
                        cbv = src_fn(pt, lo, lo + w)
                        tv1 = tok_ready
                    else:
                        tv1 = k.v(lambda e, pt=pt, lo=lo, w=w, cb=cb:
                                  e.tensor_copy(cb[:, :w], src_fn(pt, lo, lo + w)),
                                  waits=(tok_ready, war_cb[pr]))
                        cbv = cb[:, :w]
                    tv2 = k.v(lambda e, w=w, cbv=cbv, sq=sq:
                              e.tensor_mul(sq[:, :w], cbv, cbv),
                              waits=(war_sq[pr], tv1))
                    tm1 = k.mm(ps["st0"][0:1, :w], ones1_sb[:], cbv,
                               start=(pt == 0), stop=(pt == NPT - 1),
                               waits=(tv1, state["war_st"]))
                    tm2 = k.mm(ps["st1"][0:1, :w], ones1_sb[:], sq[:, :w],
                               start=(pt == 0), stop=(pt == NPT - 1),
                               waits=(tv2, state["war_st"]))
                    war_cb[pr] = tm1
                    war_sq[pr] = tm2
                state["war_cb"] = tm2
                k.v(lambda e, w=w: e.tensor_copy(rowA[0:1, :w], ps["st0"][0:1, :w]),
                    waits=(tm1, state["war_rows"]))
                tev = k.v(lambda e, w=w: e.tensor_copy(rowB[0:1, :w],
                                                       ps["st1"][0:1, :w]),
                          waits=(tm2,))
                state["war_st"] = tev
                tok_rows = rows_chunk(w, 1, tev, 1024.0)
                b1, b2 = bcast2(ps, w, onesr_sb[:], tok_rows, state["war_bc"])
                for pt in range(NPT):
                    k.v(lambda e, pt=pt, lo=lo, w=w:
                        e.tensor_sub(scr["t1"][:, :w], src_fn(pt, lo, lo + w),
                                     ps["bc0"][:, :w]),
                        waits=(b1, b2))
                    if add_to_x or xin_fn is not None:
                        k.v(lambda e, w=w:
                            e.tensor_mul(scr["t2"][:, :w], scr["t1"][:, :w],
                                         ps["bc1"][:, :w]))
                        tmv = None
                        if add_to_x:
                            tmv = k.v(lambda e, pt=pt, alo=alo, wa=wa:
                                      e.tensor_add(xs(pt, alo, alo + wa),
                                                   xs(pt, alo, alo + wa),
                                                   scr["t2"][:, :wa]))
                        if xin_fn is not None:
                            tmv = k.v(lambda e, pt=pt, alo=alo, wa=wa:
                                      e.tensor_copy(xin_fn(pt, alo, wa),
                                                    scr["t2"][:, :wa]))
                        out_tok = tmv
                    elif gain is not None:
                        k.v(lambda e, w=w:
                            e.tensor_mul(scr["t2"][:, :w], scr["t1"][:, :w],
                                         ps["bc1"][:, :w]))
                        out_tok = k.v(lambda e, pt=pt, alo=alo, wa=wa:
                                      e.tensor_scalar(dst_fn(pt, alo, wa),
                                                      scr["t2"][:, :wa],
                                                      gain[0](pt), gain[1](pt),
                                                      ALU.mult, ALU.add),
                                      waits=gain[2] if len(gain) > 2 else ())
                        if dbg and pt == 0 and lo == rng[0][0] and "dbg" in taps:
                            ddb = nc.dram_tensor("tap_dbg", [128, 2048], F32,
                                                 kind="ExternalOutput").ap()
                            db_sb = scr["dbg"]
                            k.v(lambda e: e.tensor_copy(db_sb[:, 0:512],
                                                        ps["bc0"][:, :]),
                                waits=(out_tok,))
                            k.v(lambda e: e.tensor_copy(db_sb[:, 512:1024],
                                                        ps["bc1"][:, :]))
                            k.v(lambda e: e.tensor_copy(db_sb[:, 1024:1536],
                                                        scr["t1"][:, :]))
                            tdb = k.v(lambda e: e.tensor_copy(db_sb[:, 1536:2048],
                                                              scr["t2"][:, :]))
                            k.dma(ddb, db_sb[:], waits=(tdb,), q="y")
                    elif wa == w:
                        out_tok = k.v(lambda e, pt=pt, alo=alo, wa=wa:
                                      e.tensor_mul(dst_fn(pt, alo, wa),
                                                   scr["t1"][:, :wa],
                                                   ps["bc1"][:, :wa]))
                    else:
                        tvm = k.v(lambda e, w=w:
                                  e.tensor_mul(scr["t2"][:, :w], scr["t1"][:, :w],
                                               ps["bc1"][:, :w]))
                        out_tok = k.act(dst_fn(pt, alo, wa),
                                        scr["t2"][:, off:off + wa], AF.Copy,
                                        waits=(tvm,))
                state["war_bc"] = out_tok
                state["war_rows"] = out_tok
            return out_tok

        with ExitStack() as PH:
            ps = {"st0": PB[4], "st1": PB[5], "bc0": PB[2], "bc1": PB[3]}
            pmh = PB[0][:, 0:16]
            scr = {
                "cb": PH.enter_context(nc.sbuf_tensor("s_cb", [128, 512], BF16)),
                "cb2": PH.enter_context(nc.sbuf_tensor("s_cb2", [128, 512], BF16)),
                "sq": PH.enter_context(nc.sbuf_tensor("s_sq", [128, 512], BF16)),
                "sq2": PH.enter_context(nc.sbuf_tensor("s_sq2", [128, 512], BF16)),
                "t1": PH.enter_context(nc.sbuf_tensor("s_t1", [128, 512], F32)),
                "t2": PH.enter_context(nc.sbuf_tensor("s_t2", [128, 512], F32)),
            }
            msk_f = PH.enter_context(nc.sbuf_tensor("msk_f", [1, TH], F32))
            msk_b = PH.enter_context(nc.sbuf_tensor("msk_b", [1, TH], BF16))
            tk_msk = k.dma(msk_f[:], D["kv_mask"], q="y")
            tok_xn = emit_ln(CH_TH, ps, scr, tk_x,
                             dst_fn=lambda pt, lo, w: xns(pt, lo, lo + w))
            tmv = k.v(lambda e: e.tensor_copy(msk_b[:], msk_f[:]), waits=(tk_msk,))
            k.mm(pmh[:, 0:8], onesr_sb[:], msk_b[:, 0:8],
                 waits=(tmv, tk_const[3]))
            t2m = k.mm(pmh[:, 8:16], onesr_sb[:], msk_b[:, TH - 8:TH])
            tok_mask = k.v(lambda e: e.tensor_copy(mask_hb[:], pmh[:]),
                           waits=(t2m,))
            tapn("xn", xn_sb[:], NPT * TH, (tok_xn,))
            k.flush()

        with ExitStack() as PH:
            ao_sb = PH.enter_context(nc.sbuf_tensor("ao_sb", [128, 8 * T], BF16))
            qgb = {nm: PH.enter_context(nc.sbuf_tensor(nm + "_sb", [128, 1], F32))
                   for nm in ("qg", "qb", "kg", "kb")}
            g8 = {nm: PH.enter_context(nc.sbuf_tensor(nm + "8", [128, 1], F32))
                  for nm in ("qg", "qb")}
            scr = {
                "sq": PH.enter_context(nc.sbuf_tensor("a1_sq", [128, 512], BF16)),
                "t1": PH.enter_context(nc.sbuf_tensor("a1_t1", [128, 512], F32)),
                "t2": PH.enter_context(nc.sbuf_tensor("a1_t2", [128, 512], F32)),
                "tb": PH.enter_context(nc.sbuf_tensor("a1_tb", [128, 512], BF16)),
            }
            pk = [PB[0], PB[1]]
            pqs = PB[4][0:2, :]
            pqq = PB[5][0:2, :]
            bc0, bc1 = PB[2], PB[3]
            pss = [PB[0][:, 0:144], PB[1][:, 0:144],
                   PB[0][:, 144:288], PB[1][:, 144:288]]
            pta = [PTB[0][:, 0:128], PTB[1][:, 0:128],
                   PTB[0][:, 512:640], PTB[1][:, 512:640]]
            ptb = [PTB[0][0:16, 128:256], PTB[1][0:16, 128:256],
                   PTB[0][0:16, 640:768], PTB[1][0:16, 640:768]]
            pav = [PB[2][:, 0:128], PB[3][:, 0:128],
                   PB[2][:, 128:256], PB[3][:, 128:256]]
            ptv = [PTB[0][:, 256:384], PTB[1][:, 256:384]]
            pwp = PB[2][0:32, :]
            pwt = PB[3][:, 0:32]

            P1 = PH.enter_context(ExitStack())
            wq_sb = [P1.enter_context(nc.sbuf_tensor(f"wq{i}", [128, 8 * 128], BF16))
                     for i in range(2)]
            wkv_sb = [P1.enter_context(nc.sbuf_tensor(f"wkv{i}", [128, 8 * 256],
                                                      BF16)) for i in range(2)]
            wwin_sb = P1.enter_context(nc.sbuf_tensor("wwin_sb", [128, 8 * 32],
                                                      BF16))
            bwin_sb = P1.enter_context(nc.sbuf_tensor("bwin_sb", [32, 1], F32))
            wsc1_sb = P1.enter_context(nc.sbuf_tensor("wsc1_sb", [32, 1], F32))
            wsc2_sb = P1.enter_context(nc.sbuf_tensor("wsc2_sb", [32, 1], F32))
            idf_sb = P1.enter_context(nc.sbuf_tensor("idf_sb", [128, 128], F32))
            R_sb = P1.enter_context(nc.sbuf_tensor("R_sb", [128, 144], F32))
            NEG2_sb = P1.enter_context(nc.sbuf_tensor("NEG2_sb", [128, 144],
                                                      F32))
            q_sb = P1.enter_context(nc.sbuf_tensor("q_sb", [128, T], BF16))
            kk_sb = P1.enter_context(nc.sbuf_tensor("kk_sb", [128, TH], BF16))
            v_sb = P1.enter_context(nc.sbuf_tensor("v_sb", [128, TH], BF16))
            vtm_sb = P1.enter_context(nc.sbuf_tensor("vtm_sb", [128, 17 * 128],
                                                     BF16))
            sh2_sb = P1.enter_context(nc.sbuf_tensor("sh2_sb", [128, 16 * 16], F32))
            wsh_sb = P1.enter_context(nc.sbuf_tensor("wsh_sb", [128, 16 * 16], F32))
            s3_sb = [P1.enter_context(nc.sbuf_tensor(f"s3_{i}", [128, 144], F32))
                     for i in range(4)]
            m2_sb = [P1.enter_context(nc.sbuf_tensor(f"mm2_{i}", [128, 144], F32))
                     for i in range(4)]
            ee_all = [P1.enter_context(nc.sbuf_tensor(f"eea_{i}",
                                                      [128, 16 * 144], BF16))
                      for i in range(2)]
            aw_sb = [P1.enter_context(nc.sbuf_tensor(f"aw_{i}", [128, 144], BF16))
                     for i in range(4)]
            at1_sb = [P1.enter_context(nc.sbuf_tensor(f"at1_{i}", [128, 128], BF16))
                      for i in range(4)]
            at2_sb = [P1.enter_context(nc.sbuf_tensor(f"at2_{i}", [16, 128], BF16))
                      for i in range(4)]
            nmx = [P1.enter_context(nc.sbuf_tensor(f"nmx_{i}", [128, 16], F32))
                   for i in range(2)]
            den = [P1.enter_context(nc.sbuf_tensor(f"den_{i}", [128, 16], F32))
                   for i in range(2)]

            _a0done = (tok_xn, tok_mask)
            tk_wwin = k.dma(wwin_sb[:].rearrange("p (n c) -> p n c", c=32),
                            D["wwin_t"], waits=_a0done, q="gd")
            tk_bwin = k.dma(bwin_sb[:], D["bwin_c"], waits=_a0done, q="y")
            tk_wsc = k.dma(wsc1_sb[:], wsc1_c, waits=_a0done, q="y")
            tk_wsc = k.dma(wsc2_sb[:], wsc2_c, waits=_a0done, q="y")
            tk_idf = k.dma(idf_sb[:], ident_f, waits=_a0done, q="y")
            tk_R = k.dma(R_sb[:], R_c, waits=_a0done, q="y")
            tk_NEG = k.dma(NEG2_sb[:], NEG2_c, waits=_a0done, q="y")
            tk_gb = []
            for nm in ("qg", "qb", "kg", "kb"):
                tk_gb.append(k.dma(qgb[nm][:], D[nm + "_c"], waits=_a0done, q="y"))
            tk_wq = [None, None]
            tk_wkv = [None, None]
            tk_wq[0] = k.dma(wq_sb[0][:].rearrange("p (n c) -> p n c", c=128),
                             D["wq_t"][0], waits=_a0done, q="gd")
            tk_wkv[0] = k.dma(wkv_sb[0][:].rearrange("p (n c) -> p n c", c=256),
                              D["wkv_t"][0], waits=_a0done, q="gd")

            WP = ExitStack()
            wp_sb = WP.enter_context(nc.sbuf_tensor("wp_sb", [32, T], F32))
            wsp_tm = WP.enter_context(nc.sbuf_tensor("wsp_tm", [128, 16 * 32],
                                                     F32))
            tok_wp = None
            for ci, (lo, hi) in enumerate(CH_T):
                w = hi - lo
                tmm = None
                for pt in range(NPT):
                    tmm = k.mm(pwp[:, :w], wwin_sb[:, pt * 32:(pt + 1) * 32],
                               xns(pt, lo, hi), start=(pt == 0),
                               stop=(pt == NPT - 1),
                               waits=(tk_wwin, tok_xn, tok_wp))
                tok_wp = k.act(wp_sb[:, lo - HW:hi - HW], pwp[:, :w], AF.Silu,
                               bias=bwin_sb[:], waits=(tmm, tk_bwin))
            tt_ = k.act(wp_sb[:], wp_sb[:], AF.Tanh, scale=0.5, waits=(tok_wp,))
            tv_s = k.v(lambda e: e.tensor_scalar(wp_sb[:], wp_sb[:],
                                                 wsc1_sb[:], wsc2_sb[:],
                                                 ALU.mult, ALU.add),
                       waits=(tt_, tk_wsc))
            tok_wt = None
            for tt2 in range(16):
                trt = k.tr(pwt[:], wp_sb[:, tt2 * 128:(tt2 + 1) * 128],
                           idf_sb[0:32, 0:32], waits=(tv_s, tok_wt, tk_idf))
                tok_wt = k.v(lambda e, tt2=tt2:
                             e.tensor_copy(wsp_tm[:, tt2 * 32:(tt2 + 1) * 32],
                                           pwt[:]), waits=(trt,))
            w3 = wsp_tm[:].rearrange("p (t c) -> p t c", c=32)
            s3view = sh2_sb[:].rearrange("p (t c) -> p t c", c=16)
            wsview = wsh_sb[:].rearrange("p (t c) -> p t c", c=16)
            k.v(lambda e: e.tensor_scalar_mul(s3view, w3[:, :, 16:32], -0.5),
                waits=(tok_wt,))
            k.v(lambda e: e.tensor_mul(wsview, w3[:, :, 0:16], w3[:, :, 16:32]))
            tok_wsp = k.v(lambda e: e.tensor_scalar_mul(wsview, wsview, 0.5))
            WP.close()

            state["war_bc"] = tok_wsp
            state["war_st"] = tok_wsp
            k.v(lambda e: e.tensor_scalar_mul(g8["qg"][:], qgb["qg"][:], 0.125),
                waits=tuple(tk_gb))
            tok_g8 = k.v(lambda e: e.tensor_scalar_mul(g8["qb"][:], qgb["qb"][:],
                                                       0.125))

            def head_ln(src, toks, gcol, bcol):
                tok = None
                for tk5 in toks:
                    tsl, lo, w = tk5[0], tk5[1], tk5[2]
                    alo = tk5[3] if len(tk5) > 3 else lo
                    off = alo - lo
                    wa = lo + w - alo
                    tv2 = k.v(lambda e, src=src, lo=lo, w=w:
                              e.tensor_mul(scr["sq"][:, :w], src[:, lo:lo + w],
                                           src[:, lo:lo + w]),
                              waits=(tsl, state["war_cb"], tok))
                    t1 = k.mm(pqs[:, :w], e2h_sb[:], src[:, lo:lo + w],
                              waits=(tsl, state["war_st"], tok))
                    t2 = k.mm(pqq[:, :w], e2h_sb[:], scr["sq"][:, :w],
                              waits=(tv2, state["war_st"]))
                    state["war_cb"] = t2
                    k.v(lambda e, w=w: e.tensor_copy(rowA[0:2, :w], pqs[:, :w]),
                        waits=(t1, state["war_rows"]))
                    tev = k.v(lambda e, w=w: e.tensor_copy(rowB[0:2, :w],
                                                           pqq[:, :w]),
                              waits=(t2,))
                    state["war_st"] = tev
                    tok_rows = rows_chunk(w, 2, tev, 64.0)
                    b1, b2 = bcast2({"bc0": bc0, "bc1": bc1}, w, e2t_sb[:],
                                    tok_rows, state["war_bc"])
                    k.v(lambda e, src=src, lo=lo, w=w:
                        e.tensor_sub(scr["t1"][:, :w], src[:, lo:lo + w],
                                     bc0[:, :w]), waits=(b1, b2))
                    k.v(lambda e, w=w:
                        e.tensor_mul(scr["t2"][:, :w], scr["t1"][:, :w],
                                     bc1[:, :w]))
                    if wa == w:
                        tok = k.v(lambda e, src=src, lo=lo, w=w:
                                  e.tensor_scalar(src[:, lo:lo + w],
                                                  scr["t2"][:, :w],
                                                  gcol, bcol, ALU.mult, ALU.add),
                                  waits=(tok_g8,))
                    else:
                        tvs = k.v(lambda e, w=w:
                                  e.tensor_scalar(scr["tb"][:, :w],
                                                  scr["t2"][:, :w],
                                                  gcol, bcol, ALU.mult, ALU.add),
                                  waits=(tok_g8,))
                        tok = k.act(src[:, alo:alo + wa],
                                    scr["tb"][:, off:off + wa], AF.Copy,
                                    waits=(tvs,))
                    state["war_bc"] = tok
                    state["war_rows"] = tok
                return tok

            war_pk = [None, None]
            wfree = [None, None]
            tok_scores_done = None
            war_pass1 = [None, None]
            war_den = [None, None]

            for hp in range(8):
                par = hp % 2
                if hp < 7:
                    tk_wq[1 - par] = k.dma(
                        wq_sb[1 - par][:].rearrange("p (n c) -> p n c", c=128),
                        D["wq_t"][hp + 1], waits=(wfree[1 - par],), q="gd")
                    tk_wkv[1 - par] = k.dma(
                        wkv_sb[1 - par][:].rearrange("p (n c) -> p n c", c=256),
                        D["wkv_t"][hp + 1], waits=(wfree[1 - par],), q="gd")

                tok_q_ch = []
                for ci, (lo, hi) in enumerate(CH_T):
                    w = hi - lo
                    bank = pk[ci % 2]
                    tmm = None
                    for pt in range(NPT):
                        tmm = k.mm(bank[:, :w],
                                   wq_sb[par][:, pt * 128:(pt + 1) * 128],
                                   xns(pt, lo, hi), start=(pt == 0),
                                   stop=(pt == NPT - 1),
                                   waits=(tk_wq[par], tok_xn, war_pk[ci % 2],
                                          tok_scores_done, tok_mask))
                    tsl = k.act(q_sb[:, lo - HW:hi - HW], bank[:, :w], AF.Silu,
                                waits=(tmm,))
                    war_pk[ci % 2] = tsl
                    tok_q_ch.append((tsl, lo - HW, w))
                tok_k_ch = []
                tok_v_ch = []
                for ci, (lo, hi, alo) in enumerate(CH_TH):
                    w = hi - lo
                    bank = pk[ci % 2]
                    tmm = None
                    for pt in range(NPT):
                        tmm = k.mm(bank[:, :w],
                                   wkv_sb[par][:, pt * 256:pt * 256 + 128],
                                   xns(pt, lo, hi), start=(pt == 0),
                                   stop=(pt == NPT - 1),
                                   waits=(tk_wkv[par], war_pk[ci % 2],
                                          tok_scores_done))
                    tsl = k.act(kk_sb[:, lo:hi], bank[:, :w], AF.Silu,
                                waits=(tmm,))
                    war_pk[ci % 2] = tsl
                    tok_k_ch.append((tsl, lo, w, alo))
                    tmm = None
                    for pt in range(NPT):
                        tmm = k.mm(bank[:, :w],
                                   wkv_sb[par][:, pt * 256 + 128:pt * 256 + 256],
                                   xns(pt, lo, hi), start=(pt == 0),
                                   stop=(pt == NPT - 1), waits=(tsl,))
                    tsl = k.act(v_sb[:, lo:hi], bank[:, :w], AF.Silu,
                                waits=(tmm,))
                    war_pk[ci % 2] = tsl
                    tok_v_ch.append((tsl, lo, w))
                wfree[par] = tok_v_ch[-1][0]
                k.v(lambda e: e.tensor_mul(v_sb[:, 0:8], v_sb[:, 0:8],
                                           mask_hb[:, 0:8]),
                    waits=(tok_v_ch[0][0], tok_mask))
                tok_vmask = k.v(lambda e: e.tensor_mul(v_sb[:, TH - 8:TH],
                                                       v_sb[:, TH - 8:TH],
                                                       mask_hb[:, 8:16]),
                                waits=(tok_v_ch[-1][0],))

                tok_qn = head_ln(q_sb, tok_q_ch, g8["qg"][:], g8["qb"][:])
                tok_kn = head_ln(kk_sb, tok_k_ch, qgb["kg"][:], qgb["kb"][:])
                k.v(lambda e: e.tensor_mul(kk_sb[:, 0:8], kk_sb[:, 0:8],
                                           mask_hb[:, 0:8]), waits=(tok_kn,))
                tok_kn = k.v(lambda e: e.tensor_mul(kk_sb[:, TH - 8:TH],
                                                    kk_sb[:, TH - 8:TH],
                                                    mask_hb[:, 8:16]))

                tok_vt = {}
                war_tv = [None, None]
                for j in range(17):
                    wj = 128 if j < 16 else 16
                    pv = ptv[j % 2]
                    trt = k.tr(pv[0:wj, 0:128], v_sb[:, j * 128:j * 128 + wj],
                               idb_sb[:], waits=(tok_vmask, war_tv[j % 2]))
                    war_tv[j % 2] = k.v(
                        lambda e, j=j, wj=wj, pv=pv:
                        e.tensor_copy(vtm_sb[0:wj, j * 128:(j + 1) * 128],
                                      pv[0:wj, 0:128]), waits=(trt,))
                    tok_vt[j] = war_tv[j % 2]
                tok_vt[17] = tok_vt[16]

                war_ss = [None] * 4
                war_ex = [None] * 4
                war_at = [None] * 4
                war_av = [None] * 4
                it = 0
                it2 = 0
                for hd in range(2):
                    hb = hd * 64
                    hglob = hp * 2 + hd
                    par = hglob % 2
                    tse = None
                    for qt in range(16):
                        p2 = it % 4
                        tmm = k.mm(pss[p2][:, :],
                                   q_sb[hb:hb + 64, qt * 128:(qt + 1) * 128],
                                   kk_sb[hb:hb + 64, qt * 128:qt * 128 + 144],
                                   waits=(tok_qn, tok_kn, tok_vmask,
                                          war_ss[p2]))
                        tss = k.act(s3_sb[p2][:], R_sb[:], AF.Tanh,
                                    scale=sh2_sb[:, qt * 16 + hglob:
                                                 qt * 16 + hglob + 1],
                                    bias=wsh_sb[:, qt * 16 + hglob:
                                                qt * 16 + hglob + 1],
                                    waits=(tok_wsp, war_ss[p2], war_ex[p2],
                                           tk_R, tk_NEG))
                        tst = k.v(lambda e, p2=p2: e.scalar_tensor_tensor(
                            m2_sb[p2][:], s3_sb[p2][:], 5e3, NEG2_sb[:],
                            ALU.mult, ALU.add), waits=(tss,))
                        tad = k.v(lambda e, p2=p2: e.tensor_add(
                            s3_sb[p2][:], pss[p2][:], m2_sb[p2][:]),
                            waits=(tmm, tst,
                                   war_pass1[par] if qt == 0 else None))
                        war_ss[p2] = tad
                        tmx = k.v(lambda e, p2=p2, qt=qt, par=par:
                                  e.reduce_max(nmx[par][:, qt:qt + 1],
                                               s3_sb[p2][:], AX.X, negate=True))
                        tse = k.act(ee_all[par][:, qt * 144:(qt + 1) * 144],
                                    s3_sb[p2][:], AF.Exp,
                                    bias=nmx[par][:, qt:qt + 1],
                                    accum=den[par][:, qt:qt + 1],
                                    waits=(tmx, war_den[par]))
                        war_ex[p2] = tse
                        it += 1
                    war_pass1[par] = tse
                    k.act(den[par][:], den[par][:], AF.Ln, waits=(tse,))
                    trc = k.act(den[par][:], den[par][:], AF.Exp, scale=-1.0)
                    taw = None
                    for qt in range(16):
                        p2 = it2 % 4
                        taw = k.v(lambda e, p2=p2, qt=qt, par=par:
                                  e.tensor_scalar(aw_sb[p2][:],
                                                  ee_all[par][:, qt * 144:
                                                              (qt + 1) * 144],
                                                  den[par][:, qt:qt + 1], 0.0,
                                                  ALU.mult, ALU.add),
                                  waits=(trc, war_at[p2]))
                        tr1 = k.tr(pta[p2][:], aw_sb[p2][:, 0:128], idb_sb[:],
                                   waits=(taw, war_at[p2]))
                        tr2 = k.tr(ptb[p2][:], aw_sb[p2][:, 128:144], idb_sb[:])
                        te1 = k.act(at1_sb[p2][:], pta[p2][:], AF.Copy,
                                    waits=(tr1,))
                        te2 = k.act(at2_sb[p2][:], ptb[p2][:], AF.Copy,
                                    waits=(tr2,))
                        k.mm(pav[p2][hb:hb + 64, :],
                             vtm_sb[:, qt * 128 + hb:qt * 128 + hb + 64],
                             at1_sb[p2][:], start=True, stop=False,
                             waits=(te1, tok_vt[qt], tok_vt[qt + 1],
                                    war_av[p2]))
                        tav = k.mm(pav[p2][hb:hb + 64, :],
                                   vtm_sb[0:16, (qt + 1) * 128 + hb:
                                          (qt + 1) * 128 + hb + 64],
                                   at2_sb[p2][:], start=False, stop=True,
                                   waits=(te2,))
                        war_at[p2] = tav
                        tev = k.act(ao_sb[hb:hb + 64, hp * T + qt * 128:
                                          hp * T + (qt + 1) * 128],
                                    pav[p2][hb:hb + 64, :], AF.Copy,
                                    waits=(tav,))
                        war_av[p2] = tev
                        tok_scores_done = tev
                        it2 += 1
                    war_den[par] = taw
                state["war_bc"] = tok_scores_done
                if hp == 7:
                    tapn("q", q_sb[:], T, (tok_qn,))
                    tapn("kk", kk_sb[:], TH, (tok_kn,))
                    tapn("v", v_sb[:], TH, (tok_vmask,))
                    tapn("wsp", wsp_tm[:], 16 * 32, (tok_wsp,))
                    tapn("ao", ao_sb[:], 8 * T, (tok_scores_done,))

            k.flush()
            P1.close()

            wout_sb = PH.enter_context(nc.sbuf_tensor("wout_sb", [128, 8 * 1024],
                                                      BF16))
            tk_wout = k.dma(wout_sb[:].rearrange("p (n c) -> p n c", c=1024),
                            D["wout_t"], waits=(tok_scores_done,), q="gd")
            tok_x_upd = None
            war_tb = None
            for cc in range(8):
                for ci, (lo, hi) in enumerate(CH_T):
                    w = hi - lo
                    bank = pk[ci % 2]
                    tmm = None
                    for pt in range(NPT):
                        tmm = k.mm(bank[:, :w],
                                   wout_sb[:, pt * 1024 + cc * 128:
                                           pt * 1024 + (cc + 1) * 128],
                                   ao_sb[:, pt * T + lo - HW:pt * T + hi - HW],
                                   start=(pt == 0), stop=(pt == NPT - 1),
                                   waits=(tk_wout, tok_scores_done,
                                          war_pk[ci % 2]))
                    tsl = k.act(scr["tb"][:, :w], bank[:, :w], AF.Silu,
                                waits=(tmm, war_tb))
                    war_pk[ci % 2] = tsl
                    tok_x_upd = k.v(lambda e, cc=cc, lo=lo, w=w:
                                    e.tensor_add(xs(cc, lo, lo + w),
                                                 xs(cc, lo, lo + w),
                                                 scr["tb"][:, :w]), waits=(tsl,))
                    war_tb = tok_x_upd
            k.flush()
        tok_x1 = tok_x_upd

        if "x1" in tap_d:
            k.dma(tap_d["x1"], x_sb[:].rearrange("p (n c) -> p n c", c=TH),
                  waits=(tok_x1,), q="y")
            k.flush()

        CH_OWN = [(HW, 520), (520, 1032), (1032, 1544), (1544, 2056)]
        with ExitStack() as PH:
            scr = {
                "t1": PH.enter_context(nc.sbuf_tensor("a2_t1", [128, 512], F32)),
                "t2": PH.enter_context(nc.sbuf_tensor("a2_t2", [128, 512], F32)),
            }
            ps = {"st0": PB[4], "st1": PB[5], "bc0": PB[2], "bc1": PB[3]}
            pk = [PB[0], PB[1]]
            gam_sb = PH.enter_context(nc.sbuf_tensor("gam_sb", [128, 8], F32))
            bet_sb = PH.enter_context(nc.sbuf_tensor("bet_sb", [128, 8], F32))
            f1b_sb = PH.enter_context(nc.sbuf_tensor("f1b_sb", [128, 32], F32))
            f2b_sb = PH.enter_context(nc.sbuf_tensor("f2b_sb", [128, 8], F32))
            tk_gam = k.dma(gam_sb[:], D["gam_c"], waits=(tok_x1,), q="y")
            tk_bet = k.dma(bet_sb[:], D["bet_c"], waits=(tok_x1,), q="y")
            tk_f1b = k.dma(f1b_sb[:].rearrange("p (h n) -> p h n", n=16),
                           D["fc1b_c"].rearrange("h p n -> p h n"),
                           waits=(tok_x1,), q="y")
            tk_f2b = k.dma(f2b_sb[:], D["fc2b_c"], waits=(tok_x1,), q="y")

            w1_sb = PH.enter_context(nc.sbuf_tensor("w1_sb", [128, 8 * 2048],
                                                    BF16))
            qraw = [PH.enter_context(nc.sbuf_tensor(f"qraw{i}", [128, 2048],
                                                    BF16)) for i in range(2)]
            bcs = [PH.enter_context(nc.sbuf_tensor(f"bcs{i}", [128, 512], F32))
                   for i in range(2)]
            A2a = PH.enter_context(ExitStack())
            xin_sb = A2a.enter_context(nc.sbuf_tensor("xin_sb", [128, 8 * T], BF16))
            for nm2 in ("cb", "cb2", "sq", "sq2"):
                scr[nm2] = A2a.enter_context(
                    nc.sbuf_tensor("a2_" + nm2, [128, 512], BF16))

            def xinf(pt, lo, w):
                return xin_sb[:, pt * T + lo - HW: pt * T + lo - HW + w]

            def xin_src(pt, lo, hi):
                return xin_sb[:, pt * T + lo - HW: pt * T + hi - HW]

            tok_xin = emit_ln(CH_OWN, ps, scr, tok_x1, add_to_x=True, xin_fn=xinf)
            if "dbg" in taps:
                scr["dbg"] = A2a.enter_context(
                    nc.sbuf_tensor("db_sb", [128, 2048], F32))
            tok_h = emit_ln(CH_OWN, ps, scr, tok_xin,
                            dst_fn=lambda pt, lo, w: xns(pt, lo, lo + w),
                            src_fn=xin_src, src_bf16=True,
                            gain=(lambda pt: gam_sb[:, pt:pt + 1],
                                  lambda pt: bet_sb[:, pt:pt + 1],
                                  (tk_gam, tk_bet)), dbg=True)
            tapn("xin", xin_sb[:], 8 * T, (tok_xin,))
            tapn("xin2", xin_sb[:], 8 * T, (tok_h,))
            tapn("hh", xn_sb[:], 8 * TH, (tok_h,))
            if "rows" in taps:
                drow = nc.dram_tensor("tap_rows", [2, 1024], F32,
                                      kind="ExternalOutput").ap()
                rv_sb = A2a.enter_context(nc.sbuf_tensor("rv_sb", [2, 1024], F32))
                k.v(lambda e: e.tensor_copy(rv_sb[:, 0:512], rowA[0:2, :]),
                    waits=(tok_h,))
                trr = k.v(lambda e: e.tensor_copy(rv_sb[:, 512:1024], rowB[0:2, :]))
                k.dma(drow, rv_sb[:], waits=(trr,), q="y")
            k.flush()
            A2a.close()

            w2_sb = PH.enter_context(nc.sbuf_tensor("w2_sb", [128, 16 * 1024], BF16))
            y_sb = PH.enter_context(nc.sbuf_tensor("y_sb", [128, 16 * 512], BF16))

            war_qraw = [tok_x1, tok_x1]
            war_w1 = tok_x1
            war_w2 = tok_h
            war_bcs = [tok_x1, tok_x1]
            tk_bcs_prev = None
            tok_x2 = None
            war_pk2 = [state["war_bc"], state["war_bc"]]
            for half in (0, 1):
                tok_w1 = None
                for kt in range(8):
                    tkr = k.dma(qraw[kt % 2][:], D["fc1q_t"][half][:, kt, :],
                                waits=(war_qraw[kt % 2],), q="gd")
                    for c4 in range(4):
                        ib = (kt * 4 + c4) % 2
                        tkb = k.dma(bcs[ib][:],
                                    D["fc1s_t"][half][kt * 4 + c4]
                                    .broadcast_to([128, 512]),
                                    waits=(war_bcs[ib], tk_bcs_prev),
                                    q="y", sig="m")
                        tk_bcs_prev = tkb
                        tok_w1 = k.v(
                            lambda e, kt=kt, c4=c4, ib=ib:
                            e.scalar_tensor_tensor(
                                w1_sb[:, kt * 2048 + c4 * 512:
                                      kt * 2048 + (c4 + 1) * 512],
                                qraw[kt % 2][:, c4 * 512:(c4 + 1) * 512],
                                -128.0, bcs[ib][:], ALU.add, ALU.mult),
                            waits=(tkr, tkb, war_w1))
                        war_bcs[ib] = tok_w1
                    war_qraw[kt % 2] = tok_w1
                tok_w2 = None
                for kt in range(16):
                    tkr = k.dma(qraw[kt % 2][:, 0:1024],
                                D["fc2q_t"][half][:, kt, :],
                                waits=(war_qraw[kt % 2],), q="gd")
                    for c4 in range(2):
                        ib = (kt * 2 + c4) % 2
                        tkb = k.dma(bcs[ib][:],
                                    D["fc2s_t"][half][kt * 2 + c4]
                                    .broadcast_to([128, 512]),
                                    waits=(war_bcs[ib], tk_bcs_prev),
                                    q="y", sig="m")
                        tk_bcs_prev = tkb
                        tok_w2 = k.v(
                            lambda e, kt=kt, c4=c4, ib=ib:
                            e.scalar_tensor_tensor(
                                w2_sb[:, kt * 1024 + c4 * 512:
                                      kt * 1024 + (c4 + 1) * 512],
                                qraw[kt % 2][:, c4 * 512:(c4 + 1) * 512],
                                -128.0, bcs[ib][:], ALU.add, ALU.mult),
                            waits=(tkr, tkb, war_w2))
                        war_bcs[ib] = tok_w2
                    war_qraw[kt % 2] = tok_w2
                war_y = None
                last_fc1 = None
                last_fc2 = None
                for ci, (lo, hi) in enumerate(CH_OWN):
                    w = hi - lo
                    tok_y = []
                    for nc2 in range(16):
                        bank = pk[nc2 % 2]
                        tmm = None
                        for kt in range(8):
                            tmm = k.mm(
                                bank[:, :w],
                                w1_sb[:, kt * 2048 + nc2 * 128:
                                      kt * 2048 + (nc2 + 1) * 128],
                                xns(kt, lo, hi),
                                start=(kt == 0), stop=(kt == 7),
                                waits=(tok_w1, tok_h, war_pk2[nc2 % 2], war_y))
                        last_fc1 = tmm
                        tsy = k.act(y_sb[:, nc2 * 512:nc2 * 512 + w], bank[:, :w],
                                    AF.Silu,
                                    bias=f1b_sb[:, half * 16 + nc2:
                                                half * 16 + nc2 + 1],
                                    waits=(tmm, tk_f1b))
                        war_pk2[nc2 % 2] = tsy
                        tok_y.append(tsy)
                    for cc in range(8):
                        bank = pk[cc % 2]
                        tmm = None
                        for kt in range(16):
                            tmm = k.mm(
                                bank[:, :w],
                                w2_sb[:, kt * 1024 + cc * 128:
                                      kt * 1024 + (cc + 1) * 128],
                                y_sb[:, kt * 512:kt * 512 + w],
                                start=(kt == 0), stop=(kt == 15),
                                waits=(tok_w2, tok_y[-1], war_pk2[cc % 2]))
                        last_fc2 = tmm
                        if half == 0:
                            tvz = k.v(lambda e, w=w, cc=cc, bank=bank:
                                      e.tensor_scalar(scr["t2"][:, :w],
                                                      bank[:, :w],
                                                      f2b_sb[:, cc:cc + 1],
                                                      0.5, ALU.add, ALU.mult),
                                      waits=(tmm, tk_f2b))
                            tok_x2 = k.v(lambda e, cc=cc, lo=lo, w=w:
                                         e.tensor_add(xs(cc, lo, lo + w),
                                                      xs(cc, lo, lo + w),
                                                      scr["t2"][:, :w]))
                            war_pk2[cc % 2] = tok_x2
                        else:
                            tok_x2 = k.v(lambda e, cc=cc, lo=lo, w=w, bank=bank:
                                         e.scalar_tensor_tensor(
                                             xs(cc, lo, lo + w), bank[:, :w],
                                             0.5, xs(cc, lo, lo + w),
                                             ALU.mult, ALU.add),
                                         waits=(tmm,))
                            war_pk2[cc % 2] = tok_x2
                    war_y = last_fc2
                war_w1 = last_fc1
                war_w2 = last_fc2
            tapn("w1d", w1_sb[:], 8 * 2048, (tok_x2,))
            tapn("w2d", w2_sb[:], 16 * 1024, (tok_x2,))
            tapn("yv", y_sb[:], 16 * 512, (tok_x2,))
            k.flush()

        if "x2" in tap_d:
            k.dma(tap_d["x2"], x_sb[:].rearrange("p (n c) -> p n c", c=TH),
                  waits=(tok_x2,), q="y")
            k.flush()

        MC = 256
        with ExitStack() as P3:
            scr3 = {
                "cb": P3.enter_context(nc.sbuf_tensor("a3_cb", [128, 512], BF16)),
                "cb2": P3.enter_context(nc.sbuf_tensor("a3_cb2", [128, 512], BF16)),
                "sq": P3.enter_context(nc.sbuf_tensor("a3_sq", [128, 512], BF16)),
                "sq2": P3.enter_context(nc.sbuf_tensor("a3_sq2", [128, 512], BF16)),
                "t1": P3.enter_context(nc.sbuf_tensor("a3_t1", [128, 512], F32)),
                "t2": P3.enter_context(nc.sbuf_tensor("a3_t2", [128, 512], F32)),
            }
            ps3 = {"st0": PB[4], "st1": PB[5], "bc0": PB[2], "bc1": PB[3]}
            pk3 = [PB[0], PB[1]]
            gts_sb = P3.enter_context(nc.sbuf_tensor("gts_sb", [128, 8 * T], BF16))
            aac_sb = P3.enter_context(nc.sbuf_tensor("aac_sb", [64, 1024], F32))
            atRb_sb = P3.enter_context(nc.sbuf_tensor("atRb_sb", [128, 1024],
                                                      BF16))

            tok_xn3 = emit_ln(CH_OWN, ps3, scr3, tok_x2,
                              dst_fn=lambda pt, lo, w: xns(pt, lo, lo + w))
            if "xn3" in taps:
                tapn("xn3", xn_sb[:], 8 * TH, (tok_xn3,))

            with ExitStack() as PG:
                wg_sb = PG.enter_context(nc.sbuf_tensor("wg_sb", [128, 8 * 1024],
                                                        BF16))
                egs = PG.enter_context(nc.sbuf_tensor("egs", [128, 512], BF16))
                tk_wg = k.dma(wg_sb[:].rearrange("p (n c) -> p n c", c=1024),
                              D["wg_t"], waits=(tok_x2,), q="gd")
                war_eg = None
                war_rowsg = state["war_rows"]
                war_bcg = state["war_bc"]
                war_stg = state["war_st"]
                tok_gts = None
                gi2 = 0
                for pt in range(NPT):
                    for ci, (lo, hi) in enumerate(CH_OWN):
                        w = hi - lo
                        bank = pk3[gi2 % 2]
                        tmm = None
                        for kt in range(NPT):
                            tmm = k.mm(bank[:, :w],
                                       wg_sb[:, kt * 1024 + pt * 128:
                                             kt * 1024 + (pt + 1) * 128],
                                       xns(kt, lo, hi), start=(kt == 0),
                                       stop=(kt == NPT - 1),
                                       waits=(tk_wg, tok_xn3, war_eg))
                        tse = k.act(egs[:, :w], bank[:, :w], AF.Exp, scale=0.125,
                                    waits=(tmm, war_eg))
                        tden = k.mm(ps3["st0"][0:2, :w], e2h_sb[:], egs[:, :w],
                                    waits=(tse, war_stg))
                        tcp = k.v(lambda e, w=w: e.tensor_copy(rowB[0:2, :w],
                                                               ps3["st0"][0:2, :w]),
                                  waits=(tden, war_rowsg))
                        war_stg = tcp
                        tln = k.act(rowB[0:2, :w], rowB[0:2, :w], AF.Ln,
                                    waits=(tcp,))
                        tln = k.act(rowB[0:2, :w], rowB[0:2, :w], AF.Exp,
                                    scale=-1.0)
                        k.v(lambda e, w=w: e.tensor_copy(rBh[0:2, :w],
                                                         rowB[0:2, :w]),
                            waits=(tln,))
                        trl = k.v(lambda e, w=w: e.tensor_sub(rBl[0:2, :w],
                                                              rowB[0:2, :w],
                                                              rBh[0:2, :w]))
                        k.mm(ps3["bc0"][:, :w], e2t_sb[:], rBh[0:2, :w],
                             start=True, stop=False, waits=(trl, war_bcg))
                        tbc = k.mm(ps3["bc0"][:, :w], e2t_sb[:], rBl[0:2, :w],
                                   start=False, stop=True)
                        tok_gts = k.v(
                            lambda e, pt=pt, lo=lo, w=w:
                            e.tensor_mul(gts_sb[:, pt * T + lo - HW:
                                                pt * T + lo - HW + w],
                                         egs[:, :w], ps3["bc0"][:, :w]),
                            waits=(tbc,))
                        war_eg = tok_gts
                        war_bcg = tok_gts
                        war_rowsg = tok_gts
                        gi2 += 1
                state["war_st"] = war_stg
                state["war_bc"] = war_bcg
                state["war_rows"] = war_rowsg
                k.flush()
            if "gts" in taps:
                tapn("gts", gts_sb[:], 8 * T, (tok_gts,))

            with ExitStack() as PK:
                wk_sb = PK.enter_context(nc.sbuf_tensor("wk_sb", [128, 8 * 1024],
                                                        BF16))
                wv_sb = PK.enter_context(nc.sbuf_tensor("wv_sb", [128, 8 * 1024],
                                                        BF16))
                cos_sb = PK.enter_context(nc.sbuf_tensor("cos_sb", [128, MC],
                                                         BF16))
                sin_sb = PK.enter_context(nc.sbuf_tensor("sin_sb", [128, MC],
                                                         BF16))
                latq_sb = PK.enter_context(nc.sbuf_tensor("latq_sb", [128, 512],
                                                          BF16))
                kr_sb = PK.enter_context(nc.sbuf_tensor("kr_sb", [128, 8 * MC],
                                                        BF16))
                kk2_sb = PK.enter_context(nc.sbuf_tensor("kk2_sb", [128, 8 * MC],
                                                         BF16))
                v2_sb = PK.enter_context(nc.sbuf_tensor("v2_sb", [128, 8 * MC],
                                                        BF16))
                vt2_sb = PK.enter_context(nc.sbuf_tensor("vt2_sb", [128, 32 * 64],
                                                         BF16))
                sm3 = [PK.enter_context(nc.sbuf_tensor(f"sm3_{i}", [128, 128], F32))
                       for i in range(2)]
                ee3 = [PK.enter_context(nc.sbuf_tensor(f"ee3_{i}", [128, 128],
                                                       BF16)) for i in range(2)]
                aw3 = [PK.enter_context(nc.sbuf_tensor(f"aw3_{i}", [128, 128],
                                                       BF16)) for i in range(2)]
                pT3 = [PK.enter_context(nc.sbuf_tensor(f"pT3_{i}", [128, 128],
                                                       BF16)) for i in range(2)]
                nm3 = [PK.enter_context(nc.sbuf_tensor(f"nm3_{i}", [128, 1], F32))
                       for i in range(2)]
                dn3 = [PK.enter_context(nc.sbuf_tensor(f"dn3_{i}", [128, 1], F32))
                       for i in range(2)]

                tk_wk = k.dma(wk_sb[:].rearrange("p (n c) -> p n c", c=1024),
                              D["wk_t"], waits=(tok_gts,), q="gd")
                tk_wv = k.dma(wv_sb[:].rearrange("p (n c) -> p n c", c=1024),
                              D["wv_t"], waits=(tok_gts,), q="gd")
                war_trig = tok_gts
                tk_la = k.dma(latq_sb[:, 0:256], D["latq_a"], waits=(tok_gts,),
                              q="gd")
                tk_lb = k.dma(latq_sb[:, 256:512], D["latq_b"], q="gd")
                tok_lq = k.v(lambda e: e.tensor_scalar_mul(latq_sb[:], latq_sb[:],
                                                           0.125),
                             waits=(tk_la, tk_lb))

                war_pk3 = [state["war_bc"], state["war_bc"]]
                war_kr = None
                war_v2 = None
                war_vt = [None, None]
                war_sc = [None, None]
                war_ex3 = [None, None]
                war_pt3 = [None, None]
                war_pat = [None, None]
                tok_aac = None
                it3 = 0
                for mc in range(8):
                    lo = HW + mc * MC
                    hi = lo + MC
                    tk_cos = k.dma(cos_sb[:], D["cos_t"][:, mc * MC:(mc + 1) * MC],
                                   waits=(war_trig,), q="gd")
                    tk_sin = k.dma(sin_sb[:], D["sin_t"][:, mc * MC:(mc + 1) * MC],
                                   q="gd")
                    tok_kr = None
                    for opt in range(8):
                        bank = pk3[opt % 2]
                        tmm = None
                        for kt in range(NPT):
                            tmm = k.mm(bank[:, :MC],
                                       wk_sb[:, kt * 1024 + opt * 128:
                                             kt * 1024 + (opt + 1) * 128],
                                       xns(kt, lo, hi), start=(kt == 0),
                                       stop=(kt == NPT - 1),
                                       waits=(tk_wk, war_pk3[opt % 2]))
                        tok_kr = k.act(kr_sb[:, opt * MC:(opt + 1) * MC],
                                       bank[:, :MC], AF.Copy,
                                       waits=(tmm, war_kr))
                        war_pk3[opt % 2] = tok_kr
                    tok_rope = None
                    for j in range(4):
                        ka = kr_sb[:, j * MC:(j + 1) * MC]
                        kb = kr_sb[:, (j + 4) * MC:(j + 5) * MC]
                        cs = cos_sb[:]
                        sn = sin_sb[:]
                        k.v(lambda e, ka=ka, cs=cs:
                            e.tensor_mul(scr3["t1"][:, :MC], ka, cs),
                            waits=(tok_kr, tk_cos, tk_sin, tok_rope))
                        k.v(lambda e, kb=kb, sn=sn:
                            e.tensor_mul(scr3["t2"][:, :MC], kb, sn))
                        k.v(lambda e, j=j:
                            e.tensor_sub(kk2_sb[:, j * MC:(j + 1) * MC],
                                         scr3["t1"][:, :MC], scr3["t2"][:, :MC]))
                        k.v(lambda e, ka=ka, sn=sn:
                            e.tensor_mul(scr3["t1"][:, :MC], ka, sn))
                        k.v(lambda e, kb=kb, cs=cs:
                            e.tensor_mul(scr3["t2"][:, :MC], kb, cs))
                        tok_rope = k.v(
                            lambda e, j=j:
                            e.tensor_add(kk2_sb[:, (j + 4) * MC:(j + 5) * MC],
                                         scr3["t1"][:, :MC], scr3["t2"][:, :MC]))
                    war_kr = tok_rope
                    war_trig = tok_rope
                    tok_v2 = None
                    for opt in range(8):
                        bank = pk3[opt % 2]
                        tmm = None
                        for kt in range(NPT):
                            tmm = k.mm(bank[:, :MC],
                                       wv_sb[:, kt * 1024 + opt * 128:
                                             kt * 1024 + (opt + 1) * 128],
                                       xns(kt, lo, hi), start=(kt == 0),
                                       stop=(kt == NPT - 1),
                                       waits=(tk_wv, war_pk3[opt % 2]))
                        tok_v2 = k.act(v2_sb[:, opt * MC:(opt + 1) * MC],
                                       bank[:, :MC], AF.Copy,
                                       waits=(tmm, war_v2))
                        war_pk3[opt % 2] = tok_v2
                    for c1 in range(2):
                        for h in range(16):
                            i2 = (c1 * 16 + h) % 2
                            hb2 = (h % 2) * 64
                            src = v2_sb[hb2:hb2 + 64,
                                        (h // 2) * MC + c1 * 128:
                                        (h // 2) * MC + (c1 + 1) * 128]
                            pv = PTB[i2][:, 256:320]
                            trt = k.tr(pv, src, idb_sb[hb2:hb2 + 64,
                                                       hb2:hb2 + 64],
                                       waits=(tok_v2, war_vt[i2]))
                            war_vt[i2] = k.v(
                                lambda e, c1=c1, h=h, pv=pv:
                                e.tensor_copy(vt2_sb[:, (c1 * 16 + h) * 64:
                                                     (c1 * 16 + h + 1) * 64],
                                              pv), waits=(trt,))
                    tok_vt2 = war_vt[0]
                    tok_vt2b = war_vt[1]
                    war_v2 = war_vt[1]
                    for hp2 in range(8):
                        for c1 in range(2):
                            p2 = it3 % 2
                            bank = pk3[p2]
                            for hd in range(2):
                                h = hp2 * 2 + hd
                                pb = (h % 4) * 32
                                tp = (pb, (hd * 64) % 128) if pb >= 96 else None
                                k.mm(bank[hd * 64:hd * 64 + 64, 0:128],
                                     latq_sb[pb:pb + 32,
                                             (h // 4) * 64:(h // 4) * 64 + 64],
                                     kk2_sb[pb:pb + 32,
                                            (h // 4) * MC + c1 * 128:
                                            (h // 4) * MC + (c1 + 1) * 128],
                                     start=True, stop=False,
                                     waits=(tok_lq, tok_rope, tok_v2,
                                            war_sc[p2]),
                                     tile_position=tp)
                                k.mm(bank[hd * 64:hd * 64 + 64, 0:128],
                                     latq_sb[pb:pb + 32,
                                             256 + (h // 4) * 64:
                                             256 + (h // 4) * 64 + 64],
                                     kk2_sb[pb:pb + 32,
                                            (4 + h // 4) * MC + c1 * 128:
                                            (4 + h // 4) * MC + (c1 + 1) * 128],
                                     start=False, stop=True,
                                     tile_position=tp)
                            tmm = k._emit("tensor", lambda e: e.nop(), (), "t", 1)
                            tvv = k.v(lambda e, p2=p2, bank=bank:
                                      e.tensor_add(sm3[p2][:], bank[:, 0:128],
                                                   C2_sb[:]),
                                      waits=(tmm, war_ex3[p2]))
                            war_sc[p2] = tvv
                            tvv = k.v(lambda e, p2=p2:
                                      e.reduce_max(nm3[p2][:], sm3[p2][:],
                                                   AX.X, negate=True))
                            tse = k.act(ee3[p2][:], sm3[p2][:], AF.Exp,
                                        bias=nm3[p2][:], accum=dn3[p2][:],
                                        waits=(tvv,))
                            war_ex3[p2] = tse
                            k.act(dn3[p2][:], dn3[p2][:], AF.Ln,
                                  waits=(tse,))
                            tvr = k.act(dn3[p2][:], dn3[p2][:], AF.Exp,
                                        scale=-1.0)
                            taw = k.act(aw3[p2][:], ee3[p2][:], AF.Copy,
                                        scale=dn3[p2][:], waits=(tvr,))
                            trt = k.tr(PTB[p2][:, 0:128], aw3[p2][:], idb_sb[:],
                                       waits=(taw, war_pt3[p2]))
                            tev = k.v(lambda e, p2=p2:
                                      e.tensor_copy(pT3[p2][:], PTB[p2][:, 0:128]),
                                      waits=(trt,))
                            war_pt3[p2] = tev
                            for hd in range(2):
                                h = hp2 * 2 + hd
                                pat = ps3["bc0"] if p2 == 0 else ps3["bc1"]
                                tpa = k.mm(pat[0:64, 0:64],
                                           pT3[p2][:, hd * 64:hd * 64 + 64],
                                           vt2_sb[:, (c1 * 16 + h) * 64:
                                                  (c1 * 16 + h + 1) * 64],
                                           start=True, stop=True,
                                           waits=(tev, tok_vt2, tok_vt2b,
                                                  war_pat[p2]))
                                if mc == 0:
                                    tac = k.v(lambda e, h=h, pat=pat:
                                              e.tensor_copy(
                                                  aac_sb[:, h * 64:(h + 1) * 64],
                                                  pat[0:64, 0:64]), waits=(tpa,))
                                else:
                                    tac = k.v(lambda e, h=h, pat=pat:
                                              e.tensor_add(
                                                  aac_sb[:, h * 64:(h + 1) * 64],
                                                  aac_sb[:, h * 64:(h + 1) * 64],
                                                  pat[0:64, 0:64]), waits=(tpa,))
                                war_pat[p2] = tac
                                tok_aac = tac
                            it3 += 1
                    war_pk3[0] = war_sc[0]
                    war_pk3[1] = war_sc[1]
                k.flush()

            tk_ar = k.dma(ar_in.ap(), aac_sb[:], waits=(tok_aac,), q="gd")
            tok_cc = k.g(lambda e: e.collective_compute(
                "AllReduce", ALU.add, replica_groups=[list(range(NCORES))],
                ins=[ar_in.ap().opt()], outs=[ar_out.ap().opt()]),
                waits=(tk_ar,), sig="cc")
            tk_r0 = k.dma(atRb_sb[0:64, :], ar_out.ap(), waits=(tok_cc,), q="gd")
            tk_r1 = k.dma(atRb_sb[64:128, :], ar_out.ap(), q="gd")

            with ExitStack() as PW:
                wp_sb = PW.enter_context(nc.sbuf_tensor("wp_sb2", [128, 8 * 1024],
                                                        BF16))
                w8_sb = PW.enter_context(nc.sbuf_tensor("w8_sb", [128, 8 * T],
                                                        BF16))
                tk_wp = k.dma(wp_sb[:].rearrange("p (n c) -> p n c", c=1024),
                              D["wp_t"], waits=(tok_aac,), q="gd")
                war_pw = [None, None]
                tok_w8 = None
                iw = 0
                for h in range(16):
                    hb = (h % 2) * 64
                    for ci, (lo, hi) in enumerate(CH_OWN):
                        w = hi - lo
                        bank = pk3[iw % 2]
                        tmm = k.mm(bank[hb:hb + 64, :w],
                                   atRb_sb[hb:hb + 64, h * 64:(h + 1) * 64],
                                   gts_sb[hb:hb + 64,
                                          (h // 2) * T + lo - HW:
                                          (h // 2) * T + hi - HW],
                                   start=True, stop=True,
                                   waits=(tk_r0, tk_r1, tok_gts,
                                          war_pw[iw % 2]))
                        tok_w8 = k.act(w8_sb[hb:hb + 64,
                                             (h // 2) * T + lo - HW:
                                             (h // 2) * T + hi - HW],
                                       bank[hb:hb + 64, :w], AF.Copy,
                                       waits=(tmm,))
                        war_pw[iw % 2] = tok_w8
                        iw += 1
                tok_x3 = None
                war_tb3 = None
                for cc in range(8):
                    for ci, (lo, hi) in enumerate(CH_OWN):
                        w = hi - lo
                        bank = pk3[iw % 2]
                        tmm = None
                        for kt in range(NPT):
                            tmm = k.mm(bank[:, :w],
                                       wp_sb[:, kt * 1024 + cc * 128:
                                             kt * 1024 + (cc + 1) * 128],
                                       w8_sb[:, kt * T + lo - HW:
                                             kt * T + hi - HW],
                                       start=(kt == 0), stop=(kt == NPT - 1),
                                       waits=(tk_wp, tok_w8, war_pw[iw % 2]))
                        tok_x3 = k.v(lambda e, cc=cc, lo=lo, w=w, bank=bank:
                                     e.tensor_add(xs(cc, lo, lo + w),
                                                  xs(cc, lo, lo + w),
                                                  bank[:, :w]),
                                     waits=(tmm, war_tb3))
                        war_pw[iw % 2] = tok_x3
                        war_tb3 = tok_x3
                        iw += 1
                k.flush()
            if "x3" in tap_d:
                k.dma(tap_d["x3"], x_sb[:].rearrange("p (n c) -> p n c", c=TH),
                      waits=(tok_x3,), q="y")
            tok_xn4 = emit_ln(CH_OWN, ps3, scr3, tok_x3,
                              dst_fn=lambda pt, lo, w: xns(pt, lo, lo + w))
            tapn("xn4", xn_sb[:], 8 * TH, (tok_xn4,))
            k.flush()
        LNS.close()

        m2_d = nc.dram_tensor("m2_d", [128, 32, 512], BF16)
        with ExitStack() as P4:
            with ExitStack() as PM:
                rmb_sb = PM.enter_context(nc.sbuf_tensor("rmb_sb",
                                                         [128, 32 * 128], BF16))
                rub_sb = PM.enter_context(nc.sbuf_tensor("rub_sb",
                                                         [128, 32 * 512], BF16))
                m2s = PM.enter_context(nc.sbuf_tensor("m2s", [128, 512], BF16))
                tk_rm = k.dma(rmb_sb[:].rearrange("p (n c) -> p n c", c=128),
                              D["rmbdT_t"], waits=(tok_xn4,), q="gd")
                tk_ru = k.dma(rub_sb[:].rearrange("p (n c) -> p n c", c=512),
                              D["rupbd_t"], waits=(tok_xn4,), q="gd")
                war_m2mm = None
                war_m2s = None
                for gi in range(32):
                    tmm = k.mm(PB[0][:, :],
                               rmb_sb[:, gi * 128:(gi + 1) * 128],
                               rub_sb[:, gi * 512:(gi + 1) * 512],
                               start=True, stop=True,
                               waits=(tk_rm, tk_ru, war_m2mm))
                    tev = k.act(m2s[:], PB[0][:, :], AF.Copy,
                                waits=(tmm, war_m2s))
                    war_m2mm = tev
                    war_m2s = k.dma(m2_d.ap()[:, gi, :], m2s[:], waits=(tev,),
                                    q="y")
                tok_m2 = war_m2s
                k.flush()

            hf_sb = P4.enter_context(nc.sbuf_tensor("hf_sb", [128, 16 * T], BF16))
            wcb_sb = P4.enter_context(nc.sbuf_tensor("wcb_sb", [128, 16 * 1024],
                                                     BF16))
            m2b = [P4.enter_context(nc.sbuf_tensor(f"m2b{i}", [128, 512],
                                                   BF16)) for i in range(4)]
            war_cp = [None, None]
            tok_x4 = None
            war_wcb = tok_m2
            war_m2b = [tok_m2] * 4
            war_z = [None, None]
            war_hf_ci = [None] * 4
            icp = 0
            tk_m2b_prev = None
            war_hfspill = None
            hfd = None
            if "hf" in taps:
                hfd = nc.dram_tensor("tap_hf", [128, 8, 16 * T], BF16,
                                     kind="ExternalOutput").ap()
            for fb in range(8):
                tk_wcb = k.dma(
                    wcb_sb[:].rearrange("p (n c) -> p n c", c=1024),
                    D["wc_t"][:, fb * 16:(fb + 1) * 16, :],
                    waits=(war_wcb,), q="gd")
                tk_m2b4 = []
                for gl in range(4):
                    gi = fb * 4 + gl
                    t = k.dma(m2b[gl][:], m2_d.ap()[:, gi, :],
                              waits=(war_m2b[gl], tk_m2b_prev),
                              q="y", sig="m")
                    tk_m2b_prev = t
                    tk_m2b4.append(t)
                last_mm = None
                for ci, (lo, hi) in enumerate(CH_OWN):
                    w = hi - lo
                    tok_hf_ci = None
                    iz = 0
                    for gl in range(4):
                        gi = fb * 4 + gl
                        for zc in range(4):
                            zb = PB[4 + iz % 2]
                            tmm = k.mm(zb[:, :w],
                                       m2b[gl][:, zc * 128:(zc + 1) * 128],
                                       xns(gi % 8, lo, hi),
                                       start=True, stop=True,
                                       waits=(tk_m2b4[gl], tok_xn4,
                                              war_z[iz % 2]))
                            ft = gl * 4 + zc
                            tok_hf_ci = k.act(
                                hf_sb[:, ft * T + lo - HW:ft * T + hi - HW],
                                zb[:, :w], AF.Gelu,
                                waits=(tmm, war_hf_ci[ci], war_hfspill))
                            war_z[iz % 2] = tok_hf_ci
                            iz += 1
                        if ci == 3:
                            war_m2b[gl] = tok_hf_ci
                    for cc in range(8):
                        bank = PB[icp % 2]
                        tmm = None
                        for fkt in range(16):
                            tmm = k.mm(bank[:, :w],
                                       wcb_sb[:, fkt * 1024 + cc * 128:
                                              fkt * 1024 + (cc + 1) * 128],
                                       hf_sb[:, fkt * T + lo - HW:
                                             fkt * T + hi - HW],
                                       start=(fkt == 0), stop=(fkt == 15),
                                       waits=(tk_wcb, tok_hf_ci,
                                              war_cp[icp % 2]))
                            last_mm = tmm
                        tok_x4 = k.v(lambda e, cc=cc, lo=lo, w=w, bank=bank:
                                     e.tensor_add(xs(cc, lo, lo + w),
                                                  xs(cc, lo, lo + w),
                                                  bank[:, :w]),
                                     waits=(tmm,))
                        war_cp[icp % 2] = tok_x4
                        icp += 1
                    war_hf_ci[ci] = tok_x4
                if hfd is not None:
                    war_hfspill = k.dma(hfd[:, fb, :], hf_sb[:],
                                        waits=(tok_hf_ci,), q="gd")
                war_wcb = last_mm
            k.flush()

        tok_out = k.dma(out_x,
                        x_sb[:].rearrange("p (n c) -> p n c", c=TH)[:, :, HW:HW + T],
                        waits=(tok_x4,), q="y")
        k._emit("sync", lambda e: e.nop(), (tok_out,), None, 1)
        k.flush()

    return nc



def _tile_pm(a, kt, cols):
    return np.ascontiguousarray(a.reshape(kt, 128, cols).transpose(1, 0, 2))


def stage_inputs(inp):
    f = np.float32
    x = np.asarray(inp["L0_attn_x"], f)[0]
    xT = np.ascontiguousarray(x.T)

    shared = {}
    wq = np.asarray(inp["L0_attn_Wq"], f)
    shared["wq_t"] = np.ascontiguousarray(
        wq.reshape(8, 128, 8, 128).transpose(2, 1, 0, 3))
    wkv = np.asarray(inp["L0_attn_Wkv"], f)
    kcols = wkv[:, :1024].reshape(1024, 8, 128)
    vcols = wkv[:, 1024:].reshape(1024, 8, 128)
    kv = np.concatenate([kcols, vcols], axis=2)
    shared["wkv_t"] = np.ascontiguousarray(
        kv.reshape(8, 128, 8, 256).transpose(2, 1, 0, 3))
    shared["wwin_t"] = _tile_pm(np.asarray(inp["L0_attn_Wwin"], f), 8, 32)
    shared["bwin_c"] = np.asarray(inp["L0_attn_bwin"], f).reshape(32, 1)
    for nm, key in (("qg", "L0_attn_q_g"), ("qb", "L0_attn_q_b"),
                    ("kg", "L0_attn_k_g"), ("kb", "L0_attn_k_b")):
        v = np.asarray(inp[key], f)
        shared[nm + "_c"] = np.ascontiguousarray(
            np.concatenate([v, v]).reshape(128, 1))
    shared["wout_t"] = _tile_pm(np.asarray(inp["L0_attn_Wout"], f), 8, 1024)
    shared["gam_c"] = np.ascontiguousarray(
        np.asarray(inp["L0_ffn_ln_gamma"], f).reshape(8, 128).T)
    shared["bet_c"] = np.ascontiguousarray(
        np.asarray(inp["L0_ffn_ln_beta"], f).reshape(8, 128).T)
    fc1q = np.asarray(inp["L0_ffn_fc1_q"], np.int32)
    shared["fc1q_t"] = np.ascontiguousarray(
        fc1q.reshape(8, 128, 2, 2048).transpose(2, 1, 0, 3))
    shared["fc1s_t"] = np.ascontiguousarray(
        np.asarray(inp["L0_ffn_fc1_scales"], f).reshape(8, 2, 2048)
        .transpose(1, 0, 2).reshape(2, 32, 1, 512))
    shared["fc1b_c"] = np.ascontiguousarray(
        np.asarray(inp["L0_ffn_fc1_bias"], f).reshape(2, 16, 128)
        .transpose(0, 2, 1))
    fc2q = np.asarray(inp["L0_ffn_fc2_q"], np.int32)
    shared["fc2q_t"] = np.ascontiguousarray(
        fc2q.reshape(2, 16, 128, 1024).transpose(0, 2, 1, 3))
    shared["fc2s_t"] = np.ascontiguousarray(
        np.asarray(inp["L0_ffn_fc2_scales"], f).reshape(2, 32, 1, 512))
    shared["fc2b_c"] = np.ascontiguousarray(
        np.asarray(inp["L0_ffn_fc2_bias"], f).reshape(8, 128).T)

    wk = np.asarray(inp["L1_attn_W_k"], f)
    perm = np.empty(1024, np.int64)
    for h in range(16):
        for i in range(32):
            perm[h * 32 + i] = h * 64 + 2 * i
            perm[512 + h * 32 + i] = h * 64 + 2 * i + 1
    shared["wk_t"] = _tile_pm(np.ascontiguousarray(wk.T[:, perm]), 8, 1024)
    shared["wv_t"] = _tile_pm(np.ascontiguousarray(
        np.asarray(inp["L1_attn_W_v"], f).T), 8, 1024)
    shared["wg_t"] = _tile_pm(np.ascontiguousarray(
        np.asarray(inp["L1_attn_W_g"], f).T), 8, 1024)
    shared["wp_t"] = _tile_pm(np.ascontiguousarray(
        np.asarray(inp["L1_attn_W_p"], f).T), 8, 1024)
    lq = np.asarray(inp["L1_attn_latent_queries"], f)[0]
    la = np.zeros((128, 256), f)
    lb = np.zeros((128, 256), f)
    for h in range(16):
        q = lq[:, h, :]
        pbase = (h % 4) * 32
        cbase = (h // 4) * 64
        la[pbase:pbase + 32, cbase:cbase + 64] = q.T[0::2, :]
        lb[pbase:pbase + 32, cbase:cbase + 64] = q.T[1::2, :]
    shared["latq_a"] = la
    shared["latq_b"] = lb

    rm = np.asarray(inp["L1_ffn_rail_maker"], f)
    rup = np.asarray(inp["L1_ffn_rail_up_proj"], f)
    rmT = np.zeros((32, 128, 128), f)
    rub = np.zeros((32, 128, 512), f)
    for h in range(4):
        for g in range(8):
            gi = h * 8 + g
            for rl in range(8):
                r = g * 8 + rl
                rmT[gi, rl * 16:(rl + 1) * 16, rl * 16:(rl + 1) * 16] = rm[h, r].T
                rub[gi, rl * 16:(rl + 1) * 16, rl * 64:(rl + 1) * 64] = rup[h, r].T
    shared["rmbdT_t"] = np.ascontiguousarray(rmT.transpose(1, 0, 2))
    shared["rupbd_t"] = np.ascontiguousarray(rub.transpose(1, 0, 2))
    wc = np.asarray(inp["L1_ffn_w_cproj"], f)
    shared["wc_t"] = np.ascontiguousarray(
        wc.T.reshape(128, 128, 1024).transpose(1, 0, 2))

    in_maps = []
    inv = 1.0 / (10000.0 ** (np.arange(0, 64, 2, dtype=np.float64) / 64.0))
    for c in range(NCORES):
        m = dict(shared)
        t0 = c * T
        xt = np.zeros((1024, TH), f)
        lo = max(0, t0 - HW); hi = min(16384, t0 + T + HW)
        xt[:, lo - (t0 - HW):hi - (t0 - HW)] = xT[:, lo:hi]
        m["x_t"] = np.ascontiguousarray(xt.reshape(8, 128, TH).transpose(1, 0, 2))
        km = np.zeros((1, TH), f)
        km[0, lo - (t0 - HW):hi - (t0 - HW)] = 1.0
        m["kv_mask"] = km
        ang = np.arange(t0, t0 + T, dtype=np.float64)[None, :] * inv[:, None]
        m["cos_t"] = np.ascontiguousarray(np.tile(np.cos(ang).astype(f), (4, 1)))
        m["sin_t"] = np.ascontiguousarray(np.tile(np.sin(ang).astype(f), (4, 1)))
        in_maps.append(m)
    return in_maps


_NC_CACHE = {}


def _get_nc(taps=()):
    key = tuple(taps)
    if key not in _NC_CACHE:
        _NC_CACHE[key] = build_program(taps=key)
    return _NC_CACHE[key]


def run_cores(inputs, taps=()):
    in_maps = stage_inputs(inputs)
    nc = _get_nc(taps)
    return run_bass_kernel_spmd(nc, in_maps, core_ids=list(range(NCORES)))


def kernel(**inputs):
    res = run_cores(inputs)
    y = np.empty((16384, 1024), np.float32)
    for i, r in enumerate(res.results):
        p = r["out_x"]
        y[T * i:T * (i + 1)] = p.transpose(1, 0, 2).reshape(1024, T).T
    return y[None]


# version 24 (final)
# speedup vs baseline: 1.0249x; 1.0234x over previous
import numpy as np
import ml_dtypes
from contextlib import ExitStack
import concourse.bass as bass
import concourse.mybir as mybir
from concourse.bass_utils import run_bass_kernel_spmd

F32 = mybir.dt.float32
BF16 = mybir.dt.bfloat16
I32 = mybir.dt.int32
AF = mybir.ActivationFunctionType
ALU = mybir.AluOpType
AX = mybir.AxisListType

NCORES = 8
T = 2048
HW = 8
TH = T + 2 * HW
NPT = 8
EPS = 1e-5

ENGINES = ("sync", "gpsimd", "scalar", "vector", "tensor")


class K:

    def __init__(self, nc, es, block):
        self.nc = nc
        self.block = block
        chans = ("t", "s", "v", "g", "y", "gd", "cc", "m")
        self.sem = {c: es.enter_context(nc.semaphore("sm_" + c)) for c in chans}
        self.cnt = dict.fromkeys(chans, 0)
        self.maxw = {e: dict.fromkeys(chans, 0) for e in ENGINES}
        self.pend = {e: [] for e in ENGINES}

    def _emit(self, eng, fn, waits, sig, inc):
        tok = None
        if sig is not None:
            self.cnt[sig] += inc
            tok = (sig, self.cnt[sig])
        self.pend[eng].append((fn, tuple(w for w in waits if w is not None), sig, inc))
        return tok

    def dma(self, out, in_, waits=(), q="gd", sig=None):
        eng = "gpsimd" if q == "gd" else "sync"
        return self._emit(eng, lambda e, o=out, i=in_: e.dma_start(out=o, in_=i),
                          waits, sig if sig is not None else q, 16)

    def mm(self, out, lhsT, rhs, start=True, stop=True, waits=(), sig="t",
           tile_position=None):
        return self._emit(
            "tensor",
            lambda e, o=out, l=lhsT, r=rhs, a=start, b=stop, tp=tile_position:
                e.matmul(o, l, r, start=a, stop=b, tile_position=tp),
            waits, sig, 1)

    def tr(self, out, in_, ident, waits=(), sig="t"):
        return self._emit(
            "tensor", lambda e, o=out, i=in_, d=ident: e.transpose(o, i, d),
            waits, sig, 1)

    def act(self, out, in_, func, bias=0.0, scale=1.0, accum=None, waits=(), sig="s"):
        return self._emit(
            "scalar",
            lambda e, o=out, i=in_, f=func, b=bias, sc=scale, a=accum:
                e.activation(o, i, f, bias=b, scale=sc, accum_out=a),
            waits, sig, 1)

    def v(self, fn, waits=(), sig="v"):
        return self._emit("vector", fn, waits, sig, 1)

    def g(self, fn, waits=(), sig="g", inc=1):
        return self._emit("gpsimd", fn, waits, sig, inc)

    def barrier_tok(self):
        return [(c, self.cnt[c]) for c in ("t", "s", "v", "g", "y", "gd")]

    def flush(self):
        for eng in ENGINES:
            ops = self.pend[eng]
            if not ops:
                continue
            self.pend[eng] = []
            sem = self.sem
            maxw = self.maxw[eng]

            def body(e, ops=ops, maxw=maxw, sem=sem):
                for fn, waits, sig, inc in ops:
                    for ch, val in waits:
                        if maxw[ch] < val:
                            e.wait_ge(sem[ch], val)
                            maxw[ch] = val
                    inst = fn(e)
                    if sig is not None:
                        inst.then_inc(sem[sig], inc)
            getattr(self.block, eng)(body)


def _bd(a):
    return a.astype(ml_dtypes.bfloat16)


def build_program(taps=()):
    nc = bass.Bass("TRN2", target_bir_lowering=False)

    D = {}

    def din(name, shape, dt=F32):
        D[name] = nc.dram_tensor(name, list(shape), dt, kind="ExternalInput").ap()
        return D[name]

    din("x_t", (128, NPT, TH))
    din("kv_mask", (1, TH))
    din("cos_t", (128, T))
    din("sin_t", (128, T))
    din("wq_t", (8, 128, 8, 128))
    din("wkv_t", (8, 128, 8, 256))
    din("wwin_t", (128, 8, 32))
    din("bwin_c", (32, 1))
    din("qg_c", (128, 1)); din("qb_c", (128, 1))
    din("kg_c", (128, 1)); din("kb_c", (128, 1))
    din("wout_t", (128, 8, 1024))
    din("gam_c", (128, 8)); din("bet_c", (128, 8))
    din("fc1q_t", (2, 128, 8, 2048), I32)
    din("fc1s_t", (2, 32, 1, 512))
    din("fc1b_c", (2, 128, 16))
    din("fc2q_t", (2, 128, 16, 1024), I32)
    din("fc2s_t", (2, 32, 1, 512))
    din("fc2b_c", (128, 8))
    din("wk_t", (128, 8, 1024))
    din("wv_t", (128, 8, 1024))
    din("wg_t", (128, 8, 1024))
    din("wp_t", (128, 8, 1024))
    din("latq_a", (128, 256)); din("latq_b", (128, 256))
    din("rmbdT_t", (128, 32, 128))
    din("rupbd_t", (128, 32, 512))
    din("wc_t", (128, 128, 1024))

    out_x = nc.dram_tensor("out_x", [128, NPT, T], F32, kind="ExternalOutput").ap()
    tap_d = {}
    for tname in taps:
        if tname in ("x1", "x2", "x3", "x4"):
            tap_d[tname] = nc.dram_tensor("tap_" + tname, [128, NPT, TH], F32,
                                          kind="ExternalOutput").ap()

    ar_in = nc.dram_tensor("ar_in", [64, 1024], F32)
    ar_out = nc.dram_tensor("ar_out", [64, 1024], F32)

    ident_f = nc.inline_tensor(np.eye(128, dtype=np.float32), name="c_identf").ap()
    ident_b = nc.inline_tensor(_bd(np.eye(128, dtype=np.float32)), name="c_identb").ap()
    ones1 = nc.inline_tensor(_bd(np.ones((128, 1), np.float32)), name="c_ones1").ap()
    onesr = nc.inline_tensor(_bd(np.ones((1, 128), np.float32)), name="c_onesr").ap()
    e2h = np.zeros((128, 2), np.float32); e2h[:64, 0] = 1; e2h[64:, 1] = 1
    e2h_c = nc.inline_tensor(_bd(e2h), name="c_e2h").ap()
    e2t = np.zeros((2, 128), np.float32); e2t[0, :64] = 1; e2t[1, 64:] = 1
    e2t_c = nc.inline_tensor(_bd(e2t), name="c_e2t").ap()
    jj, ll = np.meshgrid(np.arange(144), np.arange(128))
    R = np.abs(jj - ll - 8.0).astype(np.float32)
    R_c = nc.inline_tensor(R, name="c_R").ap()
    NEG2 = np.where((jj - ll >= 0) & (jj - ll <= 16), -5e3, -1e9).astype(np.float32)
    NEG2_c = nc.inline_tensor(NEG2, name="c_NEG2").ap()
    wsc1 = np.concatenate([np.full(16, 4.0, np.float32),
                           np.full(16, 4.75, np.float32)]).reshape(32, 1)
    wsc1_c = nc.inline_tensor(wsc1, name="c_wsc1").ap()
    wsc2 = np.concatenate([np.full(16, 4.5, np.float32),
                           np.full(16, 5.25, np.float32)]).reshape(32, 1)
    wsc2_c = nc.inline_tensor(wsc2, name="c_wsc2").ap()
    cc_, ll2 = np.meshgrid(np.arange(128), np.arange(64))
    C1 = np.where(cc_ <= ll2, 0.0, -1e9).astype(np.float32)
    C2_c = nc.inline_tensor(np.concatenate([C1, C1], 0), name="c_C2").ap()

    with ExitStack() as ES:
        block = ES.enter_context(nc.Block())
        k = K(nc, ES, block)

        def tapn(name, ap, cols, waits):
            if name in taps:
                d = nc.dram_tensor("tap_" + name, [128, cols], F32,
                                   kind="ExternalOutput").ap()
                k.dma(d, ap, waits=waits, q="gd")

        x_sb = ES.enter_context(nc.sbuf_tensor("x_sb", [128, NPT * TH], F32))
        xn_sb = ES.enter_context(nc.sbuf_tensor("xn_sb", [128, NPT * TH], BF16))
        PB = [ES.enter_context(nc.psum_tensor(f"pbank{i}", [128, 512], F32))
              for i in range(6)]
        PTB = [ES.enter_context(nc.psum_tensor(f"ptbank{i}", [128, 1024], BF16))
               for i in range(2)]
        LNS = ES.enter_context(ExitStack())
        idb_sb = LNS.enter_context(nc.sbuf_tensor("idb_sb", [128, 128], BF16))
        ones1_sb = LNS.enter_context(nc.sbuf_tensor("ones1_sb", [128, 1], BF16))
        onesr_sb = LNS.enter_context(nc.sbuf_tensor("onesr_sb", [1, 128], BF16))
        e2h_sb = LNS.enter_context(nc.sbuf_tensor("e2h_sb", [128, 2], BF16))
        e2t_sb = LNS.enter_context(nc.sbuf_tensor("e2t_sb", [2, 128], BF16))
        C2_sb = LNS.enter_context(nc.sbuf_tensor("C2_sb", [128, 128], F32))
        mask_hb = LNS.enter_context(nc.sbuf_tensor("mask_hb", [128, 16], BF16))
        rowA = LNS.enter_context(nc.sbuf_tensor("rowA", [2, 512], F32))
        rowB = LNS.enter_context(nc.sbuf_tensor("rowB", [2, 512], F32))
        rowC = LNS.enter_context(nc.sbuf_tensor("rowC", [2, 512], F32))
        rAh = LNS.enter_context(nc.sbuf_tensor("rAh", [2, 512], BF16))
        rAl = LNS.enter_context(nc.sbuf_tensor("rAl", [2, 512], BF16))
        rBh = LNS.enter_context(nc.sbuf_tensor("rBh", [2, 512], BF16))
        rBl = LNS.enter_context(nc.sbuf_tensor("rBl", [2, 512], BF16))
        eps_sb = LNS.enter_context(nc.sbuf_tensor("eps_sb", [128, 1], F32))

        tk_const = []
        for dst, src in ((idb_sb, ident_b), (ones1_sb, ones1),
                         (onesr_sb, onesr), (e2h_sb, e2h_c), (e2t_sb, e2t_c),
                         (C2_sb, C2_c)):
            tk_const.append(k.dma(dst[:], src, q="y"))
        tok_eps = k.v(lambda e: e.memset(eps_sb[:], EPS))
        tk_xc = []
        _tp = None
        for (clo, chi) in ((0, 512), (512, 1024), (1024, 1536), (1536, TH)):
            _tp = k.dma(
                x_sb[:].rearrange("p (n c) -> p n c", c=TH)[:, :, clo:chi],
                D["x_t"][:, :, clo:chi], waits=(_tp,), q="y", sig="m")
            tk_xc.append(_tp)

        def xs(pt, lo, hi):
            return x_sb[:, pt * TH + lo: pt * TH + hi]

        def xns(pt, lo, hi):
            return xn_sb[:, pt * TH + lo: pt * TH + hi]

        CH_TH = [(0, 512, 0), (512, 1024, 512), (1024, 1536, 1024),
                 (1536, 2048, 1536), (1552, TH, 2048)]
        CH_T = [(HW, 520), (520, 1032), (1032, 1544), (1544, 2056)]

        state = {"war_cb": None, "war_st": None, "war_bc": None, "war_rows": None}

        def rows_chunk(W, P, tok_in, nfeat):
            k.v(lambda e: e.tensor_scalar_mul(rowA[0:P, :W], rowA[0:P, :W],
                                              1.0 / nfeat), waits=(tok_in,))
            k.v(lambda e: e.tensor_mul(rowC[0:P, :W], rowA[0:P, :W],
                                       rowA[0:P, :W]))
            tvv = k.v(lambda e: e.scalar_tensor_tensor(
                rowB[0:P, :W], rowB[0:P, :W], 1.0 / nfeat, rowC[0:P, :W],
                ALU.mult, ALU.subtract))
            tss = k.act(rowB[0:P, :W], rowB[0:P, :W], AF.Ln,
                        bias=eps_sb[0:P, :], waits=(tvv, tok_eps))
            tss = k.act(rowB[0:P, :W], rowB[0:P, :W], AF.Exp, scale=-0.5)
            k.v(lambda e: e.tensor_copy(rAh[0:P, :W], rowA[0:P, :W]))
            k.v(lambda e: e.tensor_sub(rAl[0:P, :W], rowA[0:P, :W], rAh[0:P, :W]))
            k.v(lambda e: e.tensor_copy(rBh[0:P, :W], rowB[0:P, :W]),
                waits=(tss,))
            return k.v(lambda e: e.tensor_sub(rBl[0:P, :W], rowB[0:P, :W],
                                              rBh[0:P, :W]))

        def bcast2(ps, W, lhs, tok_rows, war):
            k.mm(ps["bc0"][:, :W], lhs, rAh[0:lhs.shape[0], :W],
                 start=True, stop=False, waits=(tok_rows, war))
            b1 = k.mm(ps["bc0"][:, :W], lhs, rAl[0:lhs.shape[0], :W],
                      start=False, stop=True)
            k.mm(ps["bc1"][:, :W], lhs, rBh[0:lhs.shape[0], :W],
                 start=True, stop=False)
            b2 = k.mm(ps["bc1"][:, :W], lhs, rBl[0:lhs.shape[0], :W],
                      start=False, stop=True)
            return (b1, b2)

        def emit_ln(rng, ps, scr, tok_ready, dst_fn=None, add_to_x=False,
                    xin_fn=None, src_fn=None, src_bf16=False, gain=None,
                    dbg=False):
            if src_fn is None:
                src_fn = xs
            out_tok = None
            war_cb = [state.get("war_cb"), state.get("war_cb")]
            war_sq = [state.get("war_cb"), state.get("war_cb")]
            for nch, chunk in enumerate(rng):
                tok_rdy = (tok_ready[min(nch, len(tok_ready) - 1)]
                           if isinstance(tok_ready, list) else tok_ready)
                lo, hi = chunk[0], chunk[1]
                alo = chunk[2] if len(chunk) > 2 else lo
                w = hi - lo
                off = alo - lo
                wa = hi - alo
                tm1 = tm2 = None
                for pt in range(NPT):
                    pr = pt % 2
                    cb = scr["cb"] if pr == 0 else scr["cb2"]
                    sq = scr["sq"] if pr == 0 else scr["sq2"]
                    if src_bf16:
                        cbv = src_fn(pt, lo, lo + w)
                        tv1 = tok_rdy
                    else:
                        tv1 = k.v(lambda e, pt=pt, lo=lo, w=w, cb=cb:
                                  e.tensor_copy(cb[:, :w], src_fn(pt, lo, lo + w)),
                                  waits=(tok_rdy, war_cb[pr]))
                        cbv = cb[:, :w]
                    tv2 = k.v(lambda e, w=w, cbv=cbv, sq=sq:
                              e.tensor_mul(sq[:, :w], cbv, cbv),
                              waits=(war_sq[pr], tv1))
                    tm1 = k.mm(ps["st0"][0:1, :w], ones1_sb[:], cbv,
                               start=(pt == 0), stop=(pt == NPT - 1),
                               waits=(tv1, state["war_st"]))
                    tm2 = k.mm(ps["st1"][0:1, :w], ones1_sb[:], sq[:, :w],
                               start=(pt == 0), stop=(pt == NPT - 1),
                               waits=(tv2, state["war_st"]))
                    war_cb[pr] = tm1
                    war_sq[pr] = tm2
                state["war_cb"] = tm2
                k.act(rowA[0:1, :w], ps["st0"][0:1, :w], AF.Copy,
                      waits=(tm1, state["war_rows"]))
                tev = k.act(rowB[0:1, :w], ps["st1"][0:1, :w], AF.Copy,
                            waits=(tm2,))
                state["war_st"] = tev
                tok_rows = rows_chunk(w, 1, tev, 1024.0)
                b1, b2 = bcast2(ps, w, onesr_sb[:], tok_rows, state["war_bc"])
                for pt in range(NPT):
                    k.v(lambda e, pt=pt, lo=lo, w=w:
                        e.tensor_sub(scr["t1"][:, :w], src_fn(pt, lo, lo + w),
                                     ps["bc0"][:, :w]),
                        waits=(b1, b2))
                    if add_to_x or xin_fn is not None:
                        k.v(lambda e, w=w:
                            e.tensor_mul(scr["t2"][:, :w], scr["t1"][:, :w],
                                         ps["bc1"][:, :w]))
                        tmv = None
                        if add_to_x:
                            tmv = k.v(lambda e, pt=pt, alo=alo, wa=wa:
                                      e.tensor_add(xs(pt, alo, alo + wa),
                                                   xs(pt, alo, alo + wa),
                                                   scr["t2"][:, :wa]))
                        if xin_fn is not None:
                            tmv = k.v(lambda e, pt=pt, alo=alo, wa=wa:
                                      e.tensor_copy(xin_fn(pt, alo, wa),
                                                    scr["t2"][:, :wa]))
                        out_tok = tmv
                    elif gain is not None:
                        k.v(lambda e, w=w:
                            e.tensor_mul(scr["t2"][:, :w], scr["t1"][:, :w],
                                         ps["bc1"][:, :w]))
                        out_tok = k.v(lambda e, pt=pt, alo=alo, wa=wa:
                                      e.tensor_scalar(dst_fn(pt, alo, wa),
                                                      scr["t2"][:, :wa],
                                                      gain[0](pt), gain[1](pt),
                                                      ALU.mult, ALU.add),
                                      waits=gain[2] if len(gain) > 2 else ())
                        if dbg and pt == 0 and lo == rng[0][0] and "dbg" in taps:
                            ddb = nc.dram_tensor("tap_dbg", [128, 2048], F32,
                                                 kind="ExternalOutput").ap()
                            db_sb = scr["dbg"]
                            k.v(lambda e: e.tensor_copy(db_sb[:, 0:512],
                                                        ps["bc0"][:, :]),
                                waits=(out_tok,))
                            k.v(lambda e: e.tensor_copy(db_sb[:, 512:1024],
                                                        ps["bc1"][:, :]))
                            k.v(lambda e: e.tensor_copy(db_sb[:, 1024:1536],
                                                        scr["t1"][:, :]))
                            tdb = k.v(lambda e: e.tensor_copy(db_sb[:, 1536:2048],
                                                              scr["t2"][:, :]))
                            k.dma(ddb, db_sb[:], waits=(tdb,), q="y")
                    elif wa == w:
                        out_tok = k.v(lambda e, pt=pt, alo=alo, wa=wa:
                                      e.tensor_mul(dst_fn(pt, alo, wa),
                                                   scr["t1"][:, :wa],
                                                   ps["bc1"][:, :wa]))
                    else:
                        tvm = k.v(lambda e, w=w:
                                  e.tensor_mul(scr["t2"][:, :w], scr["t1"][:, :w],
                                               ps["bc1"][:, :w]))
                        out_tok = k.act(dst_fn(pt, alo, wa),
                                        scr["t2"][:, off:off + wa], AF.Copy,
                                        waits=(tvm,))
                state["war_bc"] = out_tok
                state["war_rows"] = out_tok
            return out_tok

        with ExitStack() as PH:
            ps = {"st0": PB[4], "st1": PB[5], "bc0": PB[2], "bc1": PB[3]}
            pmh = PB[0][:, 0:16]
            scr = {
                "cb": PH.enter_context(nc.sbuf_tensor("s_cb", [128, 512], BF16)),
                "cb2": PH.enter_context(nc.sbuf_tensor("s_cb2", [128, 512], BF16)),
                "sq": PH.enter_context(nc.sbuf_tensor("s_sq", [128, 512], BF16)),
                "sq2": PH.enter_context(nc.sbuf_tensor("s_sq2", [128, 512], BF16)),
                "t1": PH.enter_context(nc.sbuf_tensor("s_t1", [128, 512], F32)),
                "t2": PH.enter_context(nc.sbuf_tensor("s_t2", [128, 512], F32)),
            }
            msk_f = PH.enter_context(nc.sbuf_tensor("msk_f", [1, TH], F32))
            msk_b = PH.enter_context(nc.sbuf_tensor("msk_b", [1, TH], BF16))
            tk_msk = k.dma(msk_f[:], D["kv_mask"], q="y")
            tok_xn = emit_ln(CH_TH, ps, scr, tk_xc,
                             dst_fn=lambda pt, lo, w: xns(pt, lo, lo + w))
            tmv = k.v(lambda e: e.tensor_copy(msk_b[:], msk_f[:]), waits=(tk_msk,))
            k.mm(pmh[:, 0:8], onesr_sb[:], msk_b[:, 0:8],
                 waits=(tmv, tk_const[3]))
            t2m = k.mm(pmh[:, 8:16], onesr_sb[:], msk_b[:, TH - 8:TH])
            tok_mask = k.v(lambda e: e.tensor_copy(mask_hb[:], pmh[:]),
                           waits=(t2m,))
            tapn("xn", xn_sb[:], NPT * TH, (tok_xn,))
            k.flush()

        with ExitStack() as PH:
            ao_sb = PH.enter_context(nc.sbuf_tensor("ao_sb", [128, 8 * T], BF16))
            qgb = {nm: PH.enter_context(nc.sbuf_tensor(nm + "_sb", [128, 1], F32))
                   for nm in ("qg", "qb", "kg", "kb")}
            g8 = {nm: PH.enter_context(nc.sbuf_tensor(nm + "8", [128, 1], F32))
                  for nm in ("qg", "qb")}
            scr = {
                "sq": PH.enter_context(nc.sbuf_tensor("a1_sq", [128, 512], BF16)),
                "t1": PH.enter_context(nc.sbuf_tensor("a1_t1", [128, 512], F32)),
                "t2": PH.enter_context(nc.sbuf_tensor("a1_t2", [128, 512], F32)),
                "tb": PH.enter_context(nc.sbuf_tensor("a1_tb", [128, 512], BF16)),
            }
            pk = [PB[0], PB[1]]
            pqs = PB[4][0:2, :]
            pqq = PB[5][0:2, :]
            bc0, bc1 = PB[2], PB[3]
            pss = [PB[0][:, 0:144], PB[1][:, 0:144],
                   PB[0][:, 144:288], PB[1][:, 144:288]]
            pta = [PTB[0][:, 0:128], PTB[1][:, 0:128],
                   PTB[0][:, 512:640], PTB[1][:, 512:640]]
            ptb = [PTB[0][0:16, 128:256], PTB[1][0:16, 128:256],
                   PTB[0][0:16, 640:768], PTB[1][0:16, 640:768]]
            pav = [PB[2][:, 0:128], PB[3][:, 0:128],
                   PB[2][:, 128:256], PB[3][:, 128:256]]
            ptv = [PTB[0][:, 256:384], PTB[1][:, 256:384]]
            pwp = PB[2][0:32, :]
            pwt = PB[3][:, 0:32]

            P1 = PH.enter_context(ExitStack())
            wq_sb = [P1.enter_context(nc.sbuf_tensor(f"wq{i}", [128, 8 * 128], BF16))
                     for i in range(2)]
            wkv_sb = [P1.enter_context(nc.sbuf_tensor(f"wkv{i}", [128, 8 * 256],
                                                      BF16)) for i in range(2)]
            wwin_sb = P1.enter_context(nc.sbuf_tensor("wwin_sb", [128, 8 * 32],
                                                      BF16))
            bwin_sb = P1.enter_context(nc.sbuf_tensor("bwin_sb", [32, 1], F32))
            wsc1_sb = P1.enter_context(nc.sbuf_tensor("wsc1_sb", [32, 1], F32))
            wsc2_sb = P1.enter_context(nc.sbuf_tensor("wsc2_sb", [32, 1], F32))
            idf_sb = P1.enter_context(nc.sbuf_tensor("idf_sb", [128, 128], F32))
            R_sb = P1.enter_context(nc.sbuf_tensor("R_sb", [128, 144], F32))
            NEG2_sb = P1.enter_context(nc.sbuf_tensor("NEG2_sb", [128, 144],
                                                      F32))
            q_sb = P1.enter_context(nc.sbuf_tensor("q_sb", [128, T], BF16))
            kk_sb = P1.enter_context(nc.sbuf_tensor("kk_sb", [128, TH], BF16))
            v_sb = P1.enter_context(nc.sbuf_tensor("v_sb", [128, TH], BF16))
            vtm_sb = P1.enter_context(nc.sbuf_tensor("vtm_sb", [128, 17 * 128],
                                                     BF16))
            sh2_sb = P1.enter_context(nc.sbuf_tensor("sh2_sb", [128, 16 * 16], F32))
            wsh_sb = P1.enter_context(nc.sbuf_tensor("wsh_sb", [128, 16 * 16], F32))
            s3_sb = [P1.enter_context(nc.sbuf_tensor(f"s3_{i}", [128, 144], F32))
                     for i in range(4)]
            m2_sb = [P1.enter_context(nc.sbuf_tensor(f"mm2_{i}", [128, 144], F32))
                     for i in range(4)]
            ee_all = [P1.enter_context(nc.sbuf_tensor(f"eea_{i}",
                                                      [128, 16 * 144], BF16))
                      for i in range(2)]
            aw_sb = [P1.enter_context(nc.sbuf_tensor(f"aw_{i}", [128, 144], BF16))
                     for i in range(4)]
            at1_sb = [P1.enter_context(nc.sbuf_tensor(f"at1_{i}", [128, 128], BF16))
                      for i in range(4)]
            at2_sb = [P1.enter_context(nc.sbuf_tensor(f"at2_{i}", [16, 128], BF16))
                      for i in range(4)]
            nmx = [P1.enter_context(nc.sbuf_tensor(f"nmx_{i}", [128, 16], F32))
                   for i in range(2)]
            den = [P1.enter_context(nc.sbuf_tensor(f"den_{i}", [128, 16], F32))
                   for i in range(2)]

            _a0done = (tok_xn, tok_mask)
            tk_wwin = k.dma(wwin_sb[:].rearrange("p (n c) -> p n c", c=32),
                            D["wwin_t"], waits=_a0done, q="gd")
            tk_bwin = k.dma(bwin_sb[:], D["bwin_c"], waits=_a0done, q="y")
            tk_wsc = k.dma(wsc1_sb[:], wsc1_c, waits=_a0done, q="y")
            tk_wsc = k.dma(wsc2_sb[:], wsc2_c, waits=_a0done, q="y")
            tk_idf = k.dma(idf_sb[:], ident_f, waits=_a0done, q="y")
            tk_R = k.dma(R_sb[:], R_c, waits=_a0done, q="y")
            tk_NEG = k.dma(NEG2_sb[:], NEG2_c, waits=_a0done, q="y")
            tk_gb = []
            for nm in ("qg", "qb", "kg", "kb"):
                tk_gb.append(k.dma(qgb[nm][:], D[nm + "_c"], waits=_a0done, q="y"))
            tk_wq = [None, None]
            tk_wkv = [None, None]
            tk_wq[0] = k.dma(wq_sb[0][:].rearrange("p (n c) -> p n c", c=128),
                             D["wq_t"][0], waits=_a0done, q="gd")
            tk_wkv[0] = k.dma(wkv_sb[0][:].rearrange("p (n c) -> p n c", c=256),
                              D["wkv_t"][0], waits=_a0done, q="gd")

            WP = ExitStack()
            wp_sb = WP.enter_context(nc.sbuf_tensor("wp_sb", [32, T], F32))
            wsp_tm = WP.enter_context(nc.sbuf_tensor("wsp_tm", [128, 16 * 32],
                                                     F32))
            tok_wp = None
            for ci, (lo, hi) in enumerate(CH_T):
                w = hi - lo
                tmm = None
                for pt in range(NPT):
                    tmm = k.mm(pwp[:, :w], wwin_sb[:, pt * 32:(pt + 1) * 32],
                               xns(pt, lo, hi), start=(pt == 0),
                               stop=(pt == NPT - 1),
                               waits=(tk_wwin, tok_xn, tok_wp))
                tok_wp = k.act(wp_sb[:, lo - HW:hi - HW], pwp[:, :w], AF.Silu,
                               bias=bwin_sb[:], waits=(tmm, tk_bwin))
            tt_ = k.act(wp_sb[:], wp_sb[:], AF.Tanh, scale=0.5, waits=(tok_wp,))
            tv_s = k.v(lambda e: e.tensor_scalar(wp_sb[:], wp_sb[:],
                                                 wsc1_sb[:], wsc2_sb[:],
                                                 ALU.mult, ALU.add),
                       waits=(tt_, tk_wsc))
            tok_wt = None
            for tt2 in range(16):
                trt = k.tr(pwt[:], wp_sb[:, tt2 * 128:(tt2 + 1) * 128],
                           idf_sb[0:32, 0:32], waits=(tv_s, tok_wt, tk_idf))
                tok_wt = k.v(lambda e, tt2=tt2:
                             e.tensor_copy(wsp_tm[:, tt2 * 32:(tt2 + 1) * 32],
                                           pwt[:]), waits=(trt,))
            w3 = wsp_tm[:].rearrange("p (t c) -> p t c", c=32)
            s3view = sh2_sb[:].rearrange("p (t c) -> p t c", c=16)
            wsview = wsh_sb[:].rearrange("p (t c) -> p t c", c=16)
            k.v(lambda e: e.tensor_scalar_mul(s3view, w3[:, :, 16:32], -0.5),
                waits=(tok_wt,))
            k.v(lambda e: e.tensor_mul(wsview, w3[:, :, 0:16], w3[:, :, 16:32]))
            tok_wsp = k.v(lambda e: e.tensor_scalar_mul(wsview, wsview, 0.5))
            WP.close()

            state["war_bc"] = tok_wsp
            state["war_st"] = tok_wsp
            k.v(lambda e: e.tensor_scalar_mul(g8["qg"][:], qgb["qg"][:], 0.125),
                waits=tuple(tk_gb))
            tok_g8 = k.v(lambda e: e.tensor_scalar_mul(g8["qb"][:], qgb["qb"][:],
                                                       0.125))

            def head_ln(src, toks, gcol, bcol):
                tok = None
                for tk5 in toks:
                    tsl, lo, w = tk5[0], tk5[1], tk5[2]
                    alo = tk5[3] if len(tk5) > 3 else lo
                    off = alo - lo
                    wa = lo + w - alo
                    tv2 = k.v(lambda e, src=src, lo=lo, w=w:
                              e.tensor_mul(scr["sq"][:, :w], src[:, lo:lo + w],
                                           src[:, lo:lo + w]),
                              waits=(tsl, state["war_cb"], tok))
                    t1 = k.mm(pqs[:, :w], e2h_sb[:], src[:, lo:lo + w],
                              waits=(tsl, state["war_st"], tok))
                    t2 = k.mm(pqq[:, :w], e2h_sb[:], scr["sq"][:, :w],
                              waits=(tv2, state["war_st"]))
                    state["war_cb"] = t2
                    k.act(rowA[0:2, :w], pqs[:, :w], AF.Copy,
                          waits=(t1, state["war_rows"]))
                    tev = k.act(rowB[0:2, :w], pqq[:, :w], AF.Copy,
                                waits=(t2,))
                    state["war_st"] = tev
                    tok_rows = rows_chunk(w, 2, tev, 64.0)
                    b1, b2 = bcast2({"bc0": bc0, "bc1": bc1}, w, e2t_sb[:],
                                    tok_rows, state["war_bc"])
                    k.v(lambda e, src=src, lo=lo, w=w:
                        e.tensor_sub(scr["t1"][:, :w], src[:, lo:lo + w],
                                     bc0[:, :w]), waits=(b1, b2))
                    k.v(lambda e, w=w:
                        e.tensor_mul(scr["t2"][:, :w], scr["t1"][:, :w],
                                     bc1[:, :w]))
                    if wa == w:
                        tok = k.v(lambda e, src=src, lo=lo, w=w:
                                  e.tensor_scalar(src[:, lo:lo + w],
                                                  scr["t2"][:, :w],
                                                  gcol, bcol, ALU.mult, ALU.add),
                                  waits=(tok_g8,))
                    else:
                        tvs = k.v(lambda e, w=w:
                                  e.tensor_scalar(scr["tb"][:, :w],
                                                  scr["t2"][:, :w],
                                                  gcol, bcol, ALU.mult, ALU.add),
                                  waits=(tok_g8,))
                        tok = k.act(src[:, alo:alo + wa],
                                    scr["tb"][:, off:off + wa], AF.Copy,
                                    waits=(tvs,))
                    state["war_bc"] = tok
                    state["war_rows"] = tok
                return tok

            war_pk = [None, None]
            wfree = [None, None]
            tok_scores_done = None
            war_pass1 = [None, None]
            war_den = [None, None]

            for hp in range(8):
                par = hp % 2
                if hp < 7:
                    tk_wq[1 - par] = k.dma(
                        wq_sb[1 - par][:].rearrange("p (n c) -> p n c", c=128),
                        D["wq_t"][hp + 1], waits=(wfree[1 - par],), q="gd")
                    tk_wkv[1 - par] = k.dma(
                        wkv_sb[1 - par][:].rearrange("p (n c) -> p n c", c=256),
                        D["wkv_t"][hp + 1], waits=(wfree[1 - par],), q="gd")

                tok_q_ch = []
                for ci, (lo, hi) in enumerate(CH_T):
                    w = hi - lo
                    bank = pk[ci % 2]
                    tmm = None
                    for pt in range(NPT):
                        tmm = k.mm(bank[:, :w],
                                   wq_sb[par][:, pt * 128:(pt + 1) * 128],
                                   xns(pt, lo, hi), start=(pt == 0),
                                   stop=(pt == NPT - 1),
                                   waits=(tk_wq[par], tok_xn, war_pk[ci % 2],
                                          tok_scores_done, tok_mask))
                    tsl = k.act(q_sb[:, lo - HW:hi - HW], bank[:, :w], AF.Silu,
                                waits=(tmm,))
                    war_pk[ci % 2] = tsl
                    tok_q_ch.append((tsl, lo - HW, w))
                tok_k_ch = []
                tok_v_ch = []
                for ci, (lo, hi, alo) in enumerate(CH_TH):
                    w = hi - lo
                    bank = pk[ci % 2]
                    tmm = None
                    for pt in range(NPT):
                        tmm = k.mm(bank[:, :w],
                                   wkv_sb[par][:, pt * 256:pt * 256 + 128],
                                   xns(pt, lo, hi), start=(pt == 0),
                                   stop=(pt == NPT - 1),
                                   waits=(tk_wkv[par], war_pk[ci % 2],
                                          tok_scores_done))
                    tsl = k.act(kk_sb[:, lo:hi], bank[:, :w], AF.Silu,
                                waits=(tmm,))
                    war_pk[ci % 2] = tsl
                    tok_k_ch.append((tsl, lo, w, alo))
                    tmm = None
                    for pt in range(NPT):
                        tmm = k.mm(bank[:, :w],
                                   wkv_sb[par][:, pt * 256 + 128:pt * 256 + 256],
                                   xns(pt, lo, hi), start=(pt == 0),
                                   stop=(pt == NPT - 1), waits=(tsl,))
                    tsl = k.act(v_sb[:, lo:hi], bank[:, :w], AF.Silu,
                                waits=(tmm,))
                    war_pk[ci % 2] = tsl
                    tok_v_ch.append((tsl, lo, w))
                wfree[par] = tok_v_ch[-1][0]
                k.v(lambda e: e.tensor_mul(v_sb[:, 0:8], v_sb[:, 0:8],
                                           mask_hb[:, 0:8]),
                    waits=(tok_v_ch[0][0], tok_mask))
                tok_vmask = k.v(lambda e: e.tensor_mul(v_sb[:, TH - 8:TH],
                                                       v_sb[:, TH - 8:TH],
                                                       mask_hb[:, 8:16]),
                                waits=(tok_v_ch[-1][0],))

                tok_qn = head_ln(q_sb, tok_q_ch, g8["qg"][:], g8["qb"][:])
                tok_kn = head_ln(kk_sb, tok_k_ch, qgb["kg"][:], qgb["kb"][:])
                k.v(lambda e: e.tensor_mul(kk_sb[:, 0:8], kk_sb[:, 0:8],
                                           mask_hb[:, 0:8]), waits=(tok_kn,))
                tok_kn = k.v(lambda e: e.tensor_mul(kk_sb[:, TH - 8:TH],
                                                    kk_sb[:, TH - 8:TH],
                                                    mask_hb[:, 8:16]))

                tok_vt = {}
                war_tv = [None, None]
                for j in range(17):
                    wj = 128 if j < 16 else 16
                    pv = ptv[j % 2]
                    trt = k.tr(pv[0:wj, 0:128], v_sb[:, j * 128:j * 128 + wj],
                               idb_sb[:], waits=(tok_vmask, war_tv[j % 2]))
                    war_tv[j % 2] = k.act(
                        vtm_sb[0:wj, j * 128:(j + 1) * 128],
                        pv[0:wj, 0:128], AF.Copy, waits=(trt,))
                    tok_vt[j] = war_tv[j % 2]
                tok_vt[17] = tok_vt[16]

                war_ss = [None] * 4
                war_ex = [None] * 4
                war_at = [None] * 4
                war_av = [None] * 4
                it = 0
                it2 = 0
                for hd in range(2):
                    hb = hd * 64
                    hglob = hp * 2 + hd
                    par = hglob % 2
                    tse = None
                    for qt in range(16):
                        p2 = it % 4
                        tmm = k.mm(pss[p2][:, :],
                                   q_sb[hb:hb + 64, qt * 128:(qt + 1) * 128],
                                   kk_sb[hb:hb + 64, qt * 128:qt * 128 + 144],
                                   waits=(tok_qn, tok_kn, tok_vmask,
                                          war_ss[p2]))
                        tss = k.act(s3_sb[p2][:], R_sb[:], AF.Tanh,
                                    scale=sh2_sb[:, qt * 16 + hglob:
                                                 qt * 16 + hglob + 1],
                                    bias=wsh_sb[:, qt * 16 + hglob:
                                                qt * 16 + hglob + 1],
                                    waits=(tok_wsp, war_ss[p2], war_ex[p2],
                                           tk_R, tk_NEG))
                        tst = k.v(lambda e, p2=p2: e.scalar_tensor_tensor(
                            m2_sb[p2][:], s3_sb[p2][:], 5e3, NEG2_sb[:],
                            ALU.mult, ALU.add), waits=(tss,))
                        tad = k.v(lambda e, p2=p2: e.tensor_add(
                            s3_sb[p2][:], pss[p2][:], m2_sb[p2][:]),
                            waits=(tmm, tst,
                                   war_pass1[par] if qt == 0 else None))
                        war_ss[p2] = tad
                        tmx = k.v(lambda e, p2=p2, qt=qt, par=par:
                                  e.reduce_max(nmx[par][:, qt:qt + 1],
                                               s3_sb[p2][:], AX.X, negate=True))
                        tse = k.act(ee_all[par][:, qt * 144:(qt + 1) * 144],
                                    s3_sb[p2][:], AF.Exp,
                                    bias=nmx[par][:, qt:qt + 1],
                                    accum=den[par][:, qt:qt + 1],
                                    waits=(tmx, war_den[par]))
                        war_ex[p2] = tse
                        it += 1
                    war_pass1[par] = tse
                    k.act(den[par][:], den[par][:], AF.Ln, waits=(tse,))
                    trc = k.act(den[par][:], den[par][:], AF.Exp, scale=-1.0)
                    taw = None
                    for qt in range(16):
                        p2 = it2 % 4
                        taw = k.v(lambda e, p2=p2, qt=qt, par=par:
                                  e.tensor_scalar(aw_sb[p2][:],
                                                  ee_all[par][:, qt * 144:
                                                              (qt + 1) * 144],
                                                  den[par][:, qt:qt + 1], 0.0,
                                                  ALU.mult, ALU.add),
                                  waits=(trc, war_at[p2]))
                        tr1 = k.tr(pta[p2][:], aw_sb[p2][:, 0:128], idb_sb[:],
                                   waits=(taw, war_at[p2]))
                        tr2 = k.tr(ptb[p2][:], aw_sb[p2][:, 128:144], idb_sb[:])
                        te1 = k.act(at1_sb[p2][:], pta[p2][:], AF.Copy,
                                    waits=(tr1,))
                        te2 = k.act(at2_sb[p2][:], ptb[p2][:], AF.Copy,
                                    waits=(tr2,))
                        k.mm(pav[p2][hb:hb + 64, :],
                             vtm_sb[:, qt * 128 + hb:qt * 128 + hb + 64],
                             at1_sb[p2][:], start=True, stop=False,
                             waits=(te1, tok_vt[qt], tok_vt[qt + 1],
                                    war_av[p2]))
                        tav = k.mm(pav[p2][hb:hb + 64, :],
                                   vtm_sb[0:16, (qt + 1) * 128 + hb:
                                          (qt + 1) * 128 + hb + 64],
                                   at2_sb[p2][:], start=False, stop=True,
                                   waits=(te2,))
                        war_at[p2] = tav
                        tev = k.act(ao_sb[hb:hb + 64, hp * T + qt * 128:
                                          hp * T + (qt + 1) * 128],
                                    pav[p2][hb:hb + 64, :], AF.Copy,
                                    waits=(tav,))
                        war_av[p2] = tev
                        tok_scores_done = tev
                        it2 += 1
                    war_den[par] = taw
                state["war_bc"] = tok_scores_done
                if hp == 7:
                    tapn("q", q_sb[:], T, (tok_qn,))
                    tapn("kk", kk_sb[:], TH, (tok_kn,))
                    tapn("v", v_sb[:], TH, (tok_vmask,))
                    tapn("wsp", wsp_tm[:], 16 * 32, (tok_wsp,))
                    tapn("ao", ao_sb[:], 8 * T, (tok_scores_done,))

            k.flush()
            P1.close()

            wout_sb = PH.enter_context(nc.sbuf_tensor("wout_sb", [128, 8 * 1024],
                                                      BF16))
            tk_wout = k.dma(wout_sb[:].rearrange("p (n c) -> p n c", c=1024),
                            D["wout_t"], waits=(tok_scores_done,), q="gd")
            tok_x_upd = None
            war_tb = None
            for cc in range(8):
                for ci, (lo, hi) in enumerate(CH_T):
                    w = hi - lo
                    bank = pk[ci % 2]
                    tmm = None
                    for pt in range(NPT):
                        tmm = k.mm(bank[:, :w],
                                   wout_sb[:, pt * 1024 + cc * 128:
                                           pt * 1024 + (cc + 1) * 128],
                                   ao_sb[:, pt * T + lo - HW:pt * T + hi - HW],
                                   start=(pt == 0), stop=(pt == NPT - 1),
                                   waits=(tk_wout, tok_scores_done,
                                          war_pk[ci % 2]))
                    tsl = k.act(scr["tb"][:, :w], bank[:, :w], AF.Silu,
                                waits=(tmm, war_tb))
                    war_pk[ci % 2] = tsl
                    tok_x_upd = k.v(lambda e, cc=cc, lo=lo, w=w:
                                    e.tensor_add(xs(cc, lo, lo + w),
                                                 xs(cc, lo, lo + w),
                                                 scr["tb"][:, :w]), waits=(tsl,))
                    war_tb = tok_x_upd
            k.flush()
        tok_x1 = tok_x_upd

        if "x1" in tap_d:
            k.dma(tap_d["x1"], x_sb[:].rearrange("p (n c) -> p n c", c=TH),
                  waits=(tok_x1,), q="y")
            k.flush()

        CH_OWN = [(HW, 520), (520, 1032), (1032, 1544), (1544, 2056)]
        with ExitStack() as PH:
            scr = {
                "t1": PH.enter_context(nc.sbuf_tensor("a2_t1", [128, 512], F32)),
                "t2": PH.enter_context(nc.sbuf_tensor("a2_t2", [128, 512], F32)),
            }
            ps = {"st0": PB[4], "st1": PB[5], "bc0": PB[2], "bc1": PB[3]}
            pk = [PB[0], PB[1]]
            gam_sb = PH.enter_context(nc.sbuf_tensor("gam_sb", [128, 8], F32))
            bet_sb = PH.enter_context(nc.sbuf_tensor("bet_sb", [128, 8], F32))
            f1b_sb = PH.enter_context(nc.sbuf_tensor("f1b_sb", [128, 32], F32))
            f2b_sb = PH.enter_context(nc.sbuf_tensor("f2b_sb", [128, 8], F32))
            tk_gam = k.dma(gam_sb[:], D["gam_c"], waits=(tok_x1,), q="y")
            tk_bet = k.dma(bet_sb[:], D["bet_c"], waits=(tok_x1,), q="y")
            tk_f1b = k.dma(f1b_sb[:].rearrange("p (h n) -> p h n", n=16),
                           D["fc1b_c"].rearrange("h p n -> p h n"),
                           waits=(tok_x1,), q="y")
            tk_f2b = k.dma(f2b_sb[:], D["fc2b_c"], waits=(tok_x1,), q="y")

            w1_sb = PH.enter_context(nc.sbuf_tensor("w1_sb", [128, 8 * 2048],
                                                    BF16))
            qraw = [PH.enter_context(nc.sbuf_tensor(f"qraw{i}", [128, 2048],
                                                    BF16)) for i in range(2)]
            bcs = [PH.enter_context(nc.sbuf_tensor(f"bcs{i}", [128, 512], F32))
                   for i in range(2)]
            A2a = PH.enter_context(ExitStack())
            xin_sb = A2a.enter_context(nc.sbuf_tensor("xin_sb", [128, 8 * T], BF16))
            for nm2 in ("cb", "cb2", "sq", "sq2"):
                scr[nm2] = A2a.enter_context(
                    nc.sbuf_tensor("a2_" + nm2, [128, 512], BF16))

            def xinf(pt, lo, w):
                return xin_sb[:, pt * T + lo - HW: pt * T + lo - HW + w]

            def xin_src(pt, lo, hi):
                return xin_sb[:, pt * T + lo - HW: pt * T + hi - HW]

            tok_xin = emit_ln(CH_OWN, ps, scr, tok_x1, add_to_x=True, xin_fn=xinf)
            if "dbg" in taps:
                scr["dbg"] = A2a.enter_context(
                    nc.sbuf_tensor("db_sb", [128, 2048], F32))
            tok_h = emit_ln(CH_OWN, ps, scr, tok_xin,
                            dst_fn=lambda pt, lo, w: xns(pt, lo, lo + w),
                            src_fn=xin_src, src_bf16=True,
                            gain=(lambda pt: gam_sb[:, pt:pt + 1],
                                  lambda pt: bet_sb[:, pt:pt + 1],
                                  (tk_gam, tk_bet)), dbg=True)
            tapn("xin", xin_sb[:], 8 * T, (tok_xin,))
            tapn("xin2", xin_sb[:], 8 * T, (tok_h,))
            tapn("hh", xn_sb[:], 8 * TH, (tok_h,))
            if "rows" in taps:
                drow = nc.dram_tensor("tap_rows", [2, 1024], F32,
                                      kind="ExternalOutput").ap()
                rv_sb = A2a.enter_context(nc.sbuf_tensor("rv_sb", [2, 1024], F32))
                k.v(lambda e: e.tensor_copy(rv_sb[:, 0:512], rowA[0:2, :]),
                    waits=(tok_h,))
                trr = k.v(lambda e: e.tensor_copy(rv_sb[:, 512:1024], rowB[0:2, :]))
                k.dma(drow, rv_sb[:], waits=(trr,), q="y")
            k.flush()
            A2a.close()

            w2_sb = PH.enter_context(nc.sbuf_tensor("w2_sb", [128, 16 * 1024], BF16))
            y_sb = PH.enter_context(nc.sbuf_tensor("y_sb", [128, 16 * 512], BF16))

            war_qraw = [tok_x1, tok_x1]
            war_w1 = tok_x1
            war_w2 = tok_h
            war_bcs = [tok_x1, tok_x1]
            tk_bcs_prev = None
            tok_x2 = None
            war_pk2 = [state["war_bc"], state["war_bc"]]
            for half in (0, 1):
                tok_w1 = None
                for kt in range(8):
                    tkr = k.dma(qraw[kt % 2][:], D["fc1q_t"][half][:, kt, :],
                                waits=(war_qraw[kt % 2],), q="gd")
                    for c4 in range(4):
                        ib = (kt * 4 + c4) % 2
                        tkb = k.dma(bcs[ib][:],
                                    D["fc1s_t"][half][kt * 4 + c4]
                                    .broadcast_to([128, 512]),
                                    waits=(war_bcs[ib], tk_bcs_prev),
                                    q="y", sig="m")
                        tk_bcs_prev = tkb
                        tok_w1 = k.v(
                            lambda e, kt=kt, c4=c4, ib=ib:
                            e.scalar_tensor_tensor(
                                w1_sb[:, kt * 2048 + c4 * 512:
                                      kt * 2048 + (c4 + 1) * 512],
                                qraw[kt % 2][:, c4 * 512:(c4 + 1) * 512],
                                -128.0, bcs[ib][:], ALU.add, ALU.mult),
                            waits=(tkr, tkb, war_w1))
                        war_bcs[ib] = tok_w1
                    war_qraw[kt % 2] = tok_w1
                tok_w2 = None
                for kt in range(16):
                    tkr = k.dma(qraw[kt % 2][:, 0:1024],
                                D["fc2q_t"][half][:, kt, :],
                                waits=(war_qraw[kt % 2],), q="gd")
                    for c4 in range(2):
                        ib = (kt * 2 + c4) % 2
                        tkb = k.dma(bcs[ib][:],
                                    D["fc2s_t"][half][kt * 2 + c4]
                                    .broadcast_to([128, 512]),
                                    waits=(war_bcs[ib], tk_bcs_prev),
                                    q="y", sig="m")
                        tk_bcs_prev = tkb
                        tok_w2 = k.v(
                            lambda e, kt=kt, c4=c4, ib=ib:
                            e.scalar_tensor_tensor(
                                w2_sb[:, kt * 1024 + c4 * 512:
                                      kt * 1024 + (c4 + 1) * 512],
                                qraw[kt % 2][:, c4 * 512:(c4 + 1) * 512],
                                -128.0, bcs[ib][:], ALU.add, ALU.mult),
                            waits=(tkr, tkb, war_w2))
                        war_bcs[ib] = tok_w2
                    war_qraw[kt % 2] = tok_w2
                war_y = None
                last_fc1 = None
                last_fc2 = None
                for ci, (lo, hi) in enumerate(CH_OWN):
                    w = hi - lo
                    tok_y = []
                    for nc2 in range(16):
                        bank = pk[nc2 % 2]
                        tmm = None
                        for kt in range(8):
                            tmm = k.mm(
                                bank[:, :w],
                                w1_sb[:, kt * 2048 + nc2 * 128:
                                      kt * 2048 + (nc2 + 1) * 128],
                                xns(kt, lo, hi),
                                start=(kt == 0), stop=(kt == 7),
                                waits=(tok_w1, tok_h, war_pk2[nc2 % 2], war_y))
                        last_fc1 = tmm
                        tsy = k.act(y_sb[:, nc2 * 512:nc2 * 512 + w], bank[:, :w],
                                    AF.Silu,
                                    bias=f1b_sb[:, half * 16 + nc2:
                                                half * 16 + nc2 + 1],
                                    waits=(tmm, tk_f1b))
                        war_pk2[nc2 % 2] = tsy
                        tok_y.append(tsy)
                    for cc in range(8):
                        bank = pk[cc % 2]
                        tmm = None
                        for kt in range(16):
                            tmm = k.mm(
                                bank[:, :w],
                                w2_sb[:, kt * 1024 + cc * 128:
                                      kt * 1024 + (cc + 1) * 128],
                                y_sb[:, kt * 512:kt * 512 + w],
                                start=(kt == 0), stop=(kt == 15),
                                waits=(tok_w2, tok_y[-1], war_pk2[cc % 2]))
                        last_fc2 = tmm
                        if half == 0:
                            tvz = k.v(lambda e, w=w, cc=cc, bank=bank:
                                      e.tensor_scalar(scr["t2"][:, :w],
                                                      bank[:, :w],
                                                      f2b_sb[:, cc:cc + 1],
                                                      0.5, ALU.add, ALU.mult),
                                      waits=(tmm, tk_f2b))
                            tok_x2 = k.v(lambda e, cc=cc, lo=lo, w=w:
                                         e.tensor_add(xs(cc, lo, lo + w),
                                                      xs(cc, lo, lo + w),
                                                      scr["t2"][:, :w]))
                            war_pk2[cc % 2] = tok_x2
                        else:
                            tok_x2 = k.v(lambda e, cc=cc, lo=lo, w=w, bank=bank:
                                         e.scalar_tensor_tensor(
                                             xs(cc, lo, lo + w), bank[:, :w],
                                             0.5, xs(cc, lo, lo + w),
                                             ALU.mult, ALU.add),
                                         waits=(tmm,))
                            war_pk2[cc % 2] = tok_x2
                    war_y = last_fc2
                war_w1 = last_fc1
                war_w2 = last_fc2
            tapn("w1d", w1_sb[:], 8 * 2048, (tok_x2,))
            tapn("w2d", w2_sb[:], 16 * 1024, (tok_x2,))
            tapn("yv", y_sb[:], 16 * 512, (tok_x2,))
            k.flush()

        if "x2" in tap_d:
            k.dma(tap_d["x2"], x_sb[:].rearrange("p (n c) -> p n c", c=TH),
                  waits=(tok_x2,), q="y")
            k.flush()

        MC = 256
        with ExitStack() as P3:
            scr3 = {
                "cb": P3.enter_context(nc.sbuf_tensor("a3_cb", [128, 512], BF16)),
                "cb2": P3.enter_context(nc.sbuf_tensor("a3_cb2", [128, 512], BF16)),
                "sq": P3.enter_context(nc.sbuf_tensor("a3_sq", [128, 512], BF16)),
                "sq2": P3.enter_context(nc.sbuf_tensor("a3_sq2", [128, 512], BF16)),
                "t1": P3.enter_context(nc.sbuf_tensor("a3_t1", [128, 512], F32)),
                "t2": P3.enter_context(nc.sbuf_tensor("a3_t2", [128, 512], F32)),
            }
            ps3 = {"st0": PB[4], "st1": PB[5], "bc0": PB[2], "bc1": PB[3]}
            pk3 = [PB[0], PB[1]]
            gts_sb = P3.enter_context(nc.sbuf_tensor("gts_sb", [128, 8 * T], BF16))
            aac_sb = P3.enter_context(nc.sbuf_tensor("aac_sb", [64, 1024], F32))
            atRb_sb = P3.enter_context(nc.sbuf_tensor("atRb_sb", [128, 1024],
                                                      BF16))

            tok_xn3 = emit_ln(CH_OWN, ps3, scr3, tok_x2,
                              dst_fn=lambda pt, lo, w: xns(pt, lo, lo + w))
            if "xn3" in taps:
                tapn("xn3", xn_sb[:], 8 * TH, (tok_xn3,))

            with ExitStack() as PG:
                wg_sb = PG.enter_context(nc.sbuf_tensor("wg_sb", [128, 8 * 1024],
                                                        BF16))
                egs = PG.enter_context(nc.sbuf_tensor("egs", [128, 512], BF16))
                tk_wg = k.dma(wg_sb[:].rearrange("p (n c) -> p n c", c=1024),
                              D["wg_t"], waits=(tok_x2,), q="gd")
                war_eg = None
                war_rowsg = state["war_rows"]
                war_bcg = state["war_bc"]
                war_stg = state["war_st"]
                tok_gts = None
                gi2 = 0
                for pt in range(NPT):
                    for ci, (lo, hi) in enumerate(CH_OWN):
                        w = hi - lo
                        bank = pk3[gi2 % 2]
                        tmm = None
                        for kt in range(NPT):
                            tmm = k.mm(bank[:, :w],
                                       wg_sb[:, kt * 1024 + pt * 128:
                                             kt * 1024 + (pt + 1) * 128],
                                       xns(kt, lo, hi), start=(kt == 0),
                                       stop=(kt == NPT - 1),
                                       waits=(tk_wg, tok_xn3, war_eg))
                        tse = k.act(egs[:, :w], bank[:, :w], AF.Exp, scale=0.125,
                                    waits=(tmm, war_eg))
                        tden = k.mm(ps3["st0"][0:2, :w], e2h_sb[:], egs[:, :w],
                                    waits=(tse, war_stg))
                        tcp = k.v(lambda e, w=w: e.tensor_copy(rowB[0:2, :w],
                                                               ps3["st0"][0:2, :w]),
                                  waits=(tden, war_rowsg))
                        war_stg = tcp
                        tln = k.act(rowB[0:2, :w], rowB[0:2, :w], AF.Ln,
                                    waits=(tcp,))
                        tln = k.act(rowB[0:2, :w], rowB[0:2, :w], AF.Exp,
                                    scale=-1.0)
                        k.v(lambda e, w=w: e.tensor_copy(rBh[0:2, :w],
                                                         rowB[0:2, :w]),
                            waits=(tln,))
                        trl = k.v(lambda e, w=w: e.tensor_sub(rBl[0:2, :w],
                                                              rowB[0:2, :w],
                                                              rBh[0:2, :w]))
                        k.mm(ps3["bc0"][:, :w], e2t_sb[:], rBh[0:2, :w],
                             start=True, stop=False, waits=(trl, war_bcg))
                        tbc = k.mm(ps3["bc0"][:, :w], e2t_sb[:], rBl[0:2, :w],
                                   start=False, stop=True)
                        tok_gts = k.v(
                            lambda e, pt=pt, lo=lo, w=w:
                            e.tensor_mul(gts_sb[:, pt * T + lo - HW:
                                                pt * T + lo - HW + w],
                                         egs[:, :w], ps3["bc0"][:, :w]),
                            waits=(tbc,))
                        war_eg = tok_gts
                        war_bcg = tok_gts
                        war_rowsg = tok_gts
                        gi2 += 1
                state["war_st"] = war_stg
                state["war_bc"] = war_bcg
                state["war_rows"] = war_rowsg
                k.flush()
            if "gts" in taps:
                tapn("gts", gts_sb[:], 8 * T, (tok_gts,))

            with ExitStack() as PK:
                wk_sb = PK.enter_context(nc.sbuf_tensor("wk_sb", [128, 8 * 1024],
                                                        BF16))
                wv_sb = PK.enter_context(nc.sbuf_tensor("wv_sb", [128, 8 * 1024],
                                                        BF16))
                cos_sb = PK.enter_context(nc.sbuf_tensor("cos_sb", [128, MC],
                                                         BF16))
                sin_sb = PK.enter_context(nc.sbuf_tensor("sin_sb", [128, MC],
                                                         BF16))
                latq_sb = PK.enter_context(nc.sbuf_tensor("latq_sb", [128, 512],
                                                          BF16))
                kr_sb = PK.enter_context(nc.sbuf_tensor("kr_sb", [128, 8 * MC],
                                                        BF16))
                kk2_sb = PK.enter_context(nc.sbuf_tensor("kk2_sb", [128, 8 * MC],
                                                         BF16))
                v2_sb = PK.enter_context(nc.sbuf_tensor("v2_sb", [128, 8 * MC],
                                                        BF16))
                vt2_sb = PK.enter_context(nc.sbuf_tensor("vt2_sb", [128, 32 * 64],
                                                         BF16))
                sm3 = [PK.enter_context(nc.sbuf_tensor(f"sm3_{i}", [128, 128], F32))
                       for i in range(2)]
                ee3 = [PK.enter_context(nc.sbuf_tensor(f"ee3_{i}", [128, 128],
                                                       BF16)) for i in range(2)]
                aw3 = [PK.enter_context(nc.sbuf_tensor(f"aw3_{i}", [128, 128],
                                                       BF16)) for i in range(2)]
                pT3 = [PK.enter_context(nc.sbuf_tensor(f"pT3_{i}", [128, 128],
                                                       BF16)) for i in range(2)]
                nm3 = [PK.enter_context(nc.sbuf_tensor(f"nm3_{i}", [128, 1], F32))
                       for i in range(2)]
                dn3 = [PK.enter_context(nc.sbuf_tensor(f"dn3_{i}", [128, 1], F32))
                       for i in range(2)]

                tk_wk = k.dma(wk_sb[:].rearrange("p (n c) -> p n c", c=1024),
                              D["wk_t"], waits=(tok_gts,), q="gd")
                tk_wv = k.dma(wv_sb[:].rearrange("p (n c) -> p n c", c=1024),
                              D["wv_t"], waits=(tok_gts,), q="gd")
                war_trig = tok_gts
                tk_la = k.dma(latq_sb[:, 0:256], D["latq_a"], waits=(tok_gts,),
                              q="gd")
                tk_lb = k.dma(latq_sb[:, 256:512], D["latq_b"], q="gd")
                tok_lq = k.v(lambda e: e.tensor_scalar_mul(latq_sb[:], latq_sb[:],
                                                           0.125),
                             waits=(tk_la, tk_lb))

                war_pk3 = [state["war_bc"], state["war_bc"]]
                war_kr = None
                war_v2 = None
                war_vt = [None, None]
                war_sc = [None, None]
                war_ex3 = [None, None]
                war_pt3 = [None, None]
                war_pat = [None, None]
                tok_aac = None
                it3 = 0
                for mc in range(8):
                    lo = HW + mc * MC
                    hi = lo + MC
                    tk_cos = k.dma(cos_sb[:], D["cos_t"][:, mc * MC:(mc + 1) * MC],
                                   waits=(war_trig,), q="gd")
                    tk_sin = k.dma(sin_sb[:], D["sin_t"][:, mc * MC:(mc + 1) * MC],
                                   q="gd")
                    tok_kr = None
                    for opt in range(8):
                        bank = pk3[opt % 2]
                        tmm = None
                        for kt in range(NPT):
                            tmm = k.mm(bank[:, :MC],
                                       wk_sb[:, kt * 1024 + opt * 128:
                                             kt * 1024 + (opt + 1) * 128],
                                       xns(kt, lo, hi), start=(kt == 0),
                                       stop=(kt == NPT - 1),
                                       waits=(tk_wk, war_pk3[opt % 2]))
                        tok_kr = k.act(kr_sb[:, opt * MC:(opt + 1) * MC],
                                       bank[:, :MC], AF.Copy,
                                       waits=(tmm, war_kr))
                        war_pk3[opt % 2] = tok_kr
                    tok_rope = None
                    for j in range(4):
                        ka = kr_sb[:, j * MC:(j + 1) * MC]
                        kb = kr_sb[:, (j + 4) * MC:(j + 5) * MC]
                        cs = cos_sb[:]
                        sn = sin_sb[:]
                        k.v(lambda e, ka=ka, cs=cs:
                            e.tensor_mul(scr3["t1"][:, :MC], ka, cs),
                            waits=(tok_kr, tk_cos, tk_sin, tok_rope))
                        k.v(lambda e, kb=kb, sn=sn:
                            e.tensor_mul(scr3["t2"][:, :MC], kb, sn))
                        k.v(lambda e, j=j:
                            e.tensor_sub(kk2_sb[:, j * MC:(j + 1) * MC],
                                         scr3["t1"][:, :MC], scr3["t2"][:, :MC]))
                        k.v(lambda e, ka=ka, sn=sn:
                            e.tensor_mul(scr3["t1"][:, :MC], ka, sn))
                        k.v(lambda e, kb=kb, cs=cs:
                            e.tensor_mul(scr3["t2"][:, :MC], kb, cs))
                        tok_rope = k.v(
                            lambda e, j=j:
                            e.tensor_add(kk2_sb[:, (j + 4) * MC:(j + 5) * MC],
                                         scr3["t1"][:, :MC], scr3["t2"][:, :MC]))
                    war_kr = tok_rope
                    war_trig = tok_rope
                    tok_v2 = None
                    for opt in range(8):
                        bank = pk3[opt % 2]
                        tmm = None
                        for kt in range(NPT):
                            tmm = k.mm(bank[:, :MC],
                                       wv_sb[:, kt * 1024 + opt * 128:
                                             kt * 1024 + (opt + 1) * 128],
                                       xns(kt, lo, hi), start=(kt == 0),
                                       stop=(kt == NPT - 1),
                                       waits=(tk_wv, war_pk3[opt % 2]))
                        tok_v2 = k.act(v2_sb[:, opt * MC:(opt + 1) * MC],
                                       bank[:, :MC], AF.Copy,
                                       waits=(tmm, war_v2))
                        war_pk3[opt % 2] = tok_v2
                    for c1 in range(2):
                        for h in range(16):
                            i2 = (c1 * 16 + h) % 2
                            hb2 = (h % 2) * 64
                            src = v2_sb[hb2:hb2 + 64,
                                        (h // 2) * MC + c1 * 128:
                                        (h // 2) * MC + (c1 + 1) * 128]
                            pv = PTB[i2][:, 256:320]
                            trt = k.tr(pv, src, idb_sb[hb2:hb2 + 64,
                                                       hb2:hb2 + 64],
                                       waits=(tok_v2, war_vt[i2]))
                            war_vt[i2] = k.v(
                                lambda e, c1=c1, h=h, pv=pv:
                                e.tensor_copy(vt2_sb[:, (c1 * 16 + h) * 64:
                                                     (c1 * 16 + h + 1) * 64],
                                              pv), waits=(trt,))
                    tok_vt2 = war_vt[0]
                    tok_vt2b = war_vt[1]
                    war_v2 = war_vt[1]
                    for hp2 in range(8):
                        for c1 in range(2):
                            p2 = it3 % 2
                            bank = pk3[p2]
                            for hd in range(2):
                                h = hp2 * 2 + hd
                                pb = (h % 4) * 32
                                tp = (pb, (hd * 64) % 128) if pb >= 96 else None
                                k.mm(bank[hd * 64:hd * 64 + 64, 0:128],
                                     latq_sb[pb:pb + 32,
                                             (h // 4) * 64:(h // 4) * 64 + 64],
                                     kk2_sb[pb:pb + 32,
                                            (h // 4) * MC + c1 * 128:
                                            (h // 4) * MC + (c1 + 1) * 128],
                                     start=True, stop=False,
                                     waits=(tok_lq, tok_rope, tok_v2,
                                            war_sc[p2]),
                                     tile_position=tp)
                                k.mm(bank[hd * 64:hd * 64 + 64, 0:128],
                                     latq_sb[pb:pb + 32,
                                             256 + (h // 4) * 64:
                                             256 + (h // 4) * 64 + 64],
                                     kk2_sb[pb:pb + 32,
                                            (4 + h // 4) * MC + c1 * 128:
                                            (4 + h // 4) * MC + (c1 + 1) * 128],
                                     start=False, stop=True,
                                     tile_position=tp)
                            tmm = k._emit("tensor", lambda e: e.nop(), (), "t", 1)
                            tvv = k.v(lambda e, p2=p2, bank=bank:
                                      e.tensor_add(sm3[p2][:], bank[:, 0:128],
                                                   C2_sb[:]),
                                      waits=(tmm, war_ex3[p2]))
                            war_sc[p2] = tvv
                            tvv = k.v(lambda e, p2=p2:
                                      e.reduce_max(nm3[p2][:], sm3[p2][:],
                                                   AX.X, negate=True))
                            tse = k.act(ee3[p2][:], sm3[p2][:], AF.Exp,
                                        bias=nm3[p2][:], accum=dn3[p2][:],
                                        waits=(tvv,))
                            war_ex3[p2] = tse
                            k.act(dn3[p2][:], dn3[p2][:], AF.Ln,
                                  waits=(tse,))
                            tvr = k.act(dn3[p2][:], dn3[p2][:], AF.Exp,
                                        scale=-1.0)
                            taw = k.act(aw3[p2][:], ee3[p2][:], AF.Copy,
                                        scale=dn3[p2][:], waits=(tvr,))
                            trt = k.tr(PTB[p2][:, 0:128], aw3[p2][:], idb_sb[:],
                                       waits=(taw, war_pt3[p2]))
                            tev = k.v(lambda e, p2=p2:
                                      e.tensor_copy(pT3[p2][:], PTB[p2][:, 0:128]),
                                      waits=(trt,))
                            war_pt3[p2] = tev
                            for hd in range(2):
                                h = hp2 * 2 + hd
                                pat = ps3["bc0"] if p2 == 0 else ps3["bc1"]
                                tpa = k.mm(pat[0:64, 0:64],
                                           pT3[p2][:, hd * 64:hd * 64 + 64],
                                           vt2_sb[:, (c1 * 16 + h) * 64:
                                                  (c1 * 16 + h + 1) * 64],
                                           start=True, stop=True,
                                           waits=(tev, tok_vt2, tok_vt2b,
                                                  war_pat[p2]))
                                if mc == 0:
                                    tac = k.v(lambda e, h=h, pat=pat:
                                              e.tensor_copy(
                                                  aac_sb[:, h * 64:(h + 1) * 64],
                                                  pat[0:64, 0:64]), waits=(tpa,))
                                else:
                                    tac = k.v(lambda e, h=h, pat=pat:
                                              e.tensor_add(
                                                  aac_sb[:, h * 64:(h + 1) * 64],
                                                  aac_sb[:, h * 64:(h + 1) * 64],
                                                  pat[0:64, 0:64]), waits=(tpa,))
                                war_pat[p2] = tac
                                tok_aac = tac
                            it3 += 1
                    war_pk3[0] = war_sc[0]
                    war_pk3[1] = war_sc[1]
                k.flush()

            tk_ar = k.dma(ar_in.ap(), aac_sb[:], waits=(tok_aac,), q="gd")
            tok_cc = k.g(lambda e: e.collective_compute(
                "AllReduce", ALU.add, replica_groups=[list(range(NCORES))],
                ins=[ar_in.ap().opt()], outs=[ar_out.ap().opt()]),
                waits=(tk_ar,), sig="cc")
            tk_r0 = k.dma(atRb_sb[0:64, :], ar_out.ap(), waits=(tok_cc,), q="gd")
            tk_r1 = k.dma(atRb_sb[64:128, :], ar_out.ap(), q="gd")

            with ExitStack() as PW:
                wp_sb = PW.enter_context(nc.sbuf_tensor("wp_sb2", [128, 8 * 1024],
                                                        BF16))
                w8_sb = PW.enter_context(nc.sbuf_tensor("w8_sb", [128, 8 * T],
                                                        BF16))
                tk_wp = k.dma(wp_sb[:].rearrange("p (n c) -> p n c", c=1024),
                              D["wp_t"], waits=(tok_aac,), q="gd")
                war_pw = [None, None]
                tok_w8 = None
                iw = 0
                for h in range(16):
                    hb = (h % 2) * 64
                    for ci, (lo, hi) in enumerate(CH_OWN):
                        w = hi - lo
                        bank = pk3[iw % 2]
                        tmm = k.mm(bank[hb:hb + 64, :w],
                                   atRb_sb[hb:hb + 64, h * 64:(h + 1) * 64],
                                   gts_sb[hb:hb + 64,
                                          (h // 2) * T + lo - HW:
                                          (h // 2) * T + hi - HW],
                                   start=True, stop=True,
                                   waits=(tk_r0, tk_r1, tok_gts,
                                          war_pw[iw % 2]))
                        tok_w8 = k.act(w8_sb[hb:hb + 64,
                                             (h // 2) * T + lo - HW:
                                             (h // 2) * T + hi - HW],
                                       bank[hb:hb + 64, :w], AF.Copy,
                                       waits=(tmm,))
                        war_pw[iw % 2] = tok_w8
                        iw += 1
                tok_x3 = None
                war_tb3 = None
                for cc in range(8):
                    for ci, (lo, hi) in enumerate(CH_OWN):
                        w = hi - lo
                        bank = pk3[iw % 2]
                        tmm = None
                        for kt in range(NPT):
                            tmm = k.mm(bank[:, :w],
                                       wp_sb[:, kt * 1024 + cc * 128:
                                             kt * 1024 + (cc + 1) * 128],
                                       w8_sb[:, kt * T + lo - HW:
                                             kt * T + hi - HW],
                                       start=(kt == 0), stop=(kt == NPT - 1),
                                       waits=(tk_wp, tok_w8, war_pw[iw % 2]))
                        tok_x3 = k.v(lambda e, cc=cc, lo=lo, w=w, bank=bank:
                                     e.tensor_add(xs(cc, lo, lo + w),
                                                  xs(cc, lo, lo + w),
                                                  bank[:, :w]),
                                     waits=(tmm, war_tb3))
                        war_pw[iw % 2] = tok_x3
                        war_tb3 = tok_x3
                        iw += 1
                k.flush()
            if "x3" in tap_d:
                k.dma(tap_d["x3"], x_sb[:].rearrange("p (n c) -> p n c", c=TH),
                      waits=(tok_x3,), q="y")
            tok_xn4 = emit_ln(CH_OWN, ps3, scr3, tok_x3,
                              dst_fn=lambda pt, lo, w: xns(pt, lo, lo + w))
            tapn("xn4", xn_sb[:], 8 * TH, (tok_xn4,))
            k.flush()
        LNS.close()

        m2_d = nc.dram_tensor("m2_d", [128, 32, 512], BF16)
        with ExitStack() as P4:
            with ExitStack() as PM:
                rmb_sb = PM.enter_context(nc.sbuf_tensor("rmb_sb",
                                                         [128, 32 * 128], BF16))
                rub_sb = PM.enter_context(nc.sbuf_tensor("rub_sb",
                                                         [128, 32 * 512], BF16))
                m2s = PM.enter_context(nc.sbuf_tensor("m2s", [128, 512], BF16))
                tk_rm = k.dma(rmb_sb[:].rearrange("p (n c) -> p n c", c=128),
                              D["rmbdT_t"], waits=(tok_xn4,), q="gd")
                tk_ru = k.dma(rub_sb[:].rearrange("p (n c) -> p n c", c=512),
                              D["rupbd_t"], waits=(tok_xn4,), q="gd")
                war_m2mm = None
                war_m2s = None
                for gi in range(32):
                    tmm = k.mm(PB[0][:, :],
                               rmb_sb[:, gi * 128:(gi + 1) * 128],
                               rub_sb[:, gi * 512:(gi + 1) * 512],
                               start=True, stop=True,
                               waits=(tk_rm, tk_ru, war_m2mm))
                    tev = k.act(m2s[:], PB[0][:, :], AF.Copy,
                                waits=(tmm, war_m2s))
                    war_m2mm = tev
                    war_m2s = k.dma(m2_d.ap()[:, gi, :], m2s[:], waits=(tev,),
                                    q="y")
                tok_m2 = war_m2s
                k.flush()

            hf_sb = P4.enter_context(nc.sbuf_tensor("hf_sb", [128, 16 * T], BF16))
            wcb_sb = P4.enter_context(nc.sbuf_tensor("wcb_sb", [128, 16 * 1024],
                                                     BF16))
            m2b = [P4.enter_context(nc.sbuf_tensor(f"m2b{i}", [128, 512],
                                                   BF16)) for i in range(2)]
            war_cp = [None, None]
            tok_x4 = None
            war_wcb = tok_m2
            war_m2b = [tok_m2, tok_m2]
            war_z = [None, None]
            war_hf = None
            tk_m2b_prev = None
            war_hfspill = None
            hfd = None
            if "hf" in taps:
                hfd = nc.dram_tensor("tap_hf", [128, 8, 16 * T], BF16,
                                     kind="ExternalOutput").ap()
            for fb in range(8):
                tk_wcb = k.dma(
                    wcb_sb[:].rearrange("p (n c) -> p n c", c=1024),
                    D["wc_t"][:, fb * 16:(fb + 1) * 16, :],
                    waits=(war_wcb,), q="gd")
                tok_hf = None
                for gl in range(4):
                    gi = fb * 4 + gl
                    tk_m2b = k.dma(m2b[gl % 2][:], m2_d.ap()[:, gi, :],
                                   waits=(war_m2b[gl % 2], tk_m2b_prev),
                                   q="y", sig="m")
                    tk_m2b_prev = tk_m2b
                    for zc in range(4):
                        for ci, (lo, hi) in enumerate(CH_OWN):
                            w = hi - lo
                            zb = PB[4 + (zc * 4 + ci) % 2]
                            tmm = k.mm(zb[:, :w],
                                       m2b[gl % 2][:, zc * 128:(zc + 1) * 128],
                                       xns(gi % 8, lo, hi),
                                       start=True, stop=True,
                                       waits=(tk_m2b, tok_xn4, tok_x4,
                                              war_z[(zc * 4 + ci) % 2]))
                            ft = gl * 4 + zc
                            tok_hf = k.act(
                                hf_sb[:, ft * T + lo - HW:ft * T + hi - HW],
                                zb[:, :w], AF.Gelu,
                                waits=(tmm, war_hf, war_hfspill))
                            war_z[(zc * 4 + ci) % 2] = tok_hf
                    war_m2b[gl % 2] = tok_hf
                if hfd is not None:
                    war_hfspill = k.dma(hfd[:, fb, :], hf_sb[:],
                                        waits=(tok_hf,), q="gd")
                war_hf = None
                last_mm = None
                icp = 0
                for cc in range(8):
                    for ci, (lo, hi) in enumerate(CH_OWN):
                        w = hi - lo
                        bank = PB[icp % 2]
                        tmm = None
                        for fkt in range(16):
                            tmm = k.mm(bank[:, :w],
                                       wcb_sb[:, fkt * 1024 + cc * 128:
                                              fkt * 1024 + (cc + 1) * 128],
                                       hf_sb[:, fkt * T + lo - HW:
                                             fkt * T + hi - HW],
                                       start=(fkt == 0), stop=(fkt == 15),
                                       waits=(tk_wcb, tok_hf,
                                              war_cp[icp % 2]))
                            last_mm = tmm
                        tok_x4 = k.v(lambda e, cc=cc, lo=lo, w=w, bank=bank:
                                     e.tensor_add(xs(cc, lo, lo + w),
                                                  xs(cc, lo, lo + w),
                                                  bank[:, :w]),
                                     waits=(tmm,))
                        war_cp[icp % 2] = tok_x4
                        icp += 1
                war_wcb = last_mm
                war_hf = tok_x4
            k.flush()

        tok_out = k.dma(out_x,
                        x_sb[:].rearrange("p (n c) -> p n c", c=TH)[:, :, HW:HW + T],
                        waits=(tok_x4,), q="y")
        k._emit("sync", lambda e: e.nop(), (tok_out,), None, 1)
        k.flush()

    return nc



def _tile_pm(a, kt, cols):
    return np.ascontiguousarray(a.reshape(kt, 128, cols).transpose(1, 0, 2))


def stage_inputs(inp):
    f = np.float32
    x = np.asarray(inp["L0_attn_x"], f)[0]
    xT = np.ascontiguousarray(x.T)

    shared = {}
    wq = np.asarray(inp["L0_attn_Wq"], f)
    shared["wq_t"] = np.ascontiguousarray(
        wq.reshape(8, 128, 8, 128).transpose(2, 1, 0, 3))
    wkv = np.asarray(inp["L0_attn_Wkv"], f)
    kcols = wkv[:, :1024].reshape(1024, 8, 128)
    vcols = wkv[:, 1024:].reshape(1024, 8, 128)
    kv = np.concatenate([kcols, vcols], axis=2)
    shared["wkv_t"] = np.ascontiguousarray(
        kv.reshape(8, 128, 8, 256).transpose(2, 1, 0, 3))
    shared["wwin_t"] = _tile_pm(np.asarray(inp["L0_attn_Wwin"], f), 8, 32)
    shared["bwin_c"] = np.asarray(inp["L0_attn_bwin"], f).reshape(32, 1)
    for nm, key in (("qg", "L0_attn_q_g"), ("qb", "L0_attn_q_b"),
                    ("kg", "L0_attn_k_g"), ("kb", "L0_attn_k_b")):
        v = np.asarray(inp[key], f)
        shared[nm + "_c"] = np.ascontiguousarray(
            np.concatenate([v, v]).reshape(128, 1))
    shared["wout_t"] = _tile_pm(np.asarray(inp["L0_attn_Wout"], f), 8, 1024)
    shared["gam_c"] = np.ascontiguousarray(
        np.asarray(inp["L0_ffn_ln_gamma"], f).reshape(8, 128).T)
    shared["bet_c"] = np.ascontiguousarray(
        np.asarray(inp["L0_ffn_ln_beta"], f).reshape(8, 128).T)
    fc1q = np.asarray(inp["L0_ffn_fc1_q"], np.int32)
    shared["fc1q_t"] = np.ascontiguousarray(
        fc1q.reshape(8, 128, 2, 2048).transpose(2, 1, 0, 3))
    shared["fc1s_t"] = np.ascontiguousarray(
        np.asarray(inp["L0_ffn_fc1_scales"], f).reshape(8, 2, 2048)
        .transpose(1, 0, 2).reshape(2, 32, 1, 512))
    shared["fc1b_c"] = np.ascontiguousarray(
        np.asarray(inp["L0_ffn_fc1_bias"], f).reshape(2, 16, 128)
        .transpose(0, 2, 1))
    fc2q = np.asarray(inp["L0_ffn_fc2_q"], np.int32)
    shared["fc2q_t"] = np.ascontiguousarray(
        fc2q.reshape(2, 16, 128, 1024).transpose(0, 2, 1, 3))
    shared["fc2s_t"] = np.ascontiguousarray(
        np.asarray(inp["L0_ffn_fc2_scales"], f).reshape(2, 32, 1, 512))
    shared["fc2b_c"] = np.ascontiguousarray(
        np.asarray(inp["L0_ffn_fc2_bias"], f).reshape(8, 128).T)

    wk = np.asarray(inp["L1_attn_W_k"], f)
    perm = np.empty(1024, np.int64)
    for h in range(16):
        for i in range(32):
            perm[h * 32 + i] = h * 64 + 2 * i
            perm[512 + h * 32 + i] = h * 64 + 2 * i + 1
    shared["wk_t"] = _tile_pm(np.ascontiguousarray(wk.T[:, perm]), 8, 1024)
    shared["wv_t"] = _tile_pm(np.ascontiguousarray(
        np.asarray(inp["L1_attn_W_v"], f).T), 8, 1024)
    shared["wg_t"] = _tile_pm(np.ascontiguousarray(
        np.asarray(inp["L1_attn_W_g"], f).T), 8, 1024)
    shared["wp_t"] = _tile_pm(np.ascontiguousarray(
        np.asarray(inp["L1_attn_W_p"], f).T), 8, 1024)
    lq = np.asarray(inp["L1_attn_latent_queries"], f)[0]
    la = np.zeros((128, 256), f)
    lb = np.zeros((128, 256), f)
    for h in range(16):
        q = lq[:, h, :]
        pbase = (h % 4) * 32
        cbase = (h // 4) * 64
        la[pbase:pbase + 32, cbase:cbase + 64] = q.T[0::2, :]
        lb[pbase:pbase + 32, cbase:cbase + 64] = q.T[1::2, :]
    shared["latq_a"] = la
    shared["latq_b"] = lb

    rm = np.asarray(inp["L1_ffn_rail_maker"], f)
    rup = np.asarray(inp["L1_ffn_rail_up_proj"], f)
    rmT = np.zeros((32, 128, 128), f)
    rub = np.zeros((32, 128, 512), f)
    for h in range(4):
        for g in range(8):
            gi = h * 8 + g
            for rl in range(8):
                r = g * 8 + rl
                rmT[gi, rl * 16:(rl + 1) * 16, rl * 16:(rl + 1) * 16] = rm[h, r].T
                rub[gi, rl * 16:(rl + 1) * 16, rl * 64:(rl + 1) * 64] = rup[h, r].T
    shared["rmbdT_t"] = np.ascontiguousarray(rmT.transpose(1, 0, 2))
    shared["rupbd_t"] = np.ascontiguousarray(rub.transpose(1, 0, 2))
    wc = np.asarray(inp["L1_ffn_w_cproj"], f)
    shared["wc_t"] = np.ascontiguousarray(
        wc.T.reshape(128, 128, 1024).transpose(1, 0, 2))

    in_maps = []
    inv = 1.0 / (10000.0 ** (np.arange(0, 64, 2, dtype=np.float64) / 64.0))
    for c in range(NCORES):
        m = dict(shared)
        t0 = c * T
        xt = np.zeros((1024, TH), f)
        lo = max(0, t0 - HW); hi = min(16384, t0 + T + HW)
        xt[:, lo - (t0 - HW):hi - (t0 - HW)] = xT[:, lo:hi]
        m["x_t"] = np.ascontiguousarray(xt.reshape(8, 128, TH).transpose(1, 0, 2))
        km = np.zeros((1, TH), f)
        km[0, lo - (t0 - HW):hi - (t0 - HW)] = 1.0
        m["kv_mask"] = km
        ang = np.arange(t0, t0 + T, dtype=np.float64)[None, :] * inv[:, None]
        m["cos_t"] = np.ascontiguousarray(np.tile(np.cos(ang).astype(f), (4, 1)))
        m["sin_t"] = np.ascontiguousarray(np.tile(np.sin(ang).astype(f), (4, 1)))
        in_maps.append(m)
    return in_maps


_NC_CACHE = {}


def _get_nc(taps=()):
    key = tuple(taps)
    if key not in _NC_CACHE:
        _NC_CACHE[key] = build_program(taps=key)
    return _NC_CACHE[key]


def run_cores(inputs, taps=()):
    in_maps = stage_inputs(inputs)
    nc = _get_nc(taps)
    return run_bass_kernel_spmd(nc, in_maps, core_ids=list(range(NCORES)))


def kernel(**inputs):
    res = run_cores(inputs)
    y = np.empty((16384, 1024), np.float32)
    for i, r in enumerate(res.results):
        p = r["out_x"]
        y[T * i:T * (i + 1)] = p.transpose(1, 0, 2).reshape(1024, T).T
    return y[None]


# version 25
# speedup vs baseline: 1.0368x; 1.0116x over previous
import numpy as np
import ml_dtypes
from contextlib import ExitStack
import concourse.bass as bass
import concourse.mybir as mybir
from concourse.bass_utils import run_bass_kernel_spmd

F32 = mybir.dt.float32
BF16 = mybir.dt.bfloat16
I32 = mybir.dt.int32
AF = mybir.ActivationFunctionType
ALU = mybir.AluOpType
AX = mybir.AxisListType

NCORES = 8
T = 2048
HW = 8
TH = T + 2 * HW
NPT = 8
EPS = 1e-5

ENGINES = ("sync", "gpsimd", "scalar", "vector", "tensor")


class K:

    def __init__(self, nc, es, block):
        self.nc = nc
        self.block = block
        chans = ("t", "s", "v", "g", "y", "gd", "cc", "m")
        self.sem = {c: es.enter_context(nc.semaphore("sm_" + c)) for c in chans}
        self.cnt = dict.fromkeys(chans, 0)
        self.maxw = {e: dict.fromkeys(chans, 0) for e in ENGINES}
        self.pend = {e: [] for e in ENGINES}

    def _emit(self, eng, fn, waits, sig, inc):
        tok = None
        if sig is not None:
            self.cnt[sig] += inc
            tok = (sig, self.cnt[sig])
        self.pend[eng].append((fn, tuple(w for w in waits if w is not None), sig, inc))
        return tok

    def dma(self, out, in_, waits=(), q="gd", sig=None):
        eng = "gpsimd" if q == "gd" else "sync"
        return self._emit(eng, lambda e, o=out, i=in_: e.dma_start(out=o, in_=i),
                          waits, sig if sig is not None else q, 16)

    def mm(self, out, lhsT, rhs, start=True, stop=True, waits=(), sig="t",
           tile_position=None):
        return self._emit(
            "tensor",
            lambda e, o=out, l=lhsT, r=rhs, a=start, b=stop, tp=tile_position:
                e.matmul(o, l, r, start=a, stop=b, tile_position=tp),
            waits, sig, 1)

    def tr(self, out, in_, ident, waits=(), sig="t"):
        return self._emit(
            "tensor", lambda e, o=out, i=in_, d=ident: e.transpose(o, i, d),
            waits, sig, 1)

    def act(self, out, in_, func, bias=0.0, scale=1.0, accum=None, waits=(), sig="s"):
        return self._emit(
            "scalar",
            lambda e, o=out, i=in_, f=func, b=bias, sc=scale, a=accum:
                e.activation(o, i, f, bias=b, scale=sc, accum_out=a),
            waits, sig, 1)

    def v(self, fn, waits=(), sig="v"):
        return self._emit("vector", fn, waits, sig, 1)

    def g(self, fn, waits=(), sig="g", inc=1):
        return self._emit("gpsimd", fn, waits, sig, inc)

    def barrier_tok(self):
        return [(c, self.cnt[c]) for c in ("t", "s", "v", "g", "y", "gd")]

    def flush(self):
        for eng in ENGINES:
            ops = self.pend[eng]
            if not ops:
                continue
            self.pend[eng] = []
            sem = self.sem
            maxw = self.maxw[eng]

            def body(e, ops=ops, maxw=maxw, sem=sem):
                for fn, waits, sig, inc in ops:
                    for ch, val in waits:
                        if maxw[ch] < val:
                            e.wait_ge(sem[ch], val)
                            maxw[ch] = val
                    inst = fn(e)
                    if sig is not None:
                        inst.then_inc(sem[sig], inc)
            getattr(self.block, eng)(body)


def _bd(a):
    return a.astype(ml_dtypes.bfloat16)


def build_program(taps=()):
    nc = bass.Bass("TRN2", target_bir_lowering=False)

    D = {}

    def din(name, shape, dt=F32):
        D[name] = nc.dram_tensor(name, list(shape), dt, kind="ExternalInput").ap()
        return D[name]

    din("x_t", (128, NPT, TH))
    din("kv_mask", (1, TH))
    din("cos_t", (128, T))
    din("sin_t", (128, T))
    din("wq_t", (8, 128, 8, 128))
    din("wkv_t", (8, 128, 8, 256))
    din("wwin_t", (128, 8, 32))
    din("bwin_c", (32, 1))
    din("qg_c", (128, 1)); din("qb_c", (128, 1))
    din("kg_c", (128, 1)); din("kb_c", (128, 1))
    din("wout_t", (128, 8, 1024))
    din("gam_c", (128, 8)); din("bet_c", (128, 8))
    din("fc1q_t", (2, 128, 8, 2048), I32)
    din("fc1s_t", (2, 32, 1, 512))
    din("fc1b_c", (2, 128, 16))
    din("fc2q_t", (2, 128, 16, 1024), I32)
    din("fc2s_t", (2, 32, 1, 512))
    din("fc2b_c", (128, 8))
    din("wk_t", (128, 8, 1024))
    din("wv_t", (128, 8, 1024))
    din("wg_t", (128, 8, 1024))
    din("wp_t", (128, 8, 1024))
    din("latq_a", (128, 256)); din("latq_b", (128, 256))
    din("rmbdT_t", (128, 32, 128))
    din("rupbd_t", (128, 32, 512))
    din("wc_t", (128, 128, 1024))

    out_x = nc.dram_tensor("out_x", [128, NPT, T], F32, kind="ExternalOutput").ap()
    tap_d = {}
    for tname in taps:
        if tname in ("x1", "x2", "x3", "x4"):
            tap_d[tname] = nc.dram_tensor("tap_" + tname, [128, NPT, TH], F32,
                                          kind="ExternalOutput").ap()

    ar_in = nc.dram_tensor("ar_in", [64, 1024], F32)
    ar_out = nc.dram_tensor("ar_out", [64, 1024], F32)

    ident_f = nc.inline_tensor(np.eye(128, dtype=np.float32), name="c_identf").ap()
    ident_b = nc.inline_tensor(_bd(np.eye(128, dtype=np.float32)), name="c_identb").ap()
    ones1 = nc.inline_tensor(_bd(np.ones((128, 1), np.float32)), name="c_ones1").ap()
    onesr = nc.inline_tensor(_bd(np.ones((1, 128), np.float32)), name="c_onesr").ap()
    e2h = np.zeros((128, 2), np.float32); e2h[:64, 0] = 1; e2h[64:, 1] = 1
    e2h_c = nc.inline_tensor(_bd(e2h), name="c_e2h").ap()
    e2t = np.zeros((2, 128), np.float32); e2t[0, :64] = 1; e2t[1, 64:] = 1
    e2t_c = nc.inline_tensor(_bd(e2t), name="c_e2t").ap()
    jj, ll = np.meshgrid(np.arange(144), np.arange(128))
    R = np.abs(jj - ll - 8.0).astype(np.float32)
    R_c = nc.inline_tensor(R, name="c_R").ap()
    NEG2 = np.where((jj - ll >= 0) & (jj - ll <= 16), -5e3, -1e9).astype(np.float32)
    NEG2_c = nc.inline_tensor(NEG2, name="c_NEG2").ap()
    wsc1 = np.concatenate([np.full(16, 4.0, np.float32),
                           np.full(16, 4.75, np.float32)]).reshape(32, 1)
    wsc1_c = nc.inline_tensor(wsc1, name="c_wsc1").ap()
    wsc2 = np.concatenate([np.full(16, 4.5, np.float32),
                           np.full(16, 5.25, np.float32)]).reshape(32, 1)
    wsc2_c = nc.inline_tensor(wsc2, name="c_wsc2").ap()
    cc_, ll2 = np.meshgrid(np.arange(128), np.arange(64))
    C1 = np.where(cc_ <= ll2, 0.0, -1e9).astype(np.float32)
    C2_c = nc.inline_tensor(_bd(np.concatenate([C1, C1], 0)),
                            name="c_C2").ap()

    with ExitStack() as ES:
        block = ES.enter_context(nc.Block())
        k = K(nc, ES, block)

        def tapn(name, ap, cols, waits):
            if name in taps:
                d = nc.dram_tensor("tap_" + name, [128, cols], F32,
                                   kind="ExternalOutput").ap()
                k.dma(d, ap, waits=waits, q="gd")

        x_sb = ES.enter_context(nc.sbuf_tensor("x_sb", [128, NPT * TH], F32))
        xn_sb = ES.enter_context(nc.sbuf_tensor("xn_sb", [128, NPT * TH], BF16))
        PB = [ES.enter_context(nc.psum_tensor(f"pbank{i}", [128, 512], F32))
              for i in range(6)]
        PTB = [ES.enter_context(nc.psum_tensor(f"ptbank{i}", [128, 1024], BF16))
               for i in range(2)]
        LNS = ES.enter_context(ExitStack())
        idb_sb = LNS.enter_context(nc.sbuf_tensor("idb_sb", [128, 128], BF16))
        ones1_sb = LNS.enter_context(nc.sbuf_tensor("ones1_sb", [128, 1], BF16))
        onesr_sb = LNS.enter_context(nc.sbuf_tensor("onesr_sb", [1, 128], BF16))
        e2h_sb = LNS.enter_context(nc.sbuf_tensor("e2h_sb", [128, 2], BF16))
        e2t_sb = LNS.enter_context(nc.sbuf_tensor("e2t_sb", [2, 128], BF16))
        C2_sb = LNS.enter_context(nc.sbuf_tensor("C2_sb", [128, 128], BF16))
        mask_hb = LNS.enter_context(nc.sbuf_tensor("mask_hb", [128, 16], BF16))
        rowA = LNS.enter_context(nc.sbuf_tensor("rowA", [2, 512], F32))
        rowB = LNS.enter_context(nc.sbuf_tensor("rowB", [2, 512], F32))
        rowC = LNS.enter_context(nc.sbuf_tensor("rowC", [2, 512], F32))
        rAh = LNS.enter_context(nc.sbuf_tensor("rAh", [2, 512], BF16))
        rAl = LNS.enter_context(nc.sbuf_tensor("rAl", [2, 512], BF16))
        rBh = LNS.enter_context(nc.sbuf_tensor("rBh", [2, 512], BF16))
        rBl = LNS.enter_context(nc.sbuf_tensor("rBl", [2, 512], BF16))
        eps_sb = LNS.enter_context(nc.sbuf_tensor("eps_sb", [128, 1], F32))

        tk_const = []
        for dst, src in ((idb_sb, ident_b), (ones1_sb, ones1),
                         (onesr_sb, onesr), (e2h_sb, e2h_c), (e2t_sb, e2t_c),
                         (C2_sb, C2_c)):
            tk_const.append(k.dma(dst[:], src, q="y"))
        tk_C2 = tk_const[-1]
        tok_eps = k.v(lambda e: e.memset(eps_sb[:], EPS))
        tk_xc = []
        _tp = None
        for (clo, chi) in ((0, 512), (512, 1024), (1024, 1536), (1536, TH)):
            _tp = k.dma(
                x_sb[:].rearrange("p (n c) -> p n c", c=TH)[:, :, clo:chi],
                D["x_t"][:, :, clo:chi], waits=(_tp,), q="y", sig="m")
            tk_xc.append(_tp)

        def xs(pt, lo, hi):
            return x_sb[:, pt * TH + lo: pt * TH + hi]

        def xns(pt, lo, hi):
            return xn_sb[:, pt * TH + lo: pt * TH + hi]

        CH_TH = [(0, 512, 0), (512, 1024, 512), (1024, 1536, 1024),
                 (1536, 2048, 1536), (1552, TH, 2048)]
        CH_T = [(HW, 520), (520, 1032), (1032, 1544), (1544, 2056)]

        state = {"war_cb": None, "war_st": None, "war_bc": None, "war_rows": None}

        def rows_chunk(W, P, tok_in, nfeat):
            k.v(lambda e: e.tensor_scalar_mul(rowA[0:P, :W], rowA[0:P, :W],
                                              1.0 / nfeat), waits=(tok_in,))
            k.v(lambda e: e.tensor_mul(rowC[0:P, :W], rowA[0:P, :W],
                                       rowA[0:P, :W]))
            tvv = k.v(lambda e: e.scalar_tensor_tensor(
                rowB[0:P, :W], rowB[0:P, :W], 1.0 / nfeat, rowC[0:P, :W],
                ALU.mult, ALU.subtract))
            tss = k.act(rowB[0:P, :W], rowB[0:P, :W], AF.Ln,
                        bias=eps_sb[0:P, :], waits=(tvv, tok_eps))
            tss = k.act(rowB[0:P, :W], rowB[0:P, :W], AF.Exp, scale=-0.5)
            k.v(lambda e: e.tensor_copy(rAh[0:P, :W], rowA[0:P, :W]))
            k.v(lambda e: e.tensor_sub(rAl[0:P, :W], rowA[0:P, :W], rAh[0:P, :W]))
            k.v(lambda e: e.tensor_copy(rBh[0:P, :W], rowB[0:P, :W]),
                waits=(tss,))
            return k.v(lambda e: e.tensor_sub(rBl[0:P, :W], rowB[0:P, :W],
                                              rBh[0:P, :W]))

        def bcast2(ps, W, lhs, tok_rows, war):
            k.mm(ps["bc0"][:, :W], lhs, rAh[0:lhs.shape[0], :W],
                 start=True, stop=False, waits=(tok_rows, war))
            b1 = k.mm(ps["bc0"][:, :W], lhs, rAl[0:lhs.shape[0], :W],
                      start=False, stop=True)
            k.mm(ps["bc1"][:, :W], lhs, rBh[0:lhs.shape[0], :W],
                 start=True, stop=False)
            b2 = k.mm(ps["bc1"][:, :W], lhs, rBl[0:lhs.shape[0], :W],
                      start=False, stop=True)
            return (b1, b2)

        def emit_ln(rng, ps, scr, tok_ready, dst_fn=None, add_to_x=False,
                    xin_fn=None, src_fn=None, src_bf16=False, gain=None,
                    dbg=False):
            if src_fn is None:
                src_fn = xs
            out_tok = None
            war_cb = [state.get("war_cb"), state.get("war_cb")]
            war_sq = [state.get("war_cb"), state.get("war_cb")]
            for nch, chunk in enumerate(rng):
                tok_rdy = (tok_ready[min(nch, len(tok_ready) - 1)]
                           if isinstance(tok_ready, list) else tok_ready)
                lo, hi = chunk[0], chunk[1]
                alo = chunk[2] if len(chunk) > 2 else lo
                w = hi - lo
                off = alo - lo
                wa = hi - alo
                tm1 = tm2 = None
                for pt in range(NPT):
                    pr = pt % 2
                    cb = scr["cb"] if pr == 0 else scr["cb2"]
                    sq = scr["sq"] if pr == 0 else scr["sq2"]
                    if src_bf16:
                        cbv = src_fn(pt, lo, lo + w)
                        tv1 = tok_rdy
                    else:
                        tv1 = k.v(lambda e, pt=pt, lo=lo, w=w, cb=cb:
                                  e.tensor_copy(cb[:, :w], src_fn(pt, lo, lo + w)),
                                  waits=(tok_rdy, war_cb[pr]))
                        cbv = cb[:, :w]
                    tv2 = k.v(lambda e, w=w, cbv=cbv, sq=sq:
                              e.tensor_mul(sq[:, :w], cbv, cbv),
                              waits=(war_sq[pr], tv1))
                    tm1 = k.mm(ps["st0"][0:1, :w], ones1_sb[:], cbv,
                               start=(pt == 0), stop=(pt == NPT - 1),
                               waits=(tv1, state["war_st"]))
                    tm2 = k.mm(ps["st1"][0:1, :w], ones1_sb[:], sq[:, :w],
                               start=(pt == 0), stop=(pt == NPT - 1),
                               waits=(tv2, state["war_st"]))
                    war_cb[pr] = tm1
                    war_sq[pr] = tm2
                state["war_cb"] = tm2
                k.act(rowA[0:1, :w], ps["st0"][0:1, :w], AF.Copy,
                      waits=(tm1, state["war_rows"]))
                tev = k.act(rowB[0:1, :w], ps["st1"][0:1, :w], AF.Copy,
                            waits=(tm2,))
                state["war_st"] = tev
                tok_rows = rows_chunk(w, 1, tev, 1024.0)
                b1, b2 = bcast2(ps, w, onesr_sb[:], tok_rows, state["war_bc"])
                for pt in range(NPT):
                    k.v(lambda e, pt=pt, lo=lo, w=w:
                        e.tensor_sub(scr["t1"][:, :w], src_fn(pt, lo, lo + w),
                                     ps["bc0"][:, :w]),
                        waits=(b1, b2))
                    if add_to_x or xin_fn is not None:
                        k.v(lambda e, w=w:
                            e.tensor_mul(scr["t2"][:, :w], scr["t1"][:, :w],
                                         ps["bc1"][:, :w]))
                        tmv = None
                        if add_to_x:
                            tmv = k.v(lambda e, pt=pt, alo=alo, wa=wa:
                                      e.tensor_add(xs(pt, alo, alo + wa),
                                                   xs(pt, alo, alo + wa),
                                                   scr["t2"][:, :wa]))
                        if xin_fn is not None:
                            tmv = k.v(lambda e, pt=pt, alo=alo, wa=wa:
                                      e.tensor_copy(xin_fn(pt, alo, wa),
                                                    scr["t2"][:, :wa]))
                        out_tok = tmv
                    elif gain is not None:
                        k.v(lambda e, w=w:
                            e.tensor_mul(scr["t2"][:, :w], scr["t1"][:, :w],
                                         ps["bc1"][:, :w]))
                        out_tok = k.v(lambda e, pt=pt, alo=alo, wa=wa:
                                      e.tensor_scalar(dst_fn(pt, alo, wa),
                                                      scr["t2"][:, :wa],
                                                      gain[0](pt), gain[1](pt),
                                                      ALU.mult, ALU.add),
                                      waits=gain[2] if len(gain) > 2 else ())
                        if dbg and pt == 0 and lo == rng[0][0] and "dbg" in taps:
                            ddb = nc.dram_tensor("tap_dbg", [128, 2048], F32,
                                                 kind="ExternalOutput").ap()
                            db_sb = scr["dbg"]
                            k.v(lambda e: e.tensor_copy(db_sb[:, 0:512],
                                                        ps["bc0"][:, :]),
                                waits=(out_tok,))
                            k.v(lambda e: e.tensor_copy(db_sb[:, 512:1024],
                                                        ps["bc1"][:, :]))
                            k.v(lambda e: e.tensor_copy(db_sb[:, 1024:1536],
                                                        scr["t1"][:, :]))
                            tdb = k.v(lambda e: e.tensor_copy(db_sb[:, 1536:2048],
                                                              scr["t2"][:, :]))
                            k.dma(ddb, db_sb[:], waits=(tdb,), q="y")
                    elif wa == w:
                        out_tok = k.v(lambda e, pt=pt, alo=alo, wa=wa:
                                      e.tensor_mul(dst_fn(pt, alo, wa),
                                                   scr["t1"][:, :wa],
                                                   ps["bc1"][:, :wa]))
                    else:
                        tvm = k.v(lambda e, w=w:
                                  e.tensor_mul(scr["t2"][:, :w], scr["t1"][:, :w],
                                               ps["bc1"][:, :w]))
                        out_tok = k.act(dst_fn(pt, alo, wa),
                                        scr["t2"][:, off:off + wa], AF.Copy,
                                        waits=(tvm,))
                state["war_bc"] = out_tok
                state["war_rows"] = out_tok
            return out_tok

        with ExitStack() as PH:
            ps = {"st0": PB[4], "st1": PB[5], "bc0": PB[2], "bc1": PB[3]}
            pmh = PB[0][:, 0:16]
            scr = {
                "cb": PH.enter_context(nc.sbuf_tensor("s_cb", [128, 512], BF16)),
                "cb2": PH.enter_context(nc.sbuf_tensor("s_cb2", [128, 512], BF16)),
                "sq": PH.enter_context(nc.sbuf_tensor("s_sq", [128, 512], BF16)),
                "sq2": PH.enter_context(nc.sbuf_tensor("s_sq2", [128, 512], BF16)),
                "t1": PH.enter_context(nc.sbuf_tensor("s_t1", [128, 512], F32)),
                "t2": PH.enter_context(nc.sbuf_tensor("s_t2", [128, 512], F32)),
            }
            msk_f = PH.enter_context(nc.sbuf_tensor("msk_f", [1, TH], F32))
            msk_b = PH.enter_context(nc.sbuf_tensor("msk_b", [1, TH], BF16))
            tk_msk = k.dma(msk_f[:], D["kv_mask"], q="y")
            tok_xn = emit_ln(CH_TH, ps, scr, tk_xc,
                             dst_fn=lambda pt, lo, w: xns(pt, lo, lo + w))
            tmv = k.v(lambda e: e.tensor_copy(msk_b[:], msk_f[:]), waits=(tk_msk,))
            k.mm(pmh[:, 0:8], onesr_sb[:], msk_b[:, 0:8],
                 waits=(tmv, tk_const[3]))
            t2m = k.mm(pmh[:, 8:16], onesr_sb[:], msk_b[:, TH - 8:TH])
            tok_mask = k.v(lambda e: e.tensor_copy(mask_hb[:], pmh[:]),
                           waits=(t2m,))
            tapn("xn", xn_sb[:], NPT * TH, (tok_xn,))
            k.flush()

        with ExitStack() as PH:
            ao_sb = PH.enter_context(nc.sbuf_tensor("ao_sb", [128, 8 * T], BF16))
            qgb = {nm: PH.enter_context(nc.sbuf_tensor(nm + "_sb", [128, 1], F32))
                   for nm in ("qg", "qb", "kg", "kb")}
            g8 = {nm: PH.enter_context(nc.sbuf_tensor(nm + "8", [128, 1], F32))
                  for nm in ("qg", "qb")}
            scr = {
                "sq": PH.enter_context(nc.sbuf_tensor("a1_sq", [128, 512], BF16)),
                "t1": PH.enter_context(nc.sbuf_tensor("a1_t1", [128, 512], F32)),
                "t2": PH.enter_context(nc.sbuf_tensor("a1_t2", [128, 512], F32)),
                "tb": PH.enter_context(nc.sbuf_tensor("a1_tb", [128, 512], BF16)),
            }
            pk = [PB[0], PB[1]]
            pqs = PB[4][0:2, :]
            pqq = PB[5][0:2, :]
            bc0, bc1 = PB[2], PB[3]
            pss = [PB[0][:, 0:144], PB[1][:, 0:144],
                   PB[0][:, 144:288], PB[1][:, 144:288]]
            pta = [PTB[0][:, 0:128], PTB[1][:, 0:128],
                   PTB[0][:, 512:640], PTB[1][:, 512:640]]
            ptb = [PTB[0][0:16, 128:256], PTB[1][0:16, 128:256],
                   PTB[0][0:16, 640:768], PTB[1][0:16, 640:768]]
            pav = [PB[2][:, 0:128], PB[3][:, 0:128],
                   PB[2][:, 128:256], PB[3][:, 128:256]]
            ptv = [PTB[0][:, 256:384], PTB[1][:, 256:384]]
            pwp = PB[2][0:32, :]
            pwt = PB[3][:, 0:32]

            P1 = PH.enter_context(ExitStack())
            wq_sb = [P1.enter_context(nc.sbuf_tensor(f"wq{i}", [128, 8 * 128], BF16))
                     for i in range(2)]
            wkv_sb = [P1.enter_context(nc.sbuf_tensor(f"wkv{i}", [128, 8 * 256],
                                                      BF16)) for i in range(2)]
            wwin_sb = P1.enter_context(nc.sbuf_tensor("wwin_sb", [128, 8 * 32],
                                                      BF16))
            bwin_sb = P1.enter_context(nc.sbuf_tensor("bwin_sb", [32, 1], F32))
            wsc1_sb = P1.enter_context(nc.sbuf_tensor("wsc1_sb", [32, 1], F32))
            wsc2_sb = P1.enter_context(nc.sbuf_tensor("wsc2_sb", [32, 1], F32))
            idf_sb = P1.enter_context(nc.sbuf_tensor("idf_sb", [128, 128], F32))
            R_sb = P1.enter_context(nc.sbuf_tensor("R_sb", [128, 144], F32))
            NEG2_sb = P1.enter_context(nc.sbuf_tensor("NEG2_sb", [128, 144],
                                                      BF16))
            q_sb = P1.enter_context(nc.sbuf_tensor("q_sb", [128, T], BF16))
            kk_sb = P1.enter_context(nc.sbuf_tensor("kk_sb", [128, TH], BF16))
            v_sb = P1.enter_context(nc.sbuf_tensor("v_sb", [128, TH], BF16))
            vtm_sb = P1.enter_context(nc.sbuf_tensor("vtm_sb", [128, 17 * 128],
                                                     BF16))
            sh2_sb = P1.enter_context(nc.sbuf_tensor("sh2_sb", [128, 16 * 16], F32))
            wsh_sb = P1.enter_context(nc.sbuf_tensor("wsh_sb", [128, 16 * 16], F32))
            s3_sb = [P1.enter_context(nc.sbuf_tensor(f"s3_{i}", [128, 144], F32))
                     for i in range(4)]
            ee_all = [P1.enter_context(nc.sbuf_tensor(f"eea_{i}",
                                                      [128, 16 * 144], BF16))
                      for i in range(2)]
            aw_sb = [P1.enter_context(nc.sbuf_tensor(f"aw_{i}", [128, 144], BF16))
                     for i in range(4)]
            at1_sb = [P1.enter_context(nc.sbuf_tensor(f"at1_{i}", [128, 128], BF16))
                      for i in range(4)]
            at2_sb = [P1.enter_context(nc.sbuf_tensor(f"at2_{i}", [16, 128], BF16))
                      for i in range(4)]
            nmx = [P1.enter_context(nc.sbuf_tensor(f"nmx_{i}", [128, 16], F32))
                   for i in range(2)]
            den = [P1.enter_context(nc.sbuf_tensor(f"den_{i}", [128, 16], F32))
                   for i in range(2)]

            _a0done = (tok_xn, tok_mask)
            tk_wwin = k.dma(wwin_sb[:].rearrange("p (n c) -> p n c", c=32),
                            D["wwin_t"], waits=_a0done, q="gd")
            tk_bwin = k.dma(bwin_sb[:], D["bwin_c"], waits=_a0done, q="y")
            tk_wsc = k.dma(wsc1_sb[:], wsc1_c, waits=_a0done, q="y")
            tk_wsc = k.dma(wsc2_sb[:], wsc2_c, waits=_a0done, q="y")
            tk_idf = k.dma(idf_sb[:], ident_f, waits=_a0done, q="y")
            tk_R = k.dma(R_sb[:], R_c, waits=_a0done, q="y")
            tk_NEG = k.dma(NEG2_sb[:], NEG2_c, waits=_a0done, q="gd")
            tk_gb = []
            for nm in ("qg", "qb", "kg", "kb"):
                tk_gb.append(k.dma(qgb[nm][:], D[nm + "_c"], waits=_a0done, q="y"))
            tk_wq = [None, None]
            tk_wkv = [None, None]
            tk_wq[0] = k.dma(wq_sb[0][:].rearrange("p (n c) -> p n c", c=128),
                             D["wq_t"][0], waits=_a0done, q="gd")
            tk_wkv[0] = k.dma(wkv_sb[0][:].rearrange("p (n c) -> p n c", c=256),
                              D["wkv_t"][0], waits=_a0done, q="gd")

            WP = ExitStack()
            wp_sb = WP.enter_context(nc.sbuf_tensor("wp_sb", [32, T], F32))
            wsp_tm = WP.enter_context(nc.sbuf_tensor("wsp_tm", [128, 16 * 32],
                                                     F32))
            tok_wp = None
            for ci, (lo, hi) in enumerate(CH_T):
                w = hi - lo
                tmm = None
                for pt in range(NPT):
                    tmm = k.mm(pwp[:, :w], wwin_sb[:, pt * 32:(pt + 1) * 32],
                               xns(pt, lo, hi), start=(pt == 0),
                               stop=(pt == NPT - 1),
                               waits=(tk_wwin, tok_xn, tok_wp))
                tok_wp = k.act(wp_sb[:, lo - HW:hi - HW], pwp[:, :w], AF.Silu,
                               bias=bwin_sb[:], waits=(tmm, tk_bwin))
            tt_ = k.act(wp_sb[:], wp_sb[:], AF.Tanh, scale=0.5, waits=(tok_wp,))
            tv_s = k.v(lambda e: e.tensor_scalar(wp_sb[:], wp_sb[:],
                                                 wsc1_sb[:], wsc2_sb[:],
                                                 ALU.mult, ALU.add),
                       waits=(tt_, tk_wsc))
            tok_wt = None
            for tt2 in range(16):
                trt = k.tr(pwt[:], wp_sb[:, tt2 * 128:(tt2 + 1) * 128],
                           idf_sb[0:32, 0:32], waits=(tv_s, tok_wt, tk_idf))
                tok_wt = k.v(lambda e, tt2=tt2:
                             e.tensor_copy(wsp_tm[:, tt2 * 32:(tt2 + 1) * 32],
                                           pwt[:]), waits=(trt,))
            w3 = wsp_tm[:].rearrange("p (t c) -> p t c", c=32)
            s3view = sh2_sb[:].rearrange("p (t c) -> p t c", c=16)
            wsview = wsh_sb[:].rearrange("p (t c) -> p t c", c=16)
            k.v(lambda e: e.tensor_scalar_mul(s3view, w3[:, :, 16:32], -0.5),
                waits=(tok_wt,))
            k.v(lambda e: e.tensor_mul(wsview, w3[:, :, 0:16], w3[:, :, 16:32]))
            tok_wsp = k.v(lambda e: e.tensor_scalar_mul(wsview, wsview, 0.5))
            WP.close()

            state["war_bc"] = tok_wsp
            state["war_st"] = tok_wsp
            k.v(lambda e: e.tensor_scalar_mul(g8["qg"][:], qgb["qg"][:], 0.125),
                waits=tuple(tk_gb))
            tok_g8 = k.v(lambda e: e.tensor_scalar_mul(g8["qb"][:], qgb["qb"][:],
                                                       0.125))

            def head_ln(src, toks, gcol, bcol):
                tok = None
                for tk5 in toks:
                    tsl, lo, w = tk5[0], tk5[1], tk5[2]
                    alo = tk5[3] if len(tk5) > 3 else lo
                    off = alo - lo
                    wa = lo + w - alo
                    tv2 = k.v(lambda e, src=src, lo=lo, w=w:
                              e.tensor_mul(scr["sq"][:, :w], src[:, lo:lo + w],
                                           src[:, lo:lo + w]),
                              waits=(tsl, state["war_cb"], tok))
                    t1 = k.mm(pqs[:, :w], e2h_sb[:], src[:, lo:lo + w],
                              waits=(tsl, state["war_st"], tok))
                    t2 = k.mm(pqq[:, :w], e2h_sb[:], scr["sq"][:, :w],
                              waits=(tv2, state["war_st"]))
                    state["war_cb"] = t2
                    k.act(rowA[0:2, :w], pqs[:, :w], AF.Copy,
                          waits=(t1, state["war_rows"]))
                    tev = k.act(rowB[0:2, :w], pqq[:, :w], AF.Copy,
                                waits=(t2,))
                    state["war_st"] = tev
                    tok_rows = rows_chunk(w, 2, tev, 64.0)
                    b1, b2 = bcast2({"bc0": bc0, "bc1": bc1}, w, e2t_sb[:],
                                    tok_rows, state["war_bc"])
                    k.v(lambda e, src=src, lo=lo, w=w:
                        e.tensor_sub(scr["t1"][:, :w], src[:, lo:lo + w],
                                     bc0[:, :w]), waits=(b1, b2))
                    k.v(lambda e, w=w:
                        e.tensor_mul(scr["t2"][:, :w], scr["t1"][:, :w],
                                     bc1[:, :w]))
                    if wa == w:
                        tok = k.v(lambda e, src=src, lo=lo, w=w:
                                  e.tensor_scalar(src[:, lo:lo + w],
                                                  scr["t2"][:, :w],
                                                  gcol, bcol, ALU.mult, ALU.add),
                                  waits=(tok_g8,))
                    else:
                        tvs = k.v(lambda e, w=w:
                                  e.tensor_scalar(scr["tb"][:, :w],
                                                  scr["t2"][:, :w],
                                                  gcol, bcol, ALU.mult, ALU.add),
                                  waits=(tok_g8,))
                        tok = k.act(src[:, alo:alo + wa],
                                    scr["tb"][:, off:off + wa], AF.Copy,
                                    waits=(tvs,))
                    state["war_bc"] = tok
                    state["war_rows"] = tok
                return tok

            war_pk = [None, None]
            wfree = [None, None]
            tok_scores_done = None
            war_pass1 = [None, None]
            war_den = [None, None]

            for hp in range(8):
                par = hp % 2
                if hp < 7:
                    tk_wq[1 - par] = k.dma(
                        wq_sb[1 - par][:].rearrange("p (n c) -> p n c", c=128),
                        D["wq_t"][hp + 1], waits=(wfree[1 - par],), q="gd")
                    tk_wkv[1 - par] = k.dma(
                        wkv_sb[1 - par][:].rearrange("p (n c) -> p n c", c=256),
                        D["wkv_t"][hp + 1], waits=(wfree[1 - par],), q="gd")

                tok_q_ch = []
                for ci, (lo, hi) in enumerate(CH_T):
                    w = hi - lo
                    bank = pk[ci % 2]
                    tmm = None
                    for pt in range(NPT):
                        tmm = k.mm(bank[:, :w],
                                   wq_sb[par][:, pt * 128:(pt + 1) * 128],
                                   xns(pt, lo, hi), start=(pt == 0),
                                   stop=(pt == NPT - 1),
                                   waits=(tk_wq[par], tok_xn, war_pk[ci % 2],
                                          tok_scores_done, tok_mask))
                    tsl = k.act(q_sb[:, lo - HW:hi - HW], bank[:, :w], AF.Silu,
                                waits=(tmm,))
                    war_pk[ci % 2] = tsl
                    tok_q_ch.append((tsl, lo - HW, w))
                tok_k_ch = []
                tok_v_ch = []
                for ci, (lo, hi, alo) in enumerate(CH_TH):
                    w = hi - lo
                    bank = pk[ci % 2]
                    tmm = None
                    for pt in range(NPT):
                        tmm = k.mm(bank[:, :w],
                                   wkv_sb[par][:, pt * 256:pt * 256 + 128],
                                   xns(pt, lo, hi), start=(pt == 0),
                                   stop=(pt == NPT - 1),
                                   waits=(tk_wkv[par], war_pk[ci % 2],
                                          tok_scores_done))
                    tsl = k.act(kk_sb[:, lo:hi], bank[:, :w], AF.Silu,
                                waits=(tmm,))
                    war_pk[ci % 2] = tsl
                    tok_k_ch.append((tsl, lo, w, alo))
                    tmm = None
                    for pt in range(NPT):
                        tmm = k.mm(bank[:, :w],
                                   wkv_sb[par][:, pt * 256 + 128:pt * 256 + 256],
                                   xns(pt, lo, hi), start=(pt == 0),
                                   stop=(pt == NPT - 1), waits=(tsl,))
                    tsl = k.act(v_sb[:, lo:hi], bank[:, :w], AF.Silu,
                                waits=(tmm,))
                    war_pk[ci % 2] = tsl
                    tok_v_ch.append((tsl, lo, w))
                wfree[par] = tok_v_ch[-1][0]
                k.v(lambda e: e.tensor_mul(v_sb[:, 0:8], v_sb[:, 0:8],
                                           mask_hb[:, 0:8]),
                    waits=(tok_v_ch[0][0], tok_mask))
                tok_vmask = k.v(lambda e: e.tensor_mul(v_sb[:, TH - 8:TH],
                                                       v_sb[:, TH - 8:TH],
                                                       mask_hb[:, 8:16]),
                                waits=(tok_v_ch[-1][0],))

                tok_qn = head_ln(q_sb, tok_q_ch, g8["qg"][:], g8["qb"][:])
                tok_kn = head_ln(kk_sb, tok_k_ch, qgb["kg"][:], qgb["kb"][:])
                k.v(lambda e: e.tensor_mul(kk_sb[:, 0:8], kk_sb[:, 0:8],
                                           mask_hb[:, 0:8]), waits=(tok_kn,))
                tok_kn = k.v(lambda e: e.tensor_mul(kk_sb[:, TH - 8:TH],
                                                    kk_sb[:, TH - 8:TH],
                                                    mask_hb[:, 8:16]))

                tok_vt = {}
                war_tv = [None, None]
                for j in range(17):
                    wj = 128 if j < 16 else 16
                    pv = ptv[j % 2]
                    trt = k.tr(pv[0:wj, 0:128], v_sb[:, j * 128:j * 128 + wj],
                               idb_sb[:], waits=(tok_vmask, war_tv[j % 2]))
                    war_tv[j % 2] = k.act(
                        vtm_sb[0:wj, j * 128:(j + 1) * 128],
                        pv[0:wj, 0:128], AF.Copy, waits=(trt,))
                    tok_vt[j] = war_tv[j % 2]
                tok_vt[17] = tok_vt[16]

                war_ss = [None] * 4
                war_ex = [None] * 4
                war_at = [None] * 4
                war_av = [None] * 4
                it = 0
                it2 = 0
                for hd in range(2):
                    hb = hd * 64
                    hglob = hp * 2 + hd
                    par = hglob % 2
                    tse = None
                    for qt in range(16):
                        p2 = it % 4
                        k.mm(pss[p2][:, :],
                             q_sb[hb:hb + 64, qt * 128:(qt + 1) * 128],
                             kk_sb[hb:hb + 64, qt * 128:qt * 128 + 144],
                             start=True, stop=False,
                             waits=(tok_qn, tok_kn, tok_vmask, war_ss[p2]))
                        tmm = k.mm(pss[p2][:, :], idb_sb[:], NEG2_sb[:],
                                   start=False, stop=True, waits=(tk_NEG,))
                        tss = k.act(s3_sb[p2][:], R_sb[:], AF.Tanh,
                                    scale=sh2_sb[:, qt * 16 + hglob:
                                                 qt * 16 + hglob + 1],
                                    bias=wsh_sb[:, qt * 16 + hglob:
                                                qt * 16 + hglob + 1],
                                    waits=(tok_wsp, war_ss[p2], war_ex[p2],
                                           tk_R, tk_NEG))
                        tad = k.v(lambda e, p2=p2: e.scalar_tensor_tensor(
                            s3_sb[p2][:], s3_sb[p2][:], 5e3, pss[p2][:],
                            ALU.mult, ALU.add),
                            waits=(tmm, tss,
                                   war_pass1[par] if qt == 0 else None))
                        war_ss[p2] = tad
                        tmx = k.v(lambda e, p2=p2, qt=qt, par=par:
                                  e.reduce_max(nmx[par][:, qt:qt + 1],
                                               s3_sb[p2][:], AX.X, negate=True))
                        tse = k.act(ee_all[par][:, qt * 144:(qt + 1) * 144],
                                    s3_sb[p2][:], AF.Exp,
                                    bias=nmx[par][:, qt:qt + 1],
                                    accum=den[par][:, qt:qt + 1],
                                    waits=(tmx, war_den[par]))
                        war_ex[p2] = tse
                        it += 1
                    war_pass1[par] = tse
                    k.act(den[par][:], den[par][:], AF.Ln, waits=(tse,))
                    trc = k.act(den[par][:], den[par][:], AF.Exp, scale=-1.0)
                    taw = None
                    for qt in range(16):
                        p2 = it2 % 4
                        taw = k.v(lambda e, p2=p2, qt=qt, par=par:
                                  e.tensor_scalar(aw_sb[p2][:],
                                                  ee_all[par][:, qt * 144:
                                                              (qt + 1) * 144],
                                                  den[par][:, qt:qt + 1], 0.0,
                                                  ALU.mult, ALU.add),
                                  waits=(trc, war_at[p2]))
                        tr1 = k.tr(pta[p2][:], aw_sb[p2][:, 0:128], idb_sb[:],
                                   waits=(taw, war_at[p2]))
                        tr2 = k.tr(ptb[p2][:], aw_sb[p2][:, 128:144], idb_sb[:])
                        te1 = k.act(at1_sb[p2][:], pta[p2][:], AF.Copy,
                                    waits=(tr1,))
                        te2 = k.act(at2_sb[p2][:], ptb[p2][:], AF.Copy,
                                    waits=(tr2,))
                        k.mm(pav[p2][hb:hb + 64, :],
                             vtm_sb[:, qt * 128 + hb:qt * 128 + hb + 64],
                             at1_sb[p2][:], start=True, stop=False,
                             waits=(te1, tok_vt[qt], tok_vt[qt + 1],
                                    war_av[p2]))
                        tav = k.mm(pav[p2][hb:hb + 64, :],
                                   vtm_sb[0:16, (qt + 1) * 128 + hb:
                                          (qt + 1) * 128 + hb + 64],
                                   at2_sb[p2][:], start=False, stop=True,
                                   waits=(te2,))
                        war_at[p2] = tav
                        tev = k.act(ao_sb[hb:hb + 64, hp * T + qt * 128:
                                          hp * T + (qt + 1) * 128],
                                    pav[p2][hb:hb + 64, :], AF.Copy,
                                    waits=(tav,))
                        war_av[p2] = tev
                        tok_scores_done = tev
                        it2 += 1
                    war_den[par] = taw
                state["war_bc"] = tok_scores_done
                if hp == 7:
                    tapn("q", q_sb[:], T, (tok_qn,))
                    tapn("kk", kk_sb[:], TH, (tok_kn,))
                    tapn("v", v_sb[:], TH, (tok_vmask,))
                    tapn("wsp", wsp_tm[:], 16 * 32, (tok_wsp,))
                    tapn("ao", ao_sb[:], 8 * T, (tok_scores_done,))

            k.flush()
            P1.close()

            wout_sb = PH.enter_context(nc.sbuf_tensor("wout_sb", [128, 8 * 1024],
                                                      BF16))
            tk_wout = k.dma(wout_sb[:].rearrange("p (n c) -> p n c", c=1024),
                            D["wout_t"], waits=(tok_scores_done,), q="gd")
            tok_x_upd = None
            war_tb = None
            for cc in range(8):
                for ci, (lo, hi) in enumerate(CH_T):
                    w = hi - lo
                    bank = pk[ci % 2]
                    tmm = None
                    for pt in range(NPT):
                        tmm = k.mm(bank[:, :w],
                                   wout_sb[:, pt * 1024 + cc * 128:
                                           pt * 1024 + (cc + 1) * 128],
                                   ao_sb[:, pt * T + lo - HW:pt * T + hi - HW],
                                   start=(pt == 0), stop=(pt == NPT - 1),
                                   waits=(tk_wout, tok_scores_done,
                                          war_pk[ci % 2]))
                    tsl = k.act(scr["tb"][:, :w], bank[:, :w], AF.Silu,
                                waits=(tmm, war_tb))
                    war_pk[ci % 2] = tsl
                    tok_x_upd = k.v(lambda e, cc=cc, lo=lo, w=w:
                                    e.tensor_add(xs(cc, lo, lo + w),
                                                 xs(cc, lo, lo + w),
                                                 scr["tb"][:, :w]), waits=(tsl,))
                    war_tb = tok_x_upd
            k.flush()
        tok_x1 = tok_x_upd

        if "x1" in tap_d:
            k.dma(tap_d["x1"], x_sb[:].rearrange("p (n c) -> p n c", c=TH),
                  waits=(tok_x1,), q="y")
            k.flush()

        CH_OWN = [(HW, 520), (520, 1032), (1032, 1544), (1544, 2056)]
        with ExitStack() as PH:
            scr = {
                "t1": PH.enter_context(nc.sbuf_tensor("a2_t1", [128, 512], F32)),
                "t2": PH.enter_context(nc.sbuf_tensor("a2_t2", [128, 512], F32)),
            }
            ps = {"st0": PB[4], "st1": PB[5], "bc0": PB[2], "bc1": PB[3]}
            pk = [PB[0], PB[1]]
            gam_sb = PH.enter_context(nc.sbuf_tensor("gam_sb", [128, 8], F32))
            bet_sb = PH.enter_context(nc.sbuf_tensor("bet_sb", [128, 8], F32))
            f1b_sb = PH.enter_context(nc.sbuf_tensor("f1b_sb", [128, 32], F32))
            f2b_sb = PH.enter_context(nc.sbuf_tensor("f2b_sb", [128, 8], F32))
            tk_gam = k.dma(gam_sb[:], D["gam_c"], waits=(tok_x1,), q="y")
            tk_bet = k.dma(bet_sb[:], D["bet_c"], waits=(tok_x1,), q="y")
            tk_f1b = k.dma(f1b_sb[:].rearrange("p (h n) -> p h n", n=16),
                           D["fc1b_c"].rearrange("h p n -> p h n"),
                           waits=(tok_x1,), q="y")
            tk_f2b = k.dma(f2b_sb[:], D["fc2b_c"], waits=(tok_x1,), q="y")

            w1_sb = PH.enter_context(nc.sbuf_tensor("w1_sb", [128, 8 * 2048],
                                                    BF16))
            qraw = [PH.enter_context(nc.sbuf_tensor(f"qraw{i}", [128, 2048],
                                                    BF16)) for i in range(2)]
            bcs = [PH.enter_context(nc.sbuf_tensor(f"bcs{i}", [128, 512], F32))
                   for i in range(2)]
            A2a = PH.enter_context(ExitStack())
            xin_sb = A2a.enter_context(nc.sbuf_tensor("xin_sb", [128, 8 * T], BF16))
            for nm2 in ("cb", "cb2", "sq", "sq2"):
                scr[nm2] = A2a.enter_context(
                    nc.sbuf_tensor("a2_" + nm2, [128, 512], BF16))

            def xinf(pt, lo, w):
                return xin_sb[:, pt * T + lo - HW: pt * T + lo - HW + w]

            def xin_src(pt, lo, hi):
                return xin_sb[:, pt * T + lo - HW: pt * T + hi - HW]

            tok_xin = emit_ln(CH_OWN, ps, scr, tok_x1, add_to_x=True, xin_fn=xinf)
            if "dbg" in taps:
                scr["dbg"] = A2a.enter_context(
                    nc.sbuf_tensor("db_sb", [128, 2048], F32))
            tok_h = emit_ln(CH_OWN, ps, scr, tok_xin,
                            dst_fn=lambda pt, lo, w: xns(pt, lo, lo + w),
                            src_fn=xin_src, src_bf16=True,
                            gain=(lambda pt: gam_sb[:, pt:pt + 1],
                                  lambda pt: bet_sb[:, pt:pt + 1],
                                  (tk_gam, tk_bet)), dbg=True)
            tapn("xin", xin_sb[:], 8 * T, (tok_xin,))
            tapn("xin2", xin_sb[:], 8 * T, (tok_h,))
            tapn("hh", xn_sb[:], 8 * TH, (tok_h,))
            if "rows" in taps:
                drow = nc.dram_tensor("tap_rows", [2, 1024], F32,
                                      kind="ExternalOutput").ap()
                rv_sb = A2a.enter_context(nc.sbuf_tensor("rv_sb", [2, 1024], F32))
                k.v(lambda e: e.tensor_copy(rv_sb[:, 0:512], rowA[0:2, :]),
                    waits=(tok_h,))
                trr = k.v(lambda e: e.tensor_copy(rv_sb[:, 512:1024], rowB[0:2, :]))
                k.dma(drow, rv_sb[:], waits=(trr,), q="y")
            k.flush()
            A2a.close()

            w2_sb = PH.enter_context(nc.sbuf_tensor("w2_sb", [128, 16 * 1024], BF16))
            y_sb = PH.enter_context(nc.sbuf_tensor("y_sb", [128, 16 * 512], BF16))

            war_qraw = [tok_x1, tok_x1]
            war_w1 = tok_x1
            war_w2 = tok_h
            war_bcs = [tok_x1, tok_x1]
            tk_bcs_prev = None
            tok_x2 = None
            war_pk2 = [state["war_bc"], state["war_bc"]]
            for half in (0, 1):
                tok_w1 = None
                for kt in range(8):
                    tkr = k.dma(qraw[kt % 2][:], D["fc1q_t"][half][:, kt, :],
                                waits=(war_qraw[kt % 2],), q="gd")
                    for c4 in range(4):
                        ib = (kt * 4 + c4) % 2
                        tkb = k.dma(bcs[ib][:],
                                    D["fc1s_t"][half][kt * 4 + c4]
                                    .broadcast_to([128, 512]),
                                    waits=(war_bcs[ib], tk_bcs_prev),
                                    q="y", sig="m")
                        tk_bcs_prev = tkb
                        tok_w1 = k.v(
                            lambda e, kt=kt, c4=c4, ib=ib:
                            e.scalar_tensor_tensor(
                                w1_sb[:, kt * 2048 + c4 * 512:
                                      kt * 2048 + (c4 + 1) * 512],
                                qraw[kt % 2][:, c4 * 512:(c4 + 1) * 512],
                                -128.0, bcs[ib][:], ALU.add, ALU.mult),
                            waits=(tkr, tkb, war_w1))
                        war_bcs[ib] = tok_w1
                    war_qraw[kt % 2] = tok_w1
                tok_w2 = None
                for kt in range(16):
                    tkr = k.dma(qraw[kt % 2][:, 0:1024],
                                D["fc2q_t"][half][:, kt, :],
                                waits=(war_qraw[kt % 2],), q="gd")
                    for c4 in range(2):
                        ib = (kt * 2 + c4) % 2
                        tkb = k.dma(bcs[ib][:],
                                    D["fc2s_t"][half][kt * 2 + c4]
                                    .broadcast_to([128, 512]),
                                    waits=(war_bcs[ib], tk_bcs_prev),
                                    q="y", sig="m")
                        tk_bcs_prev = tkb
                        tok_w2 = k.v(
                            lambda e, kt=kt, c4=c4, ib=ib:
                            e.scalar_tensor_tensor(
                                w2_sb[:, kt * 1024 + c4 * 512:
                                      kt * 1024 + (c4 + 1) * 512],
                                qraw[kt % 2][:, c4 * 512:(c4 + 1) * 512],
                                -128.0, bcs[ib][:], ALU.add, ALU.mult),
                            waits=(tkr, tkb, war_w2))
                        war_bcs[ib] = tok_w2
                    war_qraw[kt % 2] = tok_w2
                war_y = None
                last_fc1 = None
                last_fc2 = None
                for ci, (lo, hi) in enumerate(CH_OWN):
                    w = hi - lo
                    tok_y = []
                    for nc2 in range(16):
                        bank = pk[nc2 % 2]
                        tmm = None
                        for kt in range(8):
                            tmm = k.mm(
                                bank[:, :w],
                                w1_sb[:, kt * 2048 + nc2 * 128:
                                      kt * 2048 + (nc2 + 1) * 128],
                                xns(kt, lo, hi),
                                start=(kt == 0), stop=(kt == 7),
                                waits=(tok_w1, tok_h, war_pk2[nc2 % 2], war_y))
                        last_fc1 = tmm
                        tsy = k.act(y_sb[:, nc2 * 512:nc2 * 512 + w], bank[:, :w],
                                    AF.Silu,
                                    bias=f1b_sb[:, half * 16 + nc2:
                                                half * 16 + nc2 + 1],
                                    waits=(tmm, tk_f1b))
                        war_pk2[nc2 % 2] = tsy
                        tok_y.append(tsy)
                    for cc in range(8):
                        bank = pk[cc % 2]
                        tmm = None
                        for kt in range(16):
                            tmm = k.mm(
                                bank[:, :w],
                                w2_sb[:, kt * 1024 + cc * 128:
                                      kt * 1024 + (cc + 1) * 128],
                                y_sb[:, kt * 512:kt * 512 + w],
                                start=(kt == 0), stop=(kt == 15),
                                waits=(tok_w2, tok_y[-1], war_pk2[cc % 2]))
                        last_fc2 = tmm
                        if half == 0:
                            tvz = k.v(lambda e, w=w, cc=cc, bank=bank:
                                      e.tensor_scalar(scr["t2"][:, :w],
                                                      bank[:, :w],
                                                      f2b_sb[:, cc:cc + 1],
                                                      0.5, ALU.add, ALU.mult),
                                      waits=(tmm, tk_f2b))
                            tok_x2 = k.v(lambda e, cc=cc, lo=lo, w=w:
                                         e.tensor_add(xs(cc, lo, lo + w),
                                                      xs(cc, lo, lo + w),
                                                      scr["t2"][:, :w]))
                            war_pk2[cc % 2] = tok_x2
                        else:
                            tok_x2 = k.v(lambda e, cc=cc, lo=lo, w=w, bank=bank:
                                         e.scalar_tensor_tensor(
                                             xs(cc, lo, lo + w), bank[:, :w],
                                             0.5, xs(cc, lo, lo + w),
                                             ALU.mult, ALU.add),
                                         waits=(tmm,))
                            war_pk2[cc % 2] = tok_x2
                    war_y = last_fc2
                war_w1 = last_fc1
                war_w2 = last_fc2
            tapn("w1d", w1_sb[:], 8 * 2048, (tok_x2,))
            tapn("w2d", w2_sb[:], 16 * 1024, (tok_x2,))
            tapn("yv", y_sb[:], 16 * 512, (tok_x2,))
            k.flush()

        if "x2" in tap_d:
            k.dma(tap_d["x2"], x_sb[:].rearrange("p (n c) -> p n c", c=TH),
                  waits=(tok_x2,), q="y")
            k.flush()

        MC = 256
        with ExitStack() as P3:
            scr3 = {
                "cb": P3.enter_context(nc.sbuf_tensor("a3_cb", [128, 512], BF16)),
                "cb2": P3.enter_context(nc.sbuf_tensor("a3_cb2", [128, 512], BF16)),
                "sq": P3.enter_context(nc.sbuf_tensor("a3_sq", [128, 512], BF16)),
                "sq2": P3.enter_context(nc.sbuf_tensor("a3_sq2", [128, 512], BF16)),
                "t1": P3.enter_context(nc.sbuf_tensor("a3_t1", [128, 512], F32)),
                "t2": P3.enter_context(nc.sbuf_tensor("a3_t2", [128, 512], F32)),
            }
            ps3 = {"st0": PB[4], "st1": PB[5], "bc0": PB[2], "bc1": PB[3]}
            pk3 = [PB[0], PB[1]]
            gts_sb = P3.enter_context(nc.sbuf_tensor("gts_sb", [128, 8 * T], BF16))
            aac_sb = P3.enter_context(nc.sbuf_tensor("aac_sb", [64, 1024], F32))
            atRb_sb = P3.enter_context(nc.sbuf_tensor("atRb_sb", [128, 1024],
                                                      BF16))

            tok_xn3 = emit_ln(CH_OWN, ps3, scr3, tok_x2,
                              dst_fn=lambda pt, lo, w: xns(pt, lo, lo + w))
            if "xn3" in taps:
                tapn("xn3", xn_sb[:], 8 * TH, (tok_xn3,))

            with ExitStack() as PG:
                wg_sb = PG.enter_context(nc.sbuf_tensor("wg_sb", [128, 8 * 1024],
                                                        BF16))
                egs = PG.enter_context(nc.sbuf_tensor("egs", [128, 512], BF16))
                tk_wg = k.dma(wg_sb[:].rearrange("p (n c) -> p n c", c=1024),
                              D["wg_t"], waits=(tok_x2,), q="gd")
                war_eg = None
                war_rowsg = state["war_rows"]
                war_bcg = state["war_bc"]
                war_stg = state["war_st"]
                tok_gts = None
                gi2 = 0
                for pt in range(NPT):
                    for ci, (lo, hi) in enumerate(CH_OWN):
                        w = hi - lo
                        bank = pk3[gi2 % 2]
                        tmm = None
                        for kt in range(NPT):
                            tmm = k.mm(bank[:, :w],
                                       wg_sb[:, kt * 1024 + pt * 128:
                                             kt * 1024 + (pt + 1) * 128],
                                       xns(kt, lo, hi), start=(kt == 0),
                                       stop=(kt == NPT - 1),
                                       waits=(tk_wg, tok_xn3, war_eg))
                        tse = k.act(egs[:, :w], bank[:, :w], AF.Exp, scale=0.125,
                                    waits=(tmm, war_eg))
                        tden = k.mm(ps3["st0"][0:2, :w], e2h_sb[:], egs[:, :w],
                                    waits=(tse, war_stg))
                        tcp = k.v(lambda e, w=w: e.tensor_copy(rowB[0:2, :w],
                                                               ps3["st0"][0:2, :w]),
                                  waits=(tden, war_rowsg))
                        war_stg = tcp
                        tln = k.act(rowB[0:2, :w], rowB[0:2, :w], AF.Ln,
                                    waits=(tcp,))
                        tln = k.act(rowB[0:2, :w], rowB[0:2, :w], AF.Exp,
                                    scale=-1.0)
                        k.v(lambda e, w=w: e.tensor_copy(rBh[0:2, :w],
                                                         rowB[0:2, :w]),
                            waits=(tln,))
                        trl = k.v(lambda e, w=w: e.tensor_sub(rBl[0:2, :w],
                                                              rowB[0:2, :w],
                                                              rBh[0:2, :w]))
                        k.mm(ps3["bc0"][:, :w], e2t_sb[:], rBh[0:2, :w],
                             start=True, stop=False, waits=(trl, war_bcg))
                        tbc = k.mm(ps3["bc0"][:, :w], e2t_sb[:], rBl[0:2, :w],
                                   start=False, stop=True)
                        tok_gts = k.v(
                            lambda e, pt=pt, lo=lo, w=w:
                            e.tensor_mul(gts_sb[:, pt * T + lo - HW:
                                                pt * T + lo - HW + w],
                                         egs[:, :w], ps3["bc0"][:, :w]),
                            waits=(tbc,))
                        war_eg = tok_gts
                        war_bcg = tok_gts
                        war_rowsg = tok_gts
                        gi2 += 1
                state["war_st"] = war_stg
                state["war_bc"] = war_bcg
                state["war_rows"] = war_rowsg
                k.flush()
            if "gts" in taps:
                tapn("gts", gts_sb[:], 8 * T, (tok_gts,))

            with ExitStack() as PK:
                wk_sb = PK.enter_context(nc.sbuf_tensor("wk_sb", [128, 8 * 1024],
                                                        BF16))
                wv_sb = PK.enter_context(nc.sbuf_tensor("wv_sb", [128, 8 * 1024],
                                                        BF16))
                cos_sb = PK.enter_context(nc.sbuf_tensor("cos_sb", [128, MC],
                                                         BF16))
                sin_sb = PK.enter_context(nc.sbuf_tensor("sin_sb", [128, MC],
                                                         BF16))
                latq_sb = PK.enter_context(nc.sbuf_tensor("latq_sb", [128, 512],
                                                          BF16))
                kr_sb = PK.enter_context(nc.sbuf_tensor("kr_sb", [128, 8 * MC],
                                                        BF16))
                kk2_sb = PK.enter_context(nc.sbuf_tensor("kk2_sb", [128, 8 * MC],
                                                         BF16))
                v2_sb = PK.enter_context(nc.sbuf_tensor("v2_sb", [128, 8 * MC],
                                                        BF16))
                vt2_sb = PK.enter_context(nc.sbuf_tensor("vt2_sb", [128, 32 * 64],
                                                         BF16))
                sm3 = [PK.enter_context(nc.sbuf_tensor(f"sm3_{i}", [128, 128], F32))
                       for i in range(2)]
                ee3 = [PK.enter_context(nc.sbuf_tensor(f"ee3_{i}", [128, 128],
                                                       BF16)) for i in range(2)]
                aw3 = [PK.enter_context(nc.sbuf_tensor(f"aw3_{i}", [128, 128],
                                                       BF16)) for i in range(2)]
                pT3 = [PK.enter_context(nc.sbuf_tensor(f"pT3_{i}", [128, 128],
                                                       BF16)) for i in range(2)]
                nm3 = [PK.enter_context(nc.sbuf_tensor(f"nm3_{i}", [128, 1], F32))
                       for i in range(2)]
                dn3 = [PK.enter_context(nc.sbuf_tensor(f"dn3_{i}", [128, 1], F32))
                       for i in range(2)]

                tk_wk = k.dma(wk_sb[:].rearrange("p (n c) -> p n c", c=1024),
                              D["wk_t"], waits=(tok_gts,), q="gd")
                tk_wv = k.dma(wv_sb[:].rearrange("p (n c) -> p n c", c=1024),
                              D["wv_t"], waits=(tok_gts,), q="gd")
                war_trig = tok_gts
                tk_la = k.dma(latq_sb[:, 0:256], D["latq_a"], waits=(tok_gts,),
                              q="gd")
                tk_lb = k.dma(latq_sb[:, 256:512], D["latq_b"], q="gd")
                tok_lq = k.v(lambda e: e.tensor_scalar_mul(latq_sb[:], latq_sb[:],
                                                           0.125),
                             waits=(tk_la, tk_lb))

                war_pk3 = [state["war_bc"], state["war_bc"]]
                war_kr = None
                war_v2 = None
                war_vt = [None, None]
                war_sc = [None, None]
                war_ex3 = [None, None]
                war_pt3 = [None, None]
                war_pat = [None, None]
                tok_aac = None
                it3 = 0
                for mc in range(8):
                    lo = HW + mc * MC
                    hi = lo + MC
                    tk_cos = k.dma(cos_sb[:], D["cos_t"][:, mc * MC:(mc + 1) * MC],
                                   waits=(war_trig,), q="gd")
                    tk_sin = k.dma(sin_sb[:], D["sin_t"][:, mc * MC:(mc + 1) * MC],
                                   q="gd")
                    tok_kr = None
                    for opt in range(8):
                        bank = pk3[opt % 2]
                        tmm = None
                        for kt in range(NPT):
                            tmm = k.mm(bank[:, :MC],
                                       wk_sb[:, kt * 1024 + opt * 128:
                                             kt * 1024 + (opt + 1) * 128],
                                       xns(kt, lo, hi), start=(kt == 0),
                                       stop=(kt == NPT - 1),
                                       waits=(tk_wk, war_pk3[opt % 2]))
                        tok_kr = k.act(kr_sb[:, opt * MC:(opt + 1) * MC],
                                       bank[:, :MC], AF.Copy,
                                       waits=(tmm, war_kr))
                        war_pk3[opt % 2] = tok_kr
                    tok_rope = None
                    for j in range(4):
                        ka = kr_sb[:, j * MC:(j + 1) * MC]
                        kb = kr_sb[:, (j + 4) * MC:(j + 5) * MC]
                        cs = cos_sb[:]
                        sn = sin_sb[:]
                        k.v(lambda e, ka=ka, cs=cs:
                            e.tensor_mul(scr3["t1"][:, :MC], ka, cs),
                            waits=(tok_kr, tk_cos, tk_sin, tok_rope))
                        k.v(lambda e, kb=kb, sn=sn:
                            e.tensor_mul(scr3["t2"][:, :MC], kb, sn))
                        k.v(lambda e, j=j:
                            e.tensor_sub(kk2_sb[:, j * MC:(j + 1) * MC],
                                         scr3["t1"][:, :MC], scr3["t2"][:, :MC]))
                        k.v(lambda e, ka=ka, sn=sn:
                            e.tensor_mul(scr3["t1"][:, :MC], ka, sn))
                        k.v(lambda e, kb=kb, cs=cs:
                            e.tensor_mul(scr3["t2"][:, :MC], kb, cs))
                        tok_rope = k.v(
                            lambda e, j=j:
                            e.tensor_add(kk2_sb[:, (j + 4) * MC:(j + 5) * MC],
                                         scr3["t1"][:, :MC], scr3["t2"][:, :MC]))
                    war_kr = tok_rope
                    war_trig = tok_rope
                    tok_v2 = None
                    for opt in range(8):
                        bank = pk3[opt % 2]
                        tmm = None
                        for kt in range(NPT):
                            tmm = k.mm(bank[:, :MC],
                                       wv_sb[:, kt * 1024 + opt * 128:
                                             kt * 1024 + (opt + 1) * 128],
                                       xns(kt, lo, hi), start=(kt == 0),
                                       stop=(kt == NPT - 1),
                                       waits=(tk_wv, war_pk3[opt % 2]))
                        tok_v2 = k.act(v2_sb[:, opt * MC:(opt + 1) * MC],
                                       bank[:, :MC], AF.Copy,
                                       waits=(tmm, war_v2))
                        war_pk3[opt % 2] = tok_v2
                    for c1 in range(2):
                        for h in range(16):
                            i2 = (c1 * 16 + h) % 2
                            hb2 = (h % 2) * 64
                            src = v2_sb[hb2:hb2 + 64,
                                        (h // 2) * MC + c1 * 128:
                                        (h // 2) * MC + (c1 + 1) * 128]
                            pv = PTB[i2][:, 256:320]
                            trt = k.tr(pv, src, idb_sb[hb2:hb2 + 64,
                                                       hb2:hb2 + 64],
                                       waits=(tok_v2, war_vt[i2]))
                            war_vt[i2] = k.v(
                                lambda e, c1=c1, h=h, pv=pv:
                                e.tensor_copy(vt2_sb[:, (c1 * 16 + h) * 64:
                                                     (c1 * 16 + h + 1) * 64],
                                              pv), waits=(trt,))
                    tok_vt2 = war_vt[0]
                    tok_vt2b = war_vt[1]
                    war_v2 = war_vt[1]
                    for hp2 in range(8):
                        for c1 in range(2):
                            p2 = it3 % 2
                            bank = pk3[p2]
                            for hd in range(2):
                                h = hp2 * 2 + hd
                                pb = (h % 4) * 32
                                tp = (pb, (hd * 64) % 128) if pb >= 96 else None
                                k.mm(bank[hd * 64:hd * 64 + 64, 0:128],
                                     latq_sb[pb:pb + 32,
                                             (h // 4) * 64:(h // 4) * 64 + 64],
                                     kk2_sb[pb:pb + 32,
                                            (h // 4) * MC + c1 * 128:
                                            (h // 4) * MC + (c1 + 1) * 128],
                                     start=True, stop=False,
                                     waits=(tok_lq, tok_rope, tok_v2,
                                            war_sc[p2]),
                                     tile_position=tp)
                                k.mm(bank[hd * 64:hd * 64 + 64, 0:128],
                                     latq_sb[pb:pb + 32,
                                             256 + (h // 4) * 64:
                                             256 + (h // 4) * 64 + 64],
                                     kk2_sb[pb:pb + 32,
                                            (4 + h // 4) * MC + c1 * 128:
                                            (4 + h // 4) * MC + (c1 + 1) * 128],
                                     start=False, stop=True,
                                     tile_position=tp)
                            tmm = k.mm(bank[:, 0:128], idb_sb[:], C2_sb[:],
                                       start=False, stop=True,
                                       waits=(tk_C2,))
                            tvv = k.v(lambda e, p2=p2, bank=bank:
                                      e.reduce_max(nm3[p2][:], bank[:, 0:128],
                                                   AX.X, negate=True),
                                      waits=(tmm,))
                            tse = k.act(ee3[p2][:], bank[:, 0:128], AF.Exp,
                                        bias=nm3[p2][:], accum=dn3[p2][:],
                                        waits=(tvv,))
                            war_sc[p2] = tse
                            war_ex3[p2] = tse
                            k.act(dn3[p2][:], dn3[p2][:], AF.Ln,
                                  waits=(tse,))
                            tvr = k.act(dn3[p2][:], dn3[p2][:], AF.Exp,
                                        scale=-1.0)
                            taw = k.act(aw3[p2][:], ee3[p2][:], AF.Copy,
                                        scale=dn3[p2][:], waits=(tvr,))
                            trt = k.tr(PTB[p2][:, 0:128], aw3[p2][:], idb_sb[:],
                                       waits=(taw, war_pt3[p2]))
                            tev = k.v(lambda e, p2=p2:
                                      e.tensor_copy(pT3[p2][:], PTB[p2][:, 0:128]),
                                      waits=(trt,))
                            war_pt3[p2] = tev
                            for hd in range(2):
                                h = hp2 * 2 + hd
                                pat = ps3["bc0"] if p2 == 0 else ps3["bc1"]
                                tpa = k.mm(pat[0:64, 0:64],
                                           pT3[p2][:, hd * 64:hd * 64 + 64],
                                           vt2_sb[:, (c1 * 16 + h) * 64:
                                                  (c1 * 16 + h + 1) * 64],
                                           start=True, stop=True,
                                           waits=(tev, tok_vt2, tok_vt2b,
                                                  war_pat[p2]))
                                if mc == 0:
                                    tac = k.v(lambda e, h=h, pat=pat:
                                              e.tensor_copy(
                                                  aac_sb[:, h * 64:(h + 1) * 64],
                                                  pat[0:64, 0:64]), waits=(tpa,))
                                else:
                                    tac = k.v(lambda e, h=h, pat=pat:
                                              e.tensor_add(
                                                  aac_sb[:, h * 64:(h + 1) * 64],
                                                  aac_sb[:, h * 64:(h + 1) * 64],
                                                  pat[0:64, 0:64]), waits=(tpa,))
                                war_pat[p2] = tac
                                tok_aac = tac
                            it3 += 1
                    war_pk3[0] = war_sc[0]
                    war_pk3[1] = war_sc[1]
                k.flush()

            tk_ar = k.dma(ar_in.ap(), aac_sb[:], waits=(tok_aac,), q="gd")
            tok_cc = k.g(lambda e: e.collective_compute(
                "AllReduce", ALU.add, replica_groups=[list(range(NCORES))],
                ins=[ar_in.ap().opt()], outs=[ar_out.ap().opt()]),
                waits=(tk_ar,), sig="cc")
            tk_r0 = k.dma(atRb_sb[0:64, :], ar_out.ap(), waits=(tok_cc,), q="gd")
            tk_r1 = k.dma(atRb_sb[64:128, :], ar_out.ap(), q="gd")

            with ExitStack() as PW:
                wp_sb = PW.enter_context(nc.sbuf_tensor("wp_sb2", [128, 8 * 1024],
                                                        BF16))
                w8_sb = PW.enter_context(nc.sbuf_tensor("w8_sb", [128, 8 * T],
                                                        BF16))
                tk_wp = k.dma(wp_sb[:].rearrange("p (n c) -> p n c", c=1024),
                              D["wp_t"], waits=(tok_aac,), q="gd")
                war_pw = [None, None]
                tok_w8 = None
                iw = 0
                for h in range(16):
                    hb = (h % 2) * 64
                    for ci, (lo, hi) in enumerate(CH_OWN):
                        w = hi - lo
                        bank = pk3[iw % 2]
                        tmm = k.mm(bank[hb:hb + 64, :w],
                                   atRb_sb[hb:hb + 64, h * 64:(h + 1) * 64],
                                   gts_sb[hb:hb + 64,
                                          (h // 2) * T + lo - HW:
                                          (h // 2) * T + hi - HW],
                                   start=True, stop=True,
                                   waits=(tk_r0, tk_r1, tok_gts,
                                          war_pw[iw % 2]))
                        tok_w8 = k.act(w8_sb[hb:hb + 64,
                                             (h // 2) * T + lo - HW:
                                             (h // 2) * T + hi - HW],
                                       bank[hb:hb + 64, :w], AF.Copy,
                                       waits=(tmm,))
                        war_pw[iw % 2] = tok_w8
                        iw += 1
                tok_x3 = None
                war_tb3 = None
                for cc in range(8):
                    for ci, (lo, hi) in enumerate(CH_OWN):
                        w = hi - lo
                        bank = pk3[iw % 2]
                        tmm = None
                        for kt in range(NPT):
                            tmm = k.mm(bank[:, :w],
                                       wp_sb[:, kt * 1024 + cc * 128:
                                             kt * 1024 + (cc + 1) * 128],
                                       w8_sb[:, kt * T + lo - HW:
                                             kt * T + hi - HW],
                                       start=(kt == 0), stop=(kt == NPT - 1),
                                       waits=(tk_wp, tok_w8, war_pw[iw % 2]))
                        tok_x3 = k.v(lambda e, cc=cc, lo=lo, w=w, bank=bank:
                                     e.tensor_add(xs(cc, lo, lo + w),
                                                  xs(cc, lo, lo + w),
                                                  bank[:, :w]),
                                     waits=(tmm, war_tb3))
                        war_pw[iw % 2] = tok_x3
                        war_tb3 = tok_x3
                        iw += 1
                k.flush()
            if "x3" in tap_d:
                k.dma(tap_d["x3"], x_sb[:].rearrange("p (n c) -> p n c", c=TH),
                      waits=(tok_x3,), q="y")
            tok_xn4 = emit_ln(CH_OWN, ps3, scr3, tok_x3,
                              dst_fn=lambda pt, lo, w: xns(pt, lo, lo + w))
            tapn("xn4", xn_sb[:], 8 * TH, (tok_xn4,))
            k.flush()
        LNS.close()

        m2_d = nc.dram_tensor("m2_d", [128, 32, 512], BF16)
        with ExitStack() as P4:
            with ExitStack() as PM:
                rmb_sb = PM.enter_context(nc.sbuf_tensor("rmb_sb",
                                                         [128, 32 * 128], BF16))
                rub_sb = PM.enter_context(nc.sbuf_tensor("rub_sb",
                                                         [128, 32 * 512], BF16))
                m2s = PM.enter_context(nc.sbuf_tensor("m2s", [128, 512], BF16))
                tk_rm = k.dma(rmb_sb[:].rearrange("p (n c) -> p n c", c=128),
                              D["rmbdT_t"], waits=(tok_xn4,), q="gd")
                tk_ru = k.dma(rub_sb[:].rearrange("p (n c) -> p n c", c=512),
                              D["rupbd_t"], waits=(tok_xn4,), q="gd")
                war_m2mm = None
                war_m2s = None
                for gi in range(32):
                    tmm = k.mm(PB[0][:, :],
                               rmb_sb[:, gi * 128:(gi + 1) * 128],
                               rub_sb[:, gi * 512:(gi + 1) * 512],
                               start=True, stop=True,
                               waits=(tk_rm, tk_ru, war_m2mm))
                    tev = k.act(m2s[:], PB[0][:, :], AF.Copy,
                                waits=(tmm, war_m2s))
                    war_m2mm = tev
                    war_m2s = k.dma(m2_d.ap()[:, gi, :], m2s[:], waits=(tev,),
                                    q="y")
                tok_m2 = war_m2s
                k.flush()

            hf_sb = P4.enter_context(nc.sbuf_tensor("hf_sb", [128, 16 * T], BF16))
            wcb_sb = P4.enter_context(nc.sbuf_tensor("wcb_sb", [128, 16 * 1024],
                                                     BF16))
            m2b = [P4.enter_context(nc.sbuf_tensor(f"m2b{i}", [128, 512],
                                                   BF16)) for i in range(2)]
            war_cp = [None, None]
            tok_x4 = None
            war_wcb = tok_m2
            war_m2b = [tok_m2, tok_m2]
            war_z = [None, None]
            war_hf = None
            tk_m2b_prev = None
            war_hfspill = None
            hfd = None
            if "hf" in taps:
                hfd = nc.dram_tensor("tap_hf", [128, 8, 16 * T], BF16,
                                     kind="ExternalOutput").ap()
            for fb in range(8):
                tk_wcb = k.dma(
                    wcb_sb[:].rearrange("p (n c) -> p n c", c=1024),
                    D["wc_t"][:, fb * 16:(fb + 1) * 16, :],
                    waits=(war_wcb,), q="gd")
                tok_hf = None
                for gl in range(4):
                    gi = fb * 4 + gl
                    tk_m2b = k.dma(m2b[gl % 2][:], m2_d.ap()[:, gi, :],
                                   waits=(war_m2b[gl % 2], tk_m2b_prev),
                                   q="y", sig="m")
                    tk_m2b_prev = tk_m2b
                    for zc in range(4):
                        for ci, (lo, hi) in enumerate(CH_OWN):
                            w = hi - lo
                            zb = PB[4 + (zc * 4 + ci) % 2]
                            tmm = k.mm(zb[:, :w],
                                       m2b[gl % 2][:, zc * 128:(zc + 1) * 128],
                                       xns(gi % 8, lo, hi),
                                       start=True, stop=True,
                                       waits=(tk_m2b, tok_xn4, tok_x4,
                                              war_z[(zc * 4 + ci) % 2]))
                            ft = gl * 4 + zc
                            tok_hf = k.act(
                                hf_sb[:, ft * T + lo - HW:ft * T + hi - HW],
                                zb[:, :w], AF.Gelu,
                                waits=(tmm, war_hf, war_hfspill))
                            war_z[(zc * 4 + ci) % 2] = tok_hf
                    war_m2b[gl % 2] = tok_hf
                if hfd is not None:
                    war_hfspill = k.dma(hfd[:, fb, :], hf_sb[:],
                                        waits=(tok_hf,), q="gd")
                war_hf = None
                last_mm = None
                icp = 0
                for cc in range(8):
                    for ci, (lo, hi) in enumerate(CH_OWN):
                        w = hi - lo
                        bank = PB[icp % 2]
                        tmm = None
                        for fkt in range(16):
                            tmm = k.mm(bank[:, :w],
                                       wcb_sb[:, fkt * 1024 + cc * 128:
                                              fkt * 1024 + (cc + 1) * 128],
                                       hf_sb[:, fkt * T + lo - HW:
                                             fkt * T + hi - HW],
                                       start=(fkt == 0), stop=(fkt == 15),
                                       waits=(tk_wcb, tok_hf,
                                              war_cp[icp % 2]))
                            last_mm = tmm
                        tok_x4 = k.v(lambda e, cc=cc, lo=lo, w=w, bank=bank:
                                     e.tensor_add(xs(cc, lo, lo + w),
                                                  xs(cc, lo, lo + w),
                                                  bank[:, :w]),
                                     waits=(tmm,))
                        war_cp[icp % 2] = tok_x4
                        icp += 1
                war_wcb = last_mm
                war_hf = tok_x4
            k.flush()

        tok_out = k.dma(out_x,
                        x_sb[:].rearrange("p (n c) -> p n c", c=TH)[:, :, HW:HW + T],
                        waits=(tok_x4,), q="y")
        k._emit("sync", lambda e: e.nop(), (tok_out,), None, 1)
        k.flush()

    return nc



def _tile_pm(a, kt, cols):
    return np.ascontiguousarray(a.reshape(kt, 128, cols).transpose(1, 0, 2))


def stage_inputs(inp):
    f = np.float32
    x = np.asarray(inp["L0_attn_x"], f)[0]
    xT = np.ascontiguousarray(x.T)

    shared = {}
    wq = np.asarray(inp["L0_attn_Wq"], f)
    shared["wq_t"] = np.ascontiguousarray(
        wq.reshape(8, 128, 8, 128).transpose(2, 1, 0, 3))
    wkv = np.asarray(inp["L0_attn_Wkv"], f)
    kcols = wkv[:, :1024].reshape(1024, 8, 128)
    vcols = wkv[:, 1024:].reshape(1024, 8, 128)
    kv = np.concatenate([kcols, vcols], axis=2)
    shared["wkv_t"] = np.ascontiguousarray(
        kv.reshape(8, 128, 8, 256).transpose(2, 1, 0, 3))
    shared["wwin_t"] = _tile_pm(np.asarray(inp["L0_attn_Wwin"], f), 8, 32)
    shared["bwin_c"] = np.asarray(inp["L0_attn_bwin"], f).reshape(32, 1)
    for nm, key in (("qg", "L0_attn_q_g"), ("qb", "L0_attn_q_b"),
                    ("kg", "L0_attn_k_g"), ("kb", "L0_attn_k_b")):
        v = np.asarray(inp[key], f)
        shared[nm + "_c"] = np.ascontiguousarray(
            np.concatenate([v, v]).reshape(128, 1))
    shared["wout_t"] = _tile_pm(np.asarray(inp["L0_attn_Wout"], f), 8, 1024)
    shared["gam_c"] = np.ascontiguousarray(
        np.asarray(inp["L0_ffn_ln_gamma"], f).reshape(8, 128).T)
    shared["bet_c"] = np.ascontiguousarray(
        np.asarray(inp["L0_ffn_ln_beta"], f).reshape(8, 128).T)
    fc1q = np.asarray(inp["L0_ffn_fc1_q"], np.int32)
    shared["fc1q_t"] = np.ascontiguousarray(
        fc1q.reshape(8, 128, 2, 2048).transpose(2, 1, 0, 3))
    shared["fc1s_t"] = np.ascontiguousarray(
        np.asarray(inp["L0_ffn_fc1_scales"], f).reshape(8, 2, 2048)
        .transpose(1, 0, 2).reshape(2, 32, 1, 512))
    shared["fc1b_c"] = np.ascontiguousarray(
        np.asarray(inp["L0_ffn_fc1_bias"], f).reshape(2, 16, 128)
        .transpose(0, 2, 1))
    fc2q = np.asarray(inp["L0_ffn_fc2_q"], np.int32)
    shared["fc2q_t"] = np.ascontiguousarray(
        fc2q.reshape(2, 16, 128, 1024).transpose(0, 2, 1, 3))
    shared["fc2s_t"] = np.ascontiguousarray(
        np.asarray(inp["L0_ffn_fc2_scales"], f).reshape(2, 32, 1, 512))
    shared["fc2b_c"] = np.ascontiguousarray(
        np.asarray(inp["L0_ffn_fc2_bias"], f).reshape(8, 128).T)

    wk = np.asarray(inp["L1_attn_W_k"], f)
    perm = np.empty(1024, np.int64)
    for h in range(16):
        for i in range(32):
            perm[h * 32 + i] = h * 64 + 2 * i
            perm[512 + h * 32 + i] = h * 64 + 2 * i + 1
    shared["wk_t"] = _tile_pm(np.ascontiguousarray(wk.T[:, perm]), 8, 1024)
    shared["wv_t"] = _tile_pm(np.ascontiguousarray(
        np.asarray(inp["L1_attn_W_v"], f).T), 8, 1024)
    shared["wg_t"] = _tile_pm(np.ascontiguousarray(
        np.asarray(inp["L1_attn_W_g"], f).T), 8, 1024)
    shared["wp_t"] = _tile_pm(np.ascontiguousarray(
        np.asarray(inp["L1_attn_W_p"], f).T), 8, 1024)
    lq = np.asarray(inp["L1_attn_latent_queries"], f)[0]
    la = np.zeros((128, 256), f)
    lb = np.zeros((128, 256), f)
    for h in range(16):
        q = lq[:, h, :]
        pbase = (h % 4) * 32
        cbase = (h // 4) * 64
        la[pbase:pbase + 32, cbase:cbase + 64] = q.T[0::2, :]
        lb[pbase:pbase + 32, cbase:cbase + 64] = q.T[1::2, :]
    shared["latq_a"] = la
    shared["latq_b"] = lb

    rm = np.asarray(inp["L1_ffn_rail_maker"], f)
    rup = np.asarray(inp["L1_ffn_rail_up_proj"], f)
    rmT = np.zeros((32, 128, 128), f)
    rub = np.zeros((32, 128, 512), f)
    for h in range(4):
        for g in range(8):
            gi = h * 8 + g
            for rl in range(8):
                r = g * 8 + rl
                rmT[gi, rl * 16:(rl + 1) * 16, rl * 16:(rl + 1) * 16] = rm[h, r].T
                rub[gi, rl * 16:(rl + 1) * 16, rl * 64:(rl + 1) * 64] = rup[h, r].T
    shared["rmbdT_t"] = np.ascontiguousarray(rmT.transpose(1, 0, 2))
    shared["rupbd_t"] = np.ascontiguousarray(rub.transpose(1, 0, 2))
    wc = np.asarray(inp["L1_ffn_w_cproj"], f)
    shared["wc_t"] = np.ascontiguousarray(
        wc.T.reshape(128, 128, 1024).transpose(1, 0, 2))

    in_maps = []
    inv = 1.0 / (10000.0 ** (np.arange(0, 64, 2, dtype=np.float64) / 64.0))
    for c in range(NCORES):
        m = dict(shared)
        t0 = c * T
        xt = np.zeros((1024, TH), f)
        lo = max(0, t0 - HW); hi = min(16384, t0 + T + HW)
        xt[:, lo - (t0 - HW):hi - (t0 - HW)] = xT[:, lo:hi]
        m["x_t"] = np.ascontiguousarray(xt.reshape(8, 128, TH).transpose(1, 0, 2))
        km = np.zeros((1, TH), f)
        km[0, lo - (t0 - HW):hi - (t0 - HW)] = 1.0
        m["kv_mask"] = km
        ang = np.arange(t0, t0 + T, dtype=np.float64)[None, :] * inv[:, None]
        m["cos_t"] = np.ascontiguousarray(np.tile(np.cos(ang).astype(f), (4, 1)))
        m["sin_t"] = np.ascontiguousarray(np.tile(np.sin(ang).astype(f), (4, 1)))
        in_maps.append(m)
    return in_maps


_NC_CACHE = {}


def _get_nc(taps=()):
    key = tuple(taps)
    if key not in _NC_CACHE:
        _NC_CACHE[key] = build_program(taps=key)
    return _NC_CACHE[key]


def run_cores(inputs, taps=()):
    in_maps = stage_inputs(inputs)
    nc = _get_nc(taps)
    return run_bass_kernel_spmd(nc, in_maps, core_ids=list(range(NCORES)))


def kernel(**inputs):
    res = run_cores(inputs)
    y = np.empty((16384, 1024), np.float32)
    for i, r in enumerate(res.results):
        p = r["out_x"]
        y[T * i:T * (i + 1)] = p.transpose(1, 0, 2).reshape(1024, T).T
    return y[None]
